# version 18
# baseline (speedup 1.0000x reference)
import math
import contextlib
import numpy as np
import concourse.bass as bass
import concourse.mybir as mybir
from concourse.bass_utils import run_bass_kernel_spmd

F32 = mybir.dt.float32
BF16 = mybir.dt.bfloat16
AF = mybir.ActivationFunctionType
ALU = mybir.AluOpType
AX = mybir.AxisListType

NEG_BIG = -1.0e30


def build_nc(H=2048, S=4096, NH=16, FF=4096):
    HD = H // NH
    assert HD == 128 and H % 128 == 0 and FF % 128 == 0 and S % 512 == 0
    NF = H // 128
    SC = 512
    NSC = S // SC
    SQ = S // 4
    NQT = SQ // 128
    NFC = FF // 128
    QCH = min(512, SQ)
    NQC = SQ // QCH
    NKB = S // 128
    scale = 1.0 / math.sqrt(HD)

    nc = bass.Bass("TRN2", target_bir_lowering=False)

    xTs = nc.dram_tensor("xTs", [H, S], F32, kind="ExternalInput").ap()
    xTq = nc.dram_tensor("xTq", [H, SQ], F32, kind="ExternalInput").ap()
    wqT = nc.dram_tensor("wqT", [H, H], F32, kind="ExternalInput").ap()
    wkT = nc.dram_tensor("wkT", [H, H], F32, kind="ExternalInput").ap()
    wvT = nc.dram_tensor("wvT", [H, H], F32, kind="ExternalInput").ap()
    woT = nc.dram_tensor("woT", [H, H], F32, kind="ExternalInput").ap()
    wgT = nc.dram_tensor("wgT", [H, 2], F32, kind="ExternalInput").ap()
    W0aT = nc.dram_tensor("W0aT", [H, FF], F32, kind="ExternalInput").ap()
    W0bT = nc.dram_tensor("W0bT", [FF, H], F32, kind="ExternalInput").ap()
    W1aT = nc.dram_tensor("W1aT", [H, FF], F32, kind="ExternalInput").ap()
    W1bT = nc.dram_tensor("W1bT", [FF, H], F32, kind="ExternalInput").ap()
    b0a2 = nc.dram_tensor("b0a2", [128, NFC], F32, kind="ExternalInput").ap()
    b1a2 = nc.dram_tensor("b1a2", [128, NFC], F32, kind="ExternalInput").ap()
    b0b2 = nc.dram_tensor("b0b2", [128, NF], F32, kind="ExternalInput").ap()
    b1b2 = nc.dram_tensor("b1b2", [128, NF], F32, kind="ExternalInput").ap()
    masks_in = nc.dram_tensor("masks", [128, 512], F32, kind="ExternalInput").ap()
    consts = nc.dram_tensor("consts", [128, 272], F32, kind="ExternalInput").ap()
    outT = nc.dram_tensor("outT", [H, SQ], F32, kind="ExternalOutput").ap()
    KTs = nc.dram_tensor("KTs", [H, S], F32, kind="Internal").ap()
    Vs = nc.dram_tensor("Vs", [S, H], F32, kind="Internal").ap()
    Fs = nc.dram_tensor("Fs", [H, SQ], F32, kind="Internal").ap()
    sWah = nc.dram_tensor("sWah", [H, FF], BF16, kind="Internal").ap()
    sWal = nc.dram_tensor("sWal", [H, FF], BF16, kind="Internal").ap()
    sWbh = nc.dram_tensor("sWbh", [FF, H], BF16, kind="Internal").ap()
    sWbl = nc.dram_tensor("sWbl", [FF, H], BF16, kind="Internal").ap()
    sW1ah = nc.dram_tensor("sW1ah", [H, FF], BF16, kind="Internal").ap()
    sW1al = nc.dram_tensor("sW1al", [H, FF], BF16, kind="Internal").ap()
    sW1bh = nc.dram_tensor("sW1bh", [FF, H], BF16, kind="Internal").ap()
    sW1bl = nc.dram_tensor("sW1bl", [FF, H], BF16, kind="Internal").ap()
    swqh = nc.dram_tensor("swqh", [H, H], BF16, kind="Internal").ap()
    swql = nc.dram_tensor("swql", [H, H], BF16, kind="Internal").ap()
    swoh = nc.dram_tensor("swoh", [H, H], BF16, kind="Internal").ap()
    swol = nc.dram_tensor("swol", [H, H], BF16, kind="Internal").ap()

    kgroups = [list(range(g, min(g + 8, NF))) for g in range(0, NF, 8)]
    ndp = max(1, H // 512)
    VDW = min(512, H)
    vdp_pairs = [list(range(d, min(d + 2, ndp))) for d in range(0, ndp, 2)]
    nst = SC // 128
    fgroups = [list(range(g, min(g + 8, NFC))) for g in range(0, NFC, 8)]
    ogroups = [list(range(g, min(g + 8, NF))) for g in range(0, NF, 8)]
    GW = 8 * QCH

    a2_chunks = []
    for srcT, dh, dl in ((W0aT, sWah, sWal), (W0bT, sWbh, sWbl),
                         (W1aT, sW1ah, sW1al), (W1bT, sW1bh, sW1bl),
                         (wqT, swqh, swql), (woT, swoh, swol)):
        rows, cols = srcT.tensor.shape
        for rb in range(rows // 128):
            for c0 in range(0, cols, 2048):
                cw = min(2048, cols - c0)
                a2_chunks.append((srcT[rb * 128:(rb + 1) * 128, c0:c0 + cw],
                                  dh[rb * 128:(rb + 1) * 128, c0:c0 + cw],
                                  dl[rb * 128:(rb + 1) * 128, c0:c0 + cw], cw))

    ctx = contextlib.ExitStack()
    with ctx:
        xqh = ctx.enter_context(nc.sbuf_tensor("xqh", [128, NF * SQ], BF16))
        xql = ctx.enter_context(nc.sbuf_tensor("xql", [128, NF * SQ], BF16))
        RA = ctx.enter_context(nc.sbuf_tensor("RA", [128, 16384], F32))
        RB = ctx.enter_context(nc.sbuf_tensor("RB", [128, max(2 * S + 1536, 4096)], F32))
        XN = ctx.enter_context(nc.sbuf_tensor("XN", [128, 2 * max(GW, 4096)], BF16))
        WS = ctx.enter_context(nc.sbuf_tensor("WS", [128, 6144], BF16))
        RC = ctx.enter_context(nc.sbuf_tensor("RC", [128, 2304], F32))
        RS = ctx.enter_context(nc.sbuf_tensor("RS", [128, 1024], F32))
        ps = ctx.enter_context(nc.psum_tensor("ps", [128, 4096], F32))

        def bank(k, w=512):
            return ps[:, k * 512:k * 512 + w]

        ones = RS[:, 0:128]
        ident = RS[:, 128:256]
        eps_col = RS[:, 256:257]
        zero_col = RS[:, 257:258]
        maskt = RS[:, 272:784]
        c_b0a = RS[:, 784:784 + NFC]
        c_b1a = RS[:, 816:816 + NFC]
        c_b0b = RS[:, 848:848 + NF]
        c_b1b = RS[:, 864:864 + NF]
        wg_sb = RS[:, 880:880 + 2 * NF]
        mx = RS[:, 912:920]
        m_col = RS[:, 920:921]
        negm = RS[:, 921:922]
        den = RS[:, 922:923]
        rden = RS[:, 923:924]
        wgd = RS[:, 928:928 + NF]

        SP_sb = RB[:, 0:S]
        Vb = RB[:, S:2 * S]
        kcs = [RB[:, 2 * S + i * 512:2 * S + (i + 1) * 512] for i in range(3)]
        kevac = [RB[:, i * 512:(i + 1) * 512] for i in range(4)]
        sqbuf = RB[:, 2048:2560]
        rowA = RB[0:1, 2560:3072]
        rowB = RB[0:1, 3072:3584]
        rowC = RB[0:1, 3584:4096]
        r_mean = rowA
        r_msq, r_sd, r_mmr = rowB, rowB, rowB
        r_var, r_rstd = rowC, rowC

        def out0(oc):
            return RB[:, oc * QCH:(oc + 1) * QCH]
        ld2 = [RB[:, 2048 + i * 512:2048 + (i + 1) * 512] for i in range(4)]
        ob2 = [RB[:, i * 512:i * 512 + QCH] for i in range(2)]

        xn_hi = XN[:, 0:GW]
        xn_lo = XN[:, GW:2 * GW]
        a2o = [(XN[:, 0:2048], XN[:, 2048:4096]),
               (XN[:, 4096:6144], XN[:, 6144:8192])]

        wsl = [(WS[:, i * 2048:i * 2048 + 1024], WS[:, i * 2048 + 1024:(i + 1) * 2048])
               for i in range(2)]
        FHALF = max(1, NF // 2)
        wqh_b = WS[:, 4096:4096 + FHALF * 128]
        wql_b = WS[:, 4096 + FHALF * 128:4096 + 2 * FHALF * 128]
        aspl = [(WS[:, 4096 + i * 1024:4096 + i * 1024 + 512],
                 WS[:, 4096 + i * 1024 + 512:4096 + (i + 1) * 1024])
                for i in range(2)]
        xsl = [(WS[:, 4096 + i * 1024:4096 + i * 1024 + 512],
                WS[:, 4096 + i * 1024 + 512:4096 + (i + 1) * 1024])
               for i in range(2)]
        wgd_hi = WS[:, 4096:4096 + NF]
        wgd_lo = WS[:, 4096 + 64:4096 + 64 + NF]

        wb2 = [RC[:, 0:1024], RC[:, 1024:2048]]
        mask_b = RC[:, 0:512]
        r_gmask = RC[0:1, 0:512]
        invm = RC[:, 512:1024]
        Qh = RC[:, 0:SQ]
        ptb = [RC[:, 1024 + i * 512:1024 + (i + 1) * 512] for i in range(2)]

        def h32(i):
            return RA[:, i * QCH:(i + 1) * QCH]

        def xbuf(f):
            return RA[:, f * SC:(f + 1) * SC]

        a2st = [RA[:, 8192 + i * 2048:8192 + (i + 1) * 2048] for i in range(2)]

        def attnbuf(h, c0, c1):
            return RA[:, h * SQ + c0:h * SQ + c1]

        class Sem:
            def __init__(self, name):
                self.h = ctx.enter_context(nc.semaphore(name))
                self.n = 0

            def inc(self, inst, k=1):
                inst.then_inc(self.h, k)
                self.n += k
                return inst

        s_ld = Sem("s_ld")
        s_w = [Sem("s_w0"), Sem("s_w1")]
        s_p = [Sem("s_p0"), Sem("s_p1")]
        s_k = [Sem("s_k0"), Sem("s_k1"), Sem("s_k2")]
        g_st = Sem("g_st")
        g_k = [Sem(f"g_k{i}") for i in range(4)]
        g_o = [Sem("g_o0"), Sem("g_o1")]
        g_a = [Sem("g_a0"), Sem("g_a1")]
        pe_a = Sem("pe_a")
        pe_w = Sem("pe_w")
        pe_t = Sem("pe_t")
        pe_o = Sem("pe_o")
        dv_a = Sem("dv_a")
        dv_e = Sem("dv_e")
        dv_s = Sem("dv_s")
        dv_h = Sem("dv_h")
        dv_x = Sem("dv_x")
        dv_o = Sem("dv_o")
        se_a = Sem("se_a")
        se_o = Sem("se_o")
        ALL = [s_ld, g_st, pe_a, pe_w, pe_t, pe_o, dv_a, dv_e, dv_s, dv_h, dv_o, dv_x,
               se_a, se_o] + s_w + s_p + s_k + g_k + g_o + g_a

        def snap():
            return {s: s.n for s in ALL}

        def wait(eng, sem, v):
            eng.wait_ge(sem.h, v)

        def slot_load(sync, slot_sem, out_ap, in_ap):
            if slot_sem.n:
                wait(sync, slot_sem, slot_sem.n)
            slot_sem.inc(sync.dma_start(out=out_ap, in_=in_ap), 16)
            return slot_sem.n

        def mm3(tensor, out_ap, sl_hi, sl_lo, rhs_hi, rhs_lo, start, stop):
            tensor.matmul(out_ap, sl_hi, rhs_hi, start=start, stop=False)
            tensor.matmul(out_ap, sl_hi, rhs_lo, start=False, stop=False)
            return tensor.matmul(out_ap, sl_lo, rhs_hi, start=False, stop=stop)

        def pair_load(sync, slot_sem, outs, ins):
            if slot_sem.n:
                wait(sync, slot_sem, slot_sem.n)
            for o, i in zip(outs, ins):
                slot_sem.inc(sync.dma_start(out=o, in_=i), 16)
            return slot_sem.n

        with nc.named_scope("phA"), nc.Block(no_gpsimd_drain=True) as blk:
            @blk.sync
            def _(sync):
                s_ld.inc(sync.dma_start(out=maskt, in_=masks_in), 16)
                s_ld.inc(sync.dma_start(out=RS[:, 0:272], in_=consts), 16)
                s_ld.inc(sync.dma_start(out=c_b0a, in_=b0a2), 16)
                s_ld.inc(sync.dma_start(out=c_b1a, in_=b1a2), 16)
                s_ld.inc(sync.dma_start(out=c_b0b, in_=b0b2), 16)
                s_ld.inc(sync.dma_start(out=c_b1b, in_=b1b2), 16)
                for f in range(NF):
                    s_ld.inc(sync.dma_start(out=wg_sb[:, 2 * f:2 * f + 2],
                                            in_=wgT[f * 128:(f + 1) * 128, :]), 16)
                sync.wait_ge(s_ld.h, s_ld.n)


        def emit_ln(blk, src_dram, col0, width, base):
            @blk.sync
            def _(sync):
                for f in range(NF):
                    s_ld.inc(sync.dma_start(
                        out=xbuf(f)[:, 0:width],
                        in_=src_dram[f * 128:(f + 1) * 128, col0:col0 + width]), 16)
                sync.wait_ge(s_ld.h, s_ld.n)

            xtot = base[s_ld] + 16 * NF

            @blk.scalar
            def _(scalar):
                wait(scalar, s_ld, xtot)
                for f in range(NF):
                    if f > 0:
                        wait(scalar, pe_a, base[pe_a] + f)
                    se_a.inc(scalar.activation(out=sqbuf[:, 0:width],
                                               in_=xbuf(f)[:, 0:width], func=AF.Square))
                wait(scalar, dv_a, base[dv_a] + 2)
                se_a.inc(scalar.activation(out=r_sd[:, 0:width], in_=r_var[:, 0:width],
                                           func=AF.Sqrt, bias=eps_col[0:1, 0:1]))

            @blk.tensor
            def _(tensor):
                wait(tensor, s_ld, xtot)
                for f in range(NF):
                    wait(tensor, se_a, base[se_a] + f + 1)
                    tensor.matmul(ps[0:1, 0:width], ones[:, 0:1], xbuf(f)[:, 0:width],
                                  start=(f == 0), stop=(f == NF - 1))
                    pe_a.inc(tensor.matmul(ps[0:1, 512:512 + width], ones[:, 0:1],
                                           sqbuf[:, 0:width],
                                           start=(f == 0), stop=(f == NF - 1)))
                wait(tensor, dv_a, base[dv_a] + 4)
                tensor.matmul(bank(2, width), ones[0:1, :], r_rstd[:, 0:width],
                              start=True, stop=True)
                pe_a.inc(tensor.matmul(bank(3, width), ones[0:1, :], r_mmr[:, 0:width],
                                       start=True, stop=True))

            @blk.vector
            def _(vector):
                wait(vector, pe_a, base[pe_a] + NF)
                dv_a.inc(vector.tensor_scalar_mul(r_mean[:, 0:width], ps[0:1, 0:width], 1.0 / H))
                vector.tensor_scalar_mul(r_var[:, 0:width], ps[0:1, 512:512 + width], 1.0 / H)
                vector.drain()
                vector.tensor_tensor(r_msq[:, 0:width], r_mean[:, 0:width],
                                     r_mean[:, 0:width], ALU.mult)
                vector.drain()
                dv_a.inc(vector.tensor_tensor(r_var[:, 0:width], r_var[:, 0:width],
                                              r_msq[:, 0:width], ALU.subtract))
                wait(vector, se_a, base[se_a] + NF + 1)
                dv_a.inc(vector.reciprocal(r_rstd[:, 0:width], r_sd[:, 0:width]))
                vector.drain()
                dv_a.inc(vector.tensor_tensor(r_mmr[:, 0:width], r_mean[:, 0:width],
                                              r_rstd[:, 0:width], ALU.mult))
                wait(vector, pe_a, base[pe_a] + NF + 1)
                for f in range(NF):
                    vector.tensor_tensor(xbuf(f)[:, 0:width], xbuf(f)[:, 0:width],
                                         bank(2, width), ALU.mult)
                vector.drain()
                for f in range(NF):
                    dv_a.inc(vector.tensor_tensor(xbuf(f)[:, 0:width], xbuf(f)[:, 0:width],
                                                  bank(3, width), ALU.subtract))

        kv_slot_cnt = [0, 0, 0, 0]
        a2_done = 0
        a2_per_blk = (len(a2_chunks) + NSC - 1) // NSC
        for sc in range(NSC):
            with nc.named_scope(f"phB{sc}"), nc.Block(no_gpsimd_drain=True) as blk:
                base = snap()
                emit_ln(blk, xTs, sc * SC, SC, base)
                wmarks = []
                amarks = []
                a2_base = a2_done
                my_a2 = a2_chunks[a2_done:a2_done + a2_per_blk]
                a2_done += len(my_a2)

                def a2_wave(c):
                    return (3 * (c + 1) + NF - 1) // NF

                a2_pos = {}
                for ci in range(len(my_a2)):
                    if ci < 2:
                        a2_pos[ci] = 3 * ci
                    else:
                        a2_pos[ci] = max(3 * ci, NF * a2_wave(ci - 2) + 1)

                @blk.sync
                def _(sync):
                    widx = 0
                    nxt_a2 = [0]

                    def emit_a2_load(sync):
                        ci = nxt_a2[0]
                        if ci >= len(my_a2):
                            return
                        asrc, dh, dl, cw = my_a2[ci]
                        slot = (a2_base + ci) % 2
                        if ci >= 2:
                            wait(sync, dv_s, base[dv_s] + ci - 1)
                        amarks.append(slot_load(sync, s_p[slot], a2st[slot][:, 0:cw], asrc))
                        nxt_a2[0] += 1

                    for grp_kind, groups, wsrc in (("k", kgroups, wkT), ("v", vdp_pairs, wvT)):
                        for grp in groups:
                            if grp_kind == "k":
                                c0, cw = grp[0] * 128, len(grp) * 128
                            else:
                                c0, cw = grp[0] * VDW, len(grp) * VDW
                            for f in range(NF):
                                if widx >= 2:
                                    wait(sync, dv_h, base[dv_h] + widx - 1)
                                wmarks.append(slot_load(
                                    sync, s_w[widx % 2], wb2[widx % 2][:, 0:cw],
                                    wsrc[f * 128:(f + 1) * 128, c0:c0 + cw]))
                                widx += 1
                                while (nxt_a2[0] < len(my_a2)
                                       and a2_pos[nxt_a2[0]] <= widx):
                                    emit_a2_load(sync)
                    while nxt_a2[0] < len(my_a2):
                        emit_a2_load(sync)

                @blk.tensor
                def _(tensor):
                    widx = 0
                    nev = 0
                    for grp in kgroups:
                        if nev:
                            wait(tensor, dv_e, base[dv_e] + nev)
                        for f in range(NF):
                            wait(tensor, dv_h, base[dv_h] + widx + 1)
                            wait(tensor, dv_x, base[dv_x] + widx + 1)
                            for i, dc in enumerate(grp):
                                inst = mm3(tensor, bank(i),
                                           wsl[widx % 2][0][:, i * 128:(i + 1) * 128],
                                           wsl[widx % 2][1][:, i * 128:(i + 1) * 128],
                                           xsl[widx % 2][0], xsl[widx % 2][1],
                                           f == 0, f == NF - 1)
                            pe_w.inc(inst)
                            widx += 1
                        nev += len(grp)
                    for dps in vdp_pairs:
                        wait(tensor, dv_e, base[dv_e] + nev)
                        for f in range(NF):
                            wait(tensor, dv_h, base[dv_h] + widx + 1)
                            wait(tensor, dv_x, base[dv_x] + widx + 1)
                            for st in range(nst):
                                xh = xsl[widx % 2][0][:, st * 128:(st + 1) * 128]
                                xl = xsl[widx % 2][1][:, st * 128:(st + 1) * 128]
                                for di in range(len(dps)):
                                    bi = st * len(dps) + di
                                    tensor.matmul(bank(bi, VDW), xh,
                                                  wsl[widx % 2][0][:, di * VDW:(di + 1) * VDW],
                                                  start=(f == 0), stop=False)
                                    tensor.matmul(bank(bi, VDW), xh,
                                                  wsl[widx % 2][1][:, di * VDW:(di + 1) * VDW],
                                                  start=False, stop=False)
                                for di in range(len(dps)):
                                    bi = st * len(dps) + di
                                    inst = tensor.matmul(
                                        bank(bi, VDW), xl,
                                        wsl[widx % 2][0][:, di * VDW:(di + 1) * VDW],
                                        start=False, stop=(f == NF - 1))
                            pe_w.inc(inst)
                            widx += 1
                        nev += nst * len(dps)

                slot_seq = []

                @blk.scalar
                def _(scalar):
                    wait(scalar, dv_a, base[dv_a] + 4 + NF)
                    widx = 0
                    for grp_kind, groups in (("k", kgroups), ("v", vdp_pairs)):
                        for grp in groups:
                            cw = len(grp) * (128 if grp_kind == "k" else VDW)
                            for f in range(NF):
                                wait(scalar, s_w[widx % 2], wmarks[widx])
                                if widx >= 2:
                                    wait(scalar, pe_w, base[pe_w] + widx - 1)
                                se_a.inc(scalar.activation(
                                    out=wsl[widx % 2][0][:, 0:cw],
                                    in_=wb2[widx % 2][:, 0:cw], func=AF.Copy))
                                se_a.inc(scalar.activation(
                                    out=xsl[widx % 2][0], in_=xbuf(f), func=AF.Copy))
                                widx += 1

                @blk.vector
                def _(vector):
                    ev = 0
                    cnt = list(kv_slot_cnt)
                    nxt_sp = [0]
                    nsplit = [0]
                    se0 = base[se_a] + NF + 1

                    def emit_wx_split(vector, grp_w, f):
                        widx = nsplit[0]
                        cw = grp_w
                        wait(vector, se_a, se0 + 2 * (widx + 1))
                        dv_h.inc(vector.tensor_tensor(wsl[widx % 2][1][:, 0:cw],
                                                      wb2[widx % 2][:, 0:cw],
                                                      wsl[widx % 2][0][:, 0:cw],
                                                      ALU.subtract))
                        dv_x.inc(vector.tensor_tensor(xsl[widx % 2][1], xbuf(f),
                                                      xsl[widx % 2][0], ALU.subtract))
                        nsplit[0] += 1

                    def emit_a2_splits(vector, upto):
                        while nxt_sp[0] < min(upto, len(my_a2)):
                            ci = nxt_sp[0]
                            asrc, dh, dl, cw = my_a2[ci]
                            k = a2_base + ci
                            slot = k % 2
                            wait(vector, s_p[slot], amarks[ci])
                            if k // 2:
                                wait(vector, g_a[slot], 32 * (k // 2))
                            vector.tensor_copy(a2o[slot][0][:, 0:cw], a2st[slot][:, 0:cw])
                            vector.drain()
                            dv_s.inc(vector.tensor_tensor(a2o[slot][1][:, 0:cw],
                                                          a2st[slot][:, 0:cw],
                                                          a2o[slot][0][:, 0:cw],
                                                          ALU.subtract))
                            nxt_sp[0] += 1

                    def a2_allow(t, wdone):
                        n = nxt_sp[0]
                        while n < len(my_a2):
                            if 3 * n + 2 > t:
                                break
                            if n >= 2 and (3 * (n - 1) + NF - 1) // NF > wdone:
                                break
                            n += 1
                        return n

                    wave = 0
                    for g, grp in enumerate(kgroups):
                        for f in range(NF):
                            emit_wx_split(vector, len(grp) * 128, f)
                            emit_a2_splits(vector, a2_allow(nsplit[0], wave))
                        wait(vector, pe_w, base[pe_w] + NF * (g + 1))
                        for i, dc in enumerate(grp):
                            slot = ev % 4
                            if cnt[slot]:
                                wait(vector, g_k[slot], 16 * cnt[slot])
                            slot_seq.append(slot)
                            cnt[slot] += 1
                            dv_e.inc(vector.tensor_copy(kevac[slot], bank(i)))
                            ev += 1
                        wave += 1
                        emit_a2_splits(vector, (NF * wave) // 3)
                    for g, dps in enumerate(vdp_pairs):
                        for f in range(NF):
                            emit_wx_split(vector, len(dps) * VDW, f)
                            emit_a2_splits(vector, a2_allow(nsplit[0], wave))
                        wait(vector, pe_w, base[pe_w] + NF * (len(kgroups) + g + 1))
                        bi = 0
                        for st in range(nst):
                            for di in range(len(dps)):
                                slot = ev % 4
                                if cnt[slot]:
                                    wait(vector, g_k[slot], 16 * cnt[slot])
                                slot_seq.append(slot)
                                cnt[slot] += 1
                                dv_e.inc(vector.tensor_copy(kevac[slot][:, 0:VDW],
                                                            bank(bi, VDW)))
                                bi += 1
                                ev += 1
                        wave += 1
                        emit_a2_splits(vector, (NF * wave) // 3)
                    emit_a2_splits(vector, len(my_a2))

                @blk.gpsimd
                def _(g):
                    ev = 0
                    nxt_st = [0]

                    def emit_a2_stores(g, upto):
                        while nxt_st[0] < min(upto, len(my_a2)):
                            ci = nxt_st[0]
                            asrc, dh, dl, cw = my_a2[ci]
                            k = a2_base + ci
                            slot = k % 2
                            wait(g, dv_s, base[dv_s] + ci + 1)
                            if g_a[slot].n:
                                wait(g, g_a[slot], g_a[slot].n)
                            g_a[slot].inc(g.dma_start(out=dh, in_=a2o[slot][0][:, 0:cw]), 16)
                            g_a[slot].inc(g.dma_start(out=dl, in_=a2o[slot][1][:, 0:cw]), 16)
                            nxt_st[0] += 1

                    wave = 0
                    for grp in kgroups:
                        for dc in grp:
                            slot = slot_seq[ev]
                            wait(g, dv_e, base[dv_e] + ev + 1)
                            if g_k[slot].n:
                                wait(g, g_k[slot], g_k[slot].n)
                            g_k[slot].inc(g.dma_start(
                                out=KTs[dc * 128:(dc + 1) * 128, sc * SC:(sc + 1) * SC],
                                in_=kevac[slot]), 16)
                            kv_slot_cnt[slot] += 1
                            ev += 1
                        wave += 1
                        emit_a2_stores(g, (NF * wave) // 3)
                    for dps in vdp_pairs:
                        for st in range(nst):
                            for dp in dps:
                                slot = slot_seq[ev]
                                wait(g, dv_e, base[dv_e] + ev + 1)
                                if g_k[slot].n:
                                    wait(g, g_k[slot], g_k[slot].n)
                                g_k[slot].inc(g.dma_start(
                                    out=Vs[(sc * nst + st) * 128:(sc * nst + st + 1) * 128,
                                           dp * VDW:(dp + 1) * VDW],
                                    in_=kevac[slot][:, 0:VDW]), 16)
                                kv_slot_cnt[slot] += 1
                                ev += 1
                        wave += 1
                        emit_a2_stores(g, (NF * wave) // 3)
                    emit_a2_stores(g, len(my_a2))
                    if sc == NSC - 1:
                        for s2 in range(4):
                            if g_k[s2].n:
                                wait(g, g_k[s2], g_k[s2].n)
                        for s2 in range(2):
                            if g_a[s2].n:
                                wait(g, g_a[s2], g_a[s2].n)

        for qc in range(NQC):
            with nc.named_scope(f"phB2_{qc}"), nc.Block(no_gpsimd_drain=True) as blk:
                base = snap()
                emit_ln(blk, xTq, qc * QCH, QCH, base)

                @blk.scalar
                def _(scalar):
                    wait(scalar, dv_a, base[dv_a] + 4 + NF)
                    for f in range(NF):
                        se_a.inc(scalar.activation(
                            out=xqh[:, f * SQ + qc * QCH:f * SQ + (qc + 1) * QCH],
                            in_=xbuf(f)[:, 0:QCH], func=AF.Copy))

                @blk.vector
                def _(vector):
                    for f in range(NF):
                        wait(vector, se_a, base[se_a] + NF + 1 + f + 1)
                        dv_a.inc(vector.tensor_tensor(
                            xql[:, f * SQ + qc * QCH:f * SQ + (qc + 1) * QCH],
                            xbuf(f)[:, 0:QCH],
                            xqh[:, f * SQ + qc * QCH:f * SQ + (qc + 1) * QCH],
                            ALU.subtract))

        def xqh_t(f, q0):
            return xqh[:, f * SQ + q0:f * SQ + q0 + QCH]

        def xql_t(f, q0):
            return xql[:, f * SQ + q0:f * SQ + q0 + QCH]

        for qc in range(NQC):
            q0 = qc * QCH
            with nc.named_scope(f"phC{qc}"), nc.Block(no_gpsimd_drain=True) as blk:
                base = snap()

                sched = []
                for e in range(2):
                    for gi, grp in enumerate(fgroups):
                        sched.append(('h', e, gi, grp))
                        sched.append(('o', e, gi, grp))

                tiles_per_step = []
                tile_srcs = []
                for kind, e, gi, grp in sched:
                    ha = (sWah, sWal) if e == 0 else (sW1ah, sW1al)
                    hb = (sWbh, sWbl) if e == 0 else (sW1bh, sW1bl)
                    if kind == 'h':
                        c0, cw = grp[0] * 128, len(grp) * 128
                        srcs = [(ha[0][f * 128:(f + 1) * 128, c0:c0 + cw],
                                 ha[1][f * 128:(f + 1) * 128, c0:c0 + cw], cw)
                                for f in range(NF)]
                    else:
                        srcs = []
                        for og in ogroups:
                            oc0, ocw = og[0] * 128, len(og) * 128
                            for fc in grp:
                                srcs.append((hb[0][fc * 128:(fc + 1) * 128, oc0:oc0 + ocw],
                                             hb[1][fc * 128:(fc + 1) * 128, oc0:oc0 + ocw],
                                             ocw))
                    tiles_per_step.append(len(srcs))
                    tile_srcs.append(srcs)
                tile_start = [0]
                for n in tiles_per_step:
                    tile_start.append(tile_start[-1] + n)

                comp = {}
                seo_c = 0
                dvo_c = 0
                for si, (kind, e, gi, grp) in enumerate(sched):
                    if kind != 'o':
                        continue
                    for bi2, og in enumerate(ogroups):
                        if e == 0 and gi == 0:
                            seo_c += len(og)
                            comp[(si, bi2)] = (se_o, seo_c)
                        else:
                            dvo_c += 1
                            comp[(si, bi2)] = (dv_o, dvo_c)
                    if e == 0 and gi == len(fgroups) - 1:
                        dvo_c += 1
                total_dvo = dvo_c

                cmarks = []

                @blk.sync
                def _(sync):
                    if qc == 0:
                        for s2 in range(2):
                            wait(sync, g_a[s2], g_a[s2].n)
                    widx = 0
                    for si in range(len(sched)):
                        for (hsrc, lsrc, cw) in tile_srcs[si]:
                            if widx >= 2:
                                wait(sync, pe_w, base[pe_w] + widx - 1)
                            cmarks.append(pair_load(
                                sync, s_w[widx % 2],
                                (wsl[widx % 2][0][:, 0:cw], wsl[widx % 2][1][:, 0:cw]),
                                (hsrc, lsrc)))
                            widx += 1

                @blk.tensor
                def _(tensor):
                    wait(tensor, dv_a, base[dv_a] + 1)
                    for f in range(NF):
                        inst = mm3(tensor, ps[0:1, 0:QCH], wgd_hi[:, f:f + 1],
                                   wgd_lo[:, f:f + 1], xqh_t(f, q0), xql_t(f, q0),
                                   f == 0, f == NF - 1)
                    pe_a.inc(inst)
                    wait(tensor, dv_a, base[dv_a] + 2)
                    pe_a.inc(tensor.matmul(bank(1, QCH), ones[0:1, :], r_gmask[:, 0:QCH],
                                           start=True, stop=True))
                    wait(tensor, dv_a, base[dv_a] + 4)
                    widx = 0
                    n_seh = 0
                    n_dvh = 0
                    last_comp = None
                    for si, (kind, e, gi, grp) in enumerate(sched):
                        if kind == 'h':
                            if last_comp is not None:
                                wait(tensor, last_comp[0], base[last_comp[0]] + last_comp[1])
                            n_dvh_c = n_dvh
                            for f in range(NF):
                                wait(tensor, s_w[widx % 2], cmarks[widx])
                                cw = len(grp) * 128
                                for i in range(len(grp)):
                                    inst = mm3(tensor, bank(i, QCH),
                                               wsl[widx % 2][0][:, i * 128:(i + 1) * 128],
                                               wsl[widx % 2][1][:, i * 128:(i + 1) * 128],
                                               xqh_t(f, q0), xql_t(f, q0),
                                               f == 0, f == NF - 1)
                                pe_w.inc(inst)
                                widx += 1
                            n_seh += 2 * len(grp)
                        else:
                            for bi2, og in enumerate(ogroups):
                                if bi2 > 0:
                                    c = comp[(si, bi2 - 1)]
                                    wait(tensor, c[0], base[c[0]] + c[1])
                                elif n_seh:
                                    wait(tensor, se_a, base[se_a] + n_seh)
                                for ti, fc in enumerate(grp):
                                    wait(tensor, dv_h, base[dv_h] + n_dvh + ti + 1)
                                    wait(tensor, s_w[widx % 2], cmarks[widx])
                                    hh = xn_hi[:, ti * QCH:(ti + 1) * QCH]
                                    hl = xn_lo[:, ti * QCH:(ti + 1) * QCH]
                                    for i in range(len(og)):
                                        inst = mm3(tensor, bank(i, QCH),
                                                   wsl[widx % 2][0][:, i * 128:(i + 1) * 128],
                                                   wsl[widx % 2][1][:, i * 128:(i + 1) * 128],
                                                   hh, hl, ti == 0, ti == len(grp) - 1)
                                    pe_w.inc(inst)
                                    widx += 1
                            n_dvh += len(grp)
                            last_comp = comp[(si, len(ogroups) - 1)]

                @blk.scalar
                def _(scalar):
                    wait(scalar, g_st, base[g_st])
                    widx_end_h = {}
                    widx = 0
                    for si2 in range(len(sched)):
                        widx += tiles_per_step[si2]
                        widx_end_h[si2] = widx
                    for si, (kind, e, gi, grp) in enumerate(sched):
                        if kind == 'h':
                            ba = c_b0a if e == 0 else c_b1a
                            fn = AF.Gelu if e == 0 else AF.Relu
                            wait(scalar, pe_w, base[pe_w] + widx_end_h[si])
                            for i, fc in enumerate(grp):
                                se_a.inc(scalar.activation(out=h32(i), in_=bank(i, QCH),
                                                           func=fn, bias=ba[:, fc:fc + 1]))
                                se_a.inc(scalar.activation(
                                    out=xn_hi[:, i * QCH:(i + 1) * QCH], in_=bank(i, QCH),
                                    func=fn, bias=ba[:, fc:fc + 1]))
                        else:
                            bb = c_b0b if e == 0 else c_b1b
                            if gi == 0:
                                for bi2, og in enumerate(ogroups):
                                    wait(scalar, pe_w,
                                         base[pe_w] + tile_start[si] + (bi2 + 1) * len(grp))
                                    for i, oc in enumerate(og):
                                        if e == 0:
                                            se_o.inc(scalar.activation(
                                                out=out0(oc), in_=bank(i, QCH),
                                                func=AF.Identity, bias=bb[:, oc:oc + 1]))
                                        else:
                                            se_o.inc(scalar.activation(
                                                out=RA[:, (8 + i) * QCH:(9 + i) * QCH],
                                                in_=bank(i, QCH),
                                                func=AF.Identity, bias=bb[:, oc:oc + 1]))

                @blk.vector
                def _(vector):
                    if qc == 0:
                        for s2 in range(2):
                            wait(vector, g_a[s2], g_a[s2].n)
                    vector.tensor_tensor(wgd, wg_sb[:, 0::2], wg_sb[:, 1::2], ALU.subtract)
                    vector.drain()
                    vector.tensor_copy(wgd_hi, wgd)
                    vector.drain()
                    dv_a.inc(vector.tensor_tensor(wgd_lo, wgd, wgd_hi, ALU.subtract))
                    wait(vector, pe_a, base[pe_a] + 1)
                    dv_a.inc(vector.tensor_scalar(r_gmask[:, 0:QCH], ps[0:1, 0:QCH],
                                                  0.0, None, ALU.is_ge))
                    wait(vector, pe_a, base[pe_a] + 2)
                    dv_a.inc(vector.tensor_copy(mask_b[:, 0:QCH], bank(1, QCH)))
                    vector.drain()
                    dv_a.inc(vector.tensor_scalar(invm[:, 0:QCH], mask_b[:, 0:QCH],
                                                  -1.0, 1.0, ALU.mult, ALU.add))
                    n_seh = 0
                    n_seo = 0
                    for si, (kind, e, gi, grp) in enumerate(sched):
                        if kind == 'h':
                            for i in range(len(grp)):
                                n_seh += 2
                                wait(vector, se_a, base[se_a] + n_seh)
                                dv_h.inc(vector.tensor_tensor(
                                    xn_lo[:, i * QCH:(i + 1) * QCH], h32(i),
                                    xn_hi[:, i * QCH:(i + 1) * QCH], ALU.subtract))
                        else:
                            for bi2, og in enumerate(ogroups):
                                if e == 0 and gi == 0:
                                    continue
                                pw = base[pe_w] + tile_start[si] + (bi2 + 1) * len(grp)
                                wait(vector, pe_w, pw)
                                if e == 0:
                                    vector.drain()
                                    for i, oc in enumerate(og):
                                        if i == len(og) - 1:
                                            dv_o.inc(vector.tensor_tensor(
                                                out0(oc), out0(oc), bank(i, QCH), ALU.add))
                                        else:
                                            vector.tensor_tensor(out0(oc), out0(oc),
                                                                 bank(i, QCH), ALU.add)
                                else:
                                    vector.drain()
                                    if gi == 0:
                                        wait(vector, se_o,
                                             base[se_o] + NF + (bi2 + 1) * len(og))
                                        for i, oc in enumerate(og):
                                            vector.tensor_tensor(
                                                RA[:, (8 + i) * QCH:(9 + i) * QCH],
                                                RA[:, (8 + i) * QCH:(9 + i) * QCH],
                                                invm[:, 0:QCH], ALU.mult)
                                    else:
                                        for i, oc in enumerate(og):
                                            vector.tensor_tensor(
                                                RA[:, (8 + i) * QCH:(9 + i) * QCH],
                                                bank(i, QCH), invm[:, 0:QCH], ALU.mult)
                                    vector.drain()
                                    for i, oc in enumerate(og):
                                        if i == len(og) - 1:
                                            dv_o.inc(vector.tensor_tensor(
                                                out0(oc), out0(oc),
                                                RA[:, (8 + i) * QCH:(9 + i) * QCH], ALU.add))
                                        else:
                                            vector.tensor_tensor(
                                                out0(oc), out0(oc),
                                                RA[:, (8 + i) * QCH:(9 + i) * QCH], ALU.add)
                            if e == 0 and gi == len(fgroups) - 1:
                                wait(vector, se_o, base[se_o] + NF)
                                vector.drain()
                                for oc in range(NF):
                                    if oc == NF - 1:
                                        dv_o.inc(vector.tensor_tensor(
                                            out0(oc), out0(oc), mask_b[:, 0:QCH], ALU.mult))
                                    else:
                                        vector.tensor_tensor(out0(oc), out0(oc),
                                                             mask_b[:, 0:QCH], ALU.mult)

                @blk.gpsimd
                def _(g):
                    wait(g, dv_o, base[dv_o] + total_dvo)
                    for oc in range(NF):
                        g_st.inc(g.dma_start(
                            out=Fs[oc * 128:(oc + 1) * 128, q0:q0 + QCH],
                            in_=out0(oc)), 16)
                    g.wait_ge(g_st.h, g_st.n)

        nqp = (SQ + 511) // 512
        for h in range(NH):
            with nc.named_scope(f"phD{h}"), nc.Block(no_gpsimd_drain=True) as blk:
                base = snap()

                def pt_base(i0):
                    return sum(j + 1 for j in range(i0))

                def dva_base(i0):
                    return sum((j + 1) + 2 for j in range(i0))

                def ev_base(i0):
                    return nqp + sum((j + 1) + 1 for j in range(i0))

                def sci(i0, kc):
                    return sum(j + 1 for j in range(i0)) + kc
                nsci = sum(j + 1 for j in range(NQT))

                kmarks = []

                @blk.sync
                def _(sync):
                    if h == 0:
                        wait(sync, g_st, base[g_st])
                    for kb in range(NKB):
                        s_ld.inc(sync.dma_start(
                            out=Vb[:, kb * 128:(kb + 1) * 128],
                            in_=Vs[kb * 128:(kb + 1) * 128, h * 128:(h + 1) * 128]), 16)
                    for hf in range(2):
                        if hf == 1:
                            wait(sync, pe_a, base[pe_a] + 1)
                        for fi in range(FHALF):
                            f = hf * FHALF + fi
                            s_ld.inc(sync.dma_start(
                                out=wqh_b[:, fi * 128:(fi + 1) * 128],
                                in_=swqh[f * 128:(f + 1) * 128, h * 128:(h + 1) * 128]), 16)
                            s_ld.inc(sync.dma_start(
                                out=wql_b[:, fi * 128:(fi + 1) * 128],
                                in_=swql[f * 128:(f + 1) * 128, h * 128:(h + 1) * 128]), 16)
                        sync.wait_ge(s_ld.h, s_ld.n)
                    c = 0
                    for i0 in range(NQT):
                        for kc in range(i0 + 1):
                            slot = c % 3
                            if c >= 3:
                                wait(sync, pe_a, base[pe_a] + c)
                            kmarks.append(slot_load(
                                sync, s_k[slot], kcs[slot],
                                KTs[h * 128:(h + 1) * 128, kc * 512:(kc + 1) * 512]))
                            c += 1

                sbanks = [0, 1, 5, 6, 7]

                def chunk2tile(c):
                    i0 = 0
                    while c >= i0 + 1:
                        c -= i0 + 1
                        i0 += 1
                    return i0, c

                def emit_scores(tensor, i0, k0, k1):
                    for kc in range(k0, k1):
                        c = sci(i0, kc)
                        bk = sbanks[c % 5]
                        if c == 0:
                            wait(tensor, dv_e, base[dv_e] + nqp)
                        if c >= 5:
                            i0p, kcp = chunk2tile(c - 5)
                            wait(tensor, dv_a, base[dv_a] + dva_base(i0p) + kcp + 1)
                        wait(tensor, s_k[c % 3], kmarks[c])
                        pe_a.inc(tensor.matmul(
                            bank(bk), Qh[:, i0 * 128:(i0 + 1) * 128],
                            kcs[c % 3], start=True, stop=True))

                def emit_av(tensor, i0, g, ng):
                    wait(tensor, dv_e, base[dv_e] + ev_base(i0) + g + 1)
                    for t in range(4):
                        kb = 4 * g + t
                        inst = tensor.matmul(
                            ps[:, 4 * 512:4 * 512 + 128], Vb[:, kb * 128:(kb + 1) * 128],
                            ptb[g % 2][:, t * 128:(t + 1) * 128],
                            start=(kb == 0), stop=(kb == 4 * ng - 1))
                    pe_o.inc(inst)

                @blk.tensor
                def _(tensor):
                    for hf in range(2):
                        wait(tensor, s_ld,
                             base[s_ld] + 16 * (NKB + 2 * FHALF * (hf + 1)))
                        for qp in range(nqp):
                            qw = min(512, SQ - qp * 512)
                            for fi in range(FHALF):
                                f = hf * FHALF + fi
                                inst = mm3(tensor, bank(5 + qp, qw),
                                           wqh_b[:, fi * 128:(fi + 1) * 128],
                                           wql_b[:, fi * 128:(fi + 1) * 128],
                                           xqh[:, f * SQ + qp * 512:f * SQ + qp * 512 + qw],
                                           xql[:, f * SQ + qp * 512:f * SQ + qp * 512 + qw],
                                           hf == 0 and fi == 0,
                                           hf == 1 and fi == FHALF - 1)
                        pe_a.inc(inst)
                    emit_scores(tensor, 0, 0, 1)
                    for i0 in range(NQT):
                        ng = i0 + 1
                        if i0 + 1 < NQT:
                            emit_scores(tensor, i0 + 1, 0, min(5, i0 + 2))
                        wait(tensor, se_a, base[se_a] + 2 * (i0 + 1))
                        for g in range(ng):
                            if g >= 2:
                                wait(tensor, dv_e, base[dv_e] + ev_base(i0) + g - 1)
                            for t in range(4):
                                inst = tensor.transpose(
                                    ps[:, (2 + g % 2) * 512 + t * 128:
                                       (2 + g % 2) * 512 + (t + 1) * 128],
                                    SP_sb[:, (4 * g + t) * 128:(4 * g + t + 1) * 128],
                                    ident)
                            pe_t.inc(inst)
                            if g >= 1:
                                emit_av(tensor, i0, g - 1, ng)
                        emit_av(tensor, i0, ng - 1, ng)
                        if i0 + 1 < NQT and i0 + 2 > 5:
                            emit_scores(tensor, i0 + 1, 5, i0 + 2)

                @blk.vector
                def _(vector):
                    for qp in range(nqp):
                        qw = min(512, SQ - qp * 512)
                        wait(vector, pe_a, base[pe_a] + 2)
                        dv_e.inc(vector.tensor_scalar_mul(Qh[:, qp * 512:qp * 512 + qw],
                                                          bank(5 + qp, qw), scale))
                    for i0 in range(NQT):
                        ng = i0 + 1
                        nkc = i0 + 1
                        wait(vector, se_a, base[se_a] + 2 * i0)
                        wait(vector, pe_t, base[pe_t] + pt_base(i0))
                        for kc in range(nkc):
                            c = sci(i0, kc)
                            bk = sbanks[c % 5]
                            wait(vector, pe_a, base[pe_a] + 2 + c + 1)
                            if kc == nkc - 1:
                                vector.tensor_tensor(bank(bk), bank(bk), maskt, ALU.add)
                                vector.drain()
                            vector.reduce_max(mx[:, kc:kc + 1], bank(bk), axis=AX.X)
                            dv_a.inc(vector.tensor_copy(SP_sb[:, kc * 512:(kc + 1) * 512],
                                                        bank(bk)))
                        vector.drain()
                        dv_a.inc(vector.reduce_max(negm, mx[:, 0:nkc], axis=AX.X,
                                                   negate=True))
                        wait(vector, se_a, base[se_a] + 2 * i0 + 1)
                        vector.tensor_scalar_add(rden, den, 1e-8)
                        vector.drain()
                        dv_a.inc(vector.reciprocal(rden, rden))
                        for g in range(ng):
                            wait(vector, pe_t, base[pe_t] + pt_base(i0) + g + 1)
                            if g >= 2:
                                wait(vector, pe_o, base[pe_o] + pt_base(i0) + g - 1)
                            dv_e.inc(vector.tensor_copy(
                                ptb[g % 2],
                                ps[:, (2 + g % 2) * 512:(3 + g % 2) * 512]))
                        wait(vector, pe_o, base[pe_o] + pt_base(i0) + ng)
                        dv_e.inc(vector.tensor_copy(
                            attnbuf(h, i0 * 128, (i0 + 1) * 128),
                            ps[:, 4 * 512:4 * 512 + 128]))

                @blk.scalar
                def _(scalar):
                    for i0 in range(NQT):
                        E = 4 * i0 + 4
                        nkc = i0 + 1
                        wait(scalar, dv_a, base[dv_a] + dva_base(i0) + nkc + 1)
                        wait(scalar, pe_t, base[pe_t] + pt_base(i0))
                        se_a.inc(scalar.activation(out=SP_sb[:, 0:E * 128],
                                                   in_=SP_sb[:, 0:E * 128], func=AF.Exp,
                                                   scale=1.0, bias=negm, accum_out=den))
                        wait(scalar, dv_a, base[dv_a] + dva_base(i0) + nkc + 2)
                        se_a.inc(scalar.activation(out=SP_sb[:, 0:E * 128],
                                                   in_=SP_sb[:, 0:E * 128],
                                                   func=AF.Identity,
                                                   bias=zero_col[:, 0:1], scale=rden))

        with nc.named_scope("phE"), nc.Block(no_gpsimd_drain=True) as blk:
            base = snap()
            EQC = SQ // QCH
            wmarks = []
            pmarks = []
            smarks = []

            @blk.sync
            def _(sync):
                wait(sync, g_st, base[g_st])
                widx = 0
                lidx = 0
                for grp in ogroups:
                    c0, cw = grp[0] * 128, len(grp) * 128
                    for qcc in range(EQC):
                        q0 = qcc * QCH
                        for f in range(NF):
                            if widx >= 2:
                                wait(sync, pe_w, base[pe_w] + widx - 1)
                            wmarks.append(pair_load(
                                sync, s_w[widx % 2],
                                (wsl[widx % 2][0][:, 0:cw], wsl[widx % 2][1][:, 0:cw]),
                                (swoh[f * 128:(f + 1) * 128, c0:c0 + cw],
                                 swol[f * 128:(f + 1) * 128, c0:c0 + cw])))
                            widx += 1
                        for oc in grp:
                            slot = lidx % 2
                            if lidx >= 2:
                                wait(sync, dv_a, base[dv_a] + lidx - 1)
                            if s_p[slot].n:
                                wait(sync, s_p[slot], s_p[slot].n)
                            s_p[slot].inc(sync.dma_start(
                                out=ld2[2 * slot][:, 0:QCH],
                                in_=Fs[oc * 128:(oc + 1) * 128, q0:q0 + QCH]), 16)
                            s_p[slot].inc(sync.dma_start(
                                out=ld2[2 * slot + 1][:, 0:QCH],
                                in_=xTq[oc * 128:(oc + 1) * 128, q0:q0 + QCH]), 16)
                            pmarks.append(s_p[slot].n)
                            lidx += 1

            @blk.vector
            def _(vector):
                widx = 0
                lidx = 0
                for grp in ogroups:
                    for qcc in range(EQC):
                        q0 = qcc * QCH
                        for f in range(NF):
                            slot = widx % 2
                            if widx >= 2:
                                wait(vector, pe_w, base[pe_w] + widx - 1)
                            vector.tensor_copy(aspl[slot][0][:, 0:QCH],
                                               attnbuf(f, q0, q0 + QCH))
                            vector.drain()
                            dv_s.inc(vector.tensor_tensor(
                                aspl[slot][1][:, 0:QCH], attnbuf(f, q0, q0 + QCH),
                                aspl[slot][0][:, 0:QCH], ALU.subtract))
                            widx += 1
                        gq = widx // NF
                        wait(vector, pe_w, base[pe_w] + gq * NF)
                        for i, oc in enumerate(grp):
                            slot = lidx % 2
                            wait(vector, s_p[slot], pmarks[lidx])
                            if lidx >= 2:
                                wait(vector, g_o[slot], 16 * (lidx // 2))
                            dv_e.inc(vector.tensor_tensor(
                                ob2[slot], bank(i, QCH), ld2[2 * slot][:, 0:QCH], ALU.add))
                            vector.drain()
                            dv_a.inc(vector.tensor_tensor(
                                ob2[slot], ob2[slot], ld2[2 * slot + 1][:, 0:QCH], ALU.add))
                            lidx += 1

            @blk.tensor
            def _(tensor):
                widx = 0
                nev = 0
                for grp in ogroups:
                    for qcc in range(EQC):
                        if nev:
                            wait(tensor, dv_e, base[dv_e] + nev)
                        for f in range(NF):
                            slot = widx % 2
                            wait(tensor, s_w[slot], wmarks[widx])
                            wait(tensor, dv_s, base[dv_s] + widx + 1)
                            for i in range(len(grp)):
                                inst = mm3(tensor, bank(i, QCH),
                                           wsl[slot][0][:, i * 128:(i + 1) * 128],
                                           wsl[slot][1][:, i * 128:(i + 1) * 128],
                                           aspl[slot][0][:, 0:QCH], aspl[slot][1][:, 0:QCH],
                                           f == 0, f == NF - 1)
                            pe_w.inc(inst)
                            widx += 1
                        nev += len(grp)

            @blk.gpsimd
            def _(g):
                lidx = 0
                for grp in ogroups:
                    for qcc in range(EQC):
                        q0 = qcc * QCH
                        for oc in grp:
                            slot = lidx % 2
                            wait(g, dv_a, base[dv_a] + lidx + 1)
                            if g_o[slot].n:
                                wait(g, g_o[slot], g_o[slot].n)
                            g_o[slot].inc(g.dma_start(
                                out=outT[oc * 128:(oc + 1) * 128, q0:q0 + QCH],
                                in_=ob2[slot]), 16)
                            lidx += 1
                g.wait_ge(g_o[0].h, g_o[0].n)
                g.wait_ge(g_o[1].h, g_o[1].n)

    return nc


def _make_masks(j):
    r = np.arange(128)[:, None]
    out = np.zeros((128, 512), dtype=np.float32)
    for t in range(4):
        c = np.arange(128)[None, :]
        allowed = (128 * t + c) <= (4 * r + j)
        out[:, t * 128:(t + 1) * 128] = np.where(allowed, 0.0, NEG_BIG).astype(np.float32)
    return out


def _consts_tile():
    c = np.zeros((128, 272), dtype=np.float32)
    c[:, 0:128] = 1.0
    c[:, 128:256] = np.eye(128, dtype=np.float32)
    c[:, 256] = 1e-5
    return c


def kernel(L0_attn_x, L0_attn_wq, L0_attn_wk, L0_attn_wv, L0_attn_wo,
           L0_ffn_Wg, L0_ffn_W0a, L0_ffn_b0a, L0_ffn_W0b, L0_ffn_b0b,
           L0_ffn_W1a, L0_ffn_b1a, L0_ffn_W1b, L0_ffn_b1b):
    x = np.asarray(L0_attn_x)
    B, S, H = x.shape
    NH = 16
    FF = np.asarray(L0_ffn_W0a).shape[0]
    NF = H // 128
    NFC = FF // 128

    nc = build_nc(H=H, S=S, NH=NH, FF=FF)

    def t(a):
        return np.ascontiguousarray(np.asarray(a, dtype=np.float32).T)

    def b2(a, n):
        return np.ascontiguousarray(np.asarray(a, dtype=np.float32).reshape(n, 128).T)

    shared = {
        "wqT": t(L0_attn_wq), "wkT": t(L0_attn_wk), "wvT": t(L0_attn_wv),
        "woT": t(L0_attn_wo), "wgT": t(L0_ffn_Wg),
        "W0aT": t(L0_ffn_W0a), "W0bT": t(L0_ffn_W0b),
        "W1aT": t(L0_ffn_W1a), "W1bT": t(L0_ffn_W1b),
        "b0a2": b2(L0_ffn_b0a, NFC), "b1a2": b2(L0_ffn_b1a, NFC),
        "b0b2": b2(L0_ffn_b0b, NF), "b1b2": b2(L0_ffn_b1b, NF),
        "consts": _consts_tile(),
    }
    in_maps = []
    for core in range(8):
        b, j = core // 4, core % 4
        im = dict(shared)
        im["xTs"] = np.ascontiguousarray(x[b].T)
        im["xTq"] = np.ascontiguousarray(x[b, j::4, :].T)
        im["masks"] = _make_masks(j)
        in_maps.append(im)

    res = run_bass_kernel_spmd(nc, in_maps, core_ids=list(range(8)))

    out = np.empty((B, S, H), dtype=np.float32)
    for core in range(8):
        b, j = core // 4, core % 4
        out[b, j::4, :] = res.results[core]["outT"].T
    return out


# version 19
# speedup vs baseline: 1.0125x; 1.0125x over previous
import math
import contextlib
import numpy as np
import concourse.bass as bass
import concourse.mybir as mybir
from concourse.bass_utils import run_bass_kernel_spmd

F32 = mybir.dt.float32
BF16 = mybir.dt.bfloat16
AF = mybir.ActivationFunctionType
ALU = mybir.AluOpType
AX = mybir.AxisListType

NEG_BIG = -1.0e30


def build_nc(H=2048, S=4096, NH=16, FF=4096):
    HD = H // NH
    assert HD == 128 and H % 128 == 0 and FF % 128 == 0 and S % 512 == 0
    NF = H // 128
    SC = 512
    NSC = S // SC
    SQ = S // 4
    NQT = SQ // 128
    NFC = FF // 128
    QCH = min(512, SQ)
    NQC = SQ // QCH
    NKB = S // 128
    scale = 1.0 / math.sqrt(HD)

    nc = bass.Bass("TRN2", target_bir_lowering=False)

    xTs = nc.dram_tensor("xTs", [H, S], F32, kind="ExternalInput").ap()
    xTq = nc.dram_tensor("xTq", [H, SQ], F32, kind="ExternalInput").ap()
    wqT = nc.dram_tensor("wqT", [H, H], F32, kind="ExternalInput").ap()
    wkT = nc.dram_tensor("wkT", [H, H], F32, kind="ExternalInput").ap()
    wvT = nc.dram_tensor("wvT", [H, H], F32, kind="ExternalInput").ap()
    woT = nc.dram_tensor("woT", [H, H], F32, kind="ExternalInput").ap()
    wgT = nc.dram_tensor("wgT", [H, 2], F32, kind="ExternalInput").ap()
    W0aT = nc.dram_tensor("W0aT", [H, FF], F32, kind="ExternalInput").ap()
    W0bT = nc.dram_tensor("W0bT", [FF, H], F32, kind="ExternalInput").ap()
    W1aT = nc.dram_tensor("W1aT", [H, FF], F32, kind="ExternalInput").ap()
    W1bT = nc.dram_tensor("W1bT", [FF, H], F32, kind="ExternalInput").ap()
    b0a2 = nc.dram_tensor("b0a2", [128, NFC], F32, kind="ExternalInput").ap()
    b1a2 = nc.dram_tensor("b1a2", [128, NFC], F32, kind="ExternalInput").ap()
    b0b2 = nc.dram_tensor("b0b2", [128, NF], F32, kind="ExternalInput").ap()
    b1b2 = nc.dram_tensor("b1b2", [128, NF], F32, kind="ExternalInput").ap()
    masks_in = nc.dram_tensor("masks", [128, 512], F32, kind="ExternalInput").ap()
    consts = nc.dram_tensor("consts", [128, 272], F32, kind="ExternalInput").ap()
    outT = nc.dram_tensor("outT", [H, SQ], F32, kind="ExternalOutput").ap()
    KTs = nc.dram_tensor("KTs", [H, S], F32, kind="Internal").ap()
    Vs = nc.dram_tensor("Vs", [S, H], F32, kind="Internal").ap()
    Fs = nc.dram_tensor("Fs", [H, SQ], F32, kind="Internal").ap()
    sWah = nc.dram_tensor("sWah", [H, FF], BF16, kind="Internal").ap()
    sWal = nc.dram_tensor("sWal", [H, FF], BF16, kind="Internal").ap()
    sWbh = nc.dram_tensor("sWbh", [FF, H], BF16, kind="Internal").ap()
    sWbl = nc.dram_tensor("sWbl", [FF, H], BF16, kind="Internal").ap()
    sW1ah = nc.dram_tensor("sW1ah", [H, FF], BF16, kind="Internal").ap()
    sW1al = nc.dram_tensor("sW1al", [H, FF], BF16, kind="Internal").ap()
    sW1bh = nc.dram_tensor("sW1bh", [FF, H], BF16, kind="Internal").ap()
    sW1bl = nc.dram_tensor("sW1bl", [FF, H], BF16, kind="Internal").ap()
    swqh = nc.dram_tensor("swqh", [H, H], BF16, kind="Internal").ap()
    swql = nc.dram_tensor("swql", [H, H], BF16, kind="Internal").ap()
    swoh = nc.dram_tensor("swoh", [H, H], BF16, kind="Internal").ap()
    swol = nc.dram_tensor("swol", [H, H], BF16, kind="Internal").ap()

    kgroups = [list(range(g, min(g + 8, NF))) for g in range(0, NF, 8)]
    ndp = max(1, H // 512)
    VDW = min(512, H)
    vdp_pairs = [list(range(d, min(d + 2, ndp))) for d in range(0, ndp, 2)]
    nst = SC // 128
    fgroups = [list(range(g, min(g + 8, NFC))) for g in range(0, NFC, 8)]
    ogroups = [list(range(g, min(g + 8, NF))) for g in range(0, NF, 8)]
    GW = 8 * QCH

    a2_chunks = []
    for srcT, dh, dl in ((W0aT, sWah, sWal), (W0bT, sWbh, sWbl),
                         (W1aT, sW1ah, sW1al), (W1bT, sW1bh, sW1bl),
                         (wqT, swqh, swql), (woT, swoh, swol)):
        rows, cols = srcT.tensor.shape
        for rb in range(rows // 128):
            for c0 in range(0, cols, 2048):
                cw = min(2048, cols - c0)
                a2_chunks.append((srcT[rb * 128:(rb + 1) * 128, c0:c0 + cw],
                                  dh[rb * 128:(rb + 1) * 128, c0:c0 + cw],
                                  dl[rb * 128:(rb + 1) * 128, c0:c0 + cw], cw))

    ctx = contextlib.ExitStack()
    with ctx:
        xqh = ctx.enter_context(nc.sbuf_tensor("xqh", [128, NF * SQ], BF16))
        xql = ctx.enter_context(nc.sbuf_tensor("xql", [128, NF * SQ], BF16))
        RA = ctx.enter_context(nc.sbuf_tensor("RA", [128, 16384], F32))
        RB = ctx.enter_context(nc.sbuf_tensor("RB", [128, max(2 * S + 1536, 4096)], F32))
        XN = ctx.enter_context(nc.sbuf_tensor("XN", [128, 2 * max(GW, 4096)], BF16))
        WS = ctx.enter_context(nc.sbuf_tensor("WS", [128, 6144], BF16))
        RC = ctx.enter_context(nc.sbuf_tensor("RC", [128, 2304], F32))
        RS = ctx.enter_context(nc.sbuf_tensor("RS", [128, 1024], F32))
        ps = ctx.enter_context(nc.psum_tensor("ps", [128, 4096], F32))

        def bank(k, w=512):
            return ps[:, k * 512:k * 512 + w]

        ones = RS[:, 0:128]
        ident = RS[:, 128:256]
        eps_col = RS[:, 256:257]
        zero_col = RS[:, 257:258]
        maskt = RS[:, 272:784]
        c_b0a = RS[:, 784:784 + NFC]
        c_b1a = RS[:, 816:816 + NFC]
        c_b0b = RS[:, 848:848 + NF]
        c_b1b = RS[:, 864:864 + NF]
        wg_sb = RS[:, 880:880 + 2 * NF]
        mx = RS[:, 912:920]
        m_col = RS[:, 920:921]
        negm = RS[:, 921:922]
        den = RS[:, 922:923]
        rden = RS[:, 923:924]
        wgd = RS[:, 928:928 + NF]

        SP_sb = RB[:, 0:S]
        Vb = RB[:, S:2 * S]
        kcs = [RB[:, 2 * S + i * 512:2 * S + (i + 1) * 512] for i in range(3)]
        kevac = [RB[:, i * 512:(i + 1) * 512] for i in range(4)]
        sqbuf = RB[:, 2048:2560]
        rowA = RB[0:1, 2560:3072]
        rowB = RB[0:1, 3072:3584]
        rowC = RB[0:1, 3584:4096]
        r_mean = rowA
        r_msq, r_sd, r_mmr = rowB, rowB, rowB
        r_var, r_rstd = rowC, rowC

        def out0(oc):
            return RB[:, oc * QCH:(oc + 1) * QCH]
        ld2 = [RB[:, 2048 + i * 512:2048 + (i + 1) * 512] for i in range(4)]
        ob2 = [RB[:, i * 512:i * 512 + QCH] for i in range(2)]

        xn_hi = XN[:, 0:GW]
        xn_lo = XN[:, GW:2 * GW]
        a2o = [(XN[:, 0:2048], XN[:, 2048:4096]),
               (XN[:, 4096:6144], XN[:, 6144:8192])]

        wsl = [(WS[:, i * 2048:i * 2048 + 1024], WS[:, i * 2048 + 1024:(i + 1) * 2048])
               for i in range(2)]
        FHALF = max(1, NF // 2)
        wqh_b = WS[:, 4096:4096 + FHALF * 128]
        wql_b = WS[:, 4096 + FHALF * 128:4096 + 2 * FHALF * 128]
        aspl = [(WS[:, 4096 + i * 1024:4096 + i * 1024 + 512],
                 WS[:, 4096 + i * 1024 + 512:4096 + (i + 1) * 1024])
                for i in range(2)]
        xsl = [(WS[:, 4096 + i * 1024:4096 + i * 1024 + 512],
                WS[:, 4096 + i * 1024 + 512:4096 + (i + 1) * 1024])
               for i in range(2)]
        wgd_hi = WS[:, 4096:4096 + NF]
        wgd_lo = WS[:, 4096 + 64:4096 + 64 + NF]

        wb2 = [RC[:, 0:1024], RC[:, 1024:2048]]
        mask_b = RC[:, 0:512]
        r_gmask = RC[0:1, 0:512]
        invm = RC[:, 512:1024]
        Qh = RC[:, 0:SQ]
        ptb = [RC[:, 1024 + i * 512:1024 + (i + 1) * 512] for i in range(2)]

        def h32(i):
            return RA[:, i * QCH:(i + 1) * QCH]

        def xbuf(f):
            return RA[:, f * SC:(f + 1) * SC]

        a2st = [RA[:, 8192 + i * 2048:8192 + (i + 1) * 2048] for i in range(2)]

        def attnbuf(h, c0, c1):
            return RA[:, h * SQ + c0:h * SQ + c1]

        class Sem:
            def __init__(self, name):
                self.h = ctx.enter_context(nc.semaphore(name))
                self.n = 0

            def inc(self, inst, k=1):
                inst.then_inc(self.h, k)
                self.n += k
                return inst

        s_ld = Sem("s_ld")
        s_w = [Sem("s_w0"), Sem("s_w1")]
        s_p = [Sem("s_p0"), Sem("s_p1")]
        s_k = [Sem("s_k0"), Sem("s_k1"), Sem("s_k2")]
        s_kg = [Sem("s_kg0"), Sem("s_kg1"), Sem("s_kg2")]
        g_st = Sem("g_st")
        g_k = [Sem(f"g_k{i}") for i in range(4)]
        g_o = [Sem("g_o0"), Sem("g_o1")]
        g_a = [Sem("g_a0"), Sem("g_a1")]
        pe_a = Sem("pe_a")
        pe_w = Sem("pe_w")
        pe_t = Sem("pe_t")
        pe_o = Sem("pe_o")
        dv_a = Sem("dv_a")
        dv_e = Sem("dv_e")
        dv_s = Sem("dv_s")
        dv_h = Sem("dv_h")
        dv_x = Sem("dv_x")
        dv_o = Sem("dv_o")
        se_a = Sem("se_a")
        se_o = Sem("se_o")
        ALL = [s_ld, g_st, pe_a, pe_w, pe_t, pe_o, dv_a, dv_e, dv_s, dv_h, dv_o, dv_x,
               se_a, se_o] + s_w + s_p + s_k + s_kg + g_k + g_o + g_a

        def snap():
            return {s: s.n for s in ALL}

        def wait(eng, sem, v):
            eng.wait_ge(sem.h, v)

        def slot_load(sync, slot_sem, out_ap, in_ap):
            if slot_sem.n:
                wait(sync, slot_sem, slot_sem.n)
            slot_sem.inc(sync.dma_start(out=out_ap, in_=in_ap), 16)
            return slot_sem.n

        def mm3(tensor, out_ap, sl_hi, sl_lo, rhs_hi, rhs_lo, start, stop):
            tensor.matmul(out_ap, sl_hi, rhs_hi, start=start, stop=False)
            tensor.matmul(out_ap, sl_hi, rhs_lo, start=False, stop=False)
            return tensor.matmul(out_ap, sl_lo, rhs_hi, start=False, stop=stop)

        def pair_load(sync, slot_sem, outs, ins):
            if slot_sem.n:
                wait(sync, slot_sem, slot_sem.n)
            for o, i in zip(outs, ins):
                slot_sem.inc(sync.dma_start(out=o, in_=i), 16)
            return slot_sem.n

        with nc.named_scope("phA"), nc.Block(no_gpsimd_drain=True) as blk:
            @blk.sync
            def _(sync):
                s_ld.inc(sync.dma_start(out=maskt, in_=masks_in), 16)
                s_ld.inc(sync.dma_start(out=RS[:, 0:272], in_=consts), 16)
                s_ld.inc(sync.dma_start(out=c_b0a, in_=b0a2), 16)
                s_ld.inc(sync.dma_start(out=c_b1a, in_=b1a2), 16)
                s_ld.inc(sync.dma_start(out=c_b0b, in_=b0b2), 16)
                s_ld.inc(sync.dma_start(out=c_b1b, in_=b1b2), 16)
                for f in range(NF):
                    s_ld.inc(sync.dma_start(out=wg_sb[:, 2 * f:2 * f + 2],
                                            in_=wgT[f * 128:(f + 1) * 128, :]), 16)
                sync.wait_ge(s_ld.h, s_ld.n)


        def emit_ln(blk, src_dram, col0, width, base):
            @blk.sync
            def _(sync):
                for f in range(NF):
                    s_ld.inc(sync.dma_start(
                        out=xbuf(f)[:, 0:width],
                        in_=src_dram[f * 128:(f + 1) * 128, col0:col0 + width]), 16)
                sync.wait_ge(s_ld.h, s_ld.n)

            xtot = base[s_ld] + 16 * NF

            @blk.scalar
            def _(scalar):
                wait(scalar, s_ld, xtot)
                for f in range(NF):
                    if f > 0:
                        wait(scalar, pe_a, base[pe_a] + f)
                    se_a.inc(scalar.activation(out=sqbuf[:, 0:width],
                                               in_=xbuf(f)[:, 0:width], func=AF.Square))
                wait(scalar, dv_a, base[dv_a] + 2)
                se_a.inc(scalar.activation(out=r_sd[:, 0:width], in_=r_var[:, 0:width],
                                           func=AF.Sqrt, bias=eps_col[0:1, 0:1]))

            @blk.tensor
            def _(tensor):
                wait(tensor, s_ld, xtot)
                for f in range(NF):
                    wait(tensor, se_a, base[se_a] + f + 1)
                    tensor.matmul(ps[0:1, 0:width], ones[:, 0:1], xbuf(f)[:, 0:width],
                                  start=(f == 0), stop=(f == NF - 1))
                    pe_a.inc(tensor.matmul(ps[0:1, 512:512 + width], ones[:, 0:1],
                                           sqbuf[:, 0:width],
                                           start=(f == 0), stop=(f == NF - 1)))
                wait(tensor, dv_a, base[dv_a] + 4)
                tensor.matmul(bank(2, width), ones[0:1, :], r_rstd[:, 0:width],
                              start=True, stop=True)
                pe_a.inc(tensor.matmul(bank(3, width), ones[0:1, :], r_mmr[:, 0:width],
                                       start=True, stop=True))

            @blk.vector
            def _(vector):
                wait(vector, pe_a, base[pe_a] + NF)
                dv_a.inc(vector.tensor_scalar_mul(r_mean[:, 0:width], ps[0:1, 0:width], 1.0 / H))
                vector.tensor_scalar_mul(r_var[:, 0:width], ps[0:1, 512:512 + width], 1.0 / H)
                vector.drain()
                vector.tensor_tensor(r_msq[:, 0:width], r_mean[:, 0:width],
                                     r_mean[:, 0:width], ALU.mult)
                vector.drain()
                dv_a.inc(vector.tensor_tensor(r_var[:, 0:width], r_var[:, 0:width],
                                              r_msq[:, 0:width], ALU.subtract))
                wait(vector, se_a, base[se_a] + NF + 1)
                dv_a.inc(vector.reciprocal(r_rstd[:, 0:width], r_sd[:, 0:width]))
                vector.drain()
                dv_a.inc(vector.tensor_tensor(r_mmr[:, 0:width], r_mean[:, 0:width],
                                              r_rstd[:, 0:width], ALU.mult))
                wait(vector, pe_a, base[pe_a] + NF + 1)
                for f in range(NF):
                    vector.tensor_tensor(xbuf(f)[:, 0:width], xbuf(f)[:, 0:width],
                                         bank(2, width), ALU.mult)
                vector.drain()
                for f in range(NF):
                    dv_a.inc(vector.tensor_tensor(xbuf(f)[:, 0:width], xbuf(f)[:, 0:width],
                                                  bank(3, width), ALU.subtract))

        kv_slot_cnt = [0, 0, 0, 0]
        a2_done = 0
        a2_per_blk = (len(a2_chunks) + NSC - 1) // NSC
        for sc in range(NSC):
            with nc.named_scope(f"phB{sc}"), nc.Block(no_gpsimd_drain=True) as blk:
                base = snap()
                emit_ln(blk, xTs, sc * SC, SC, base)
                wmarks = []
                amarks = []
                a2_base = a2_done
                my_a2 = a2_chunks[a2_done:a2_done + a2_per_blk]
                a2_done += len(my_a2)

                def a2_wave(c):
                    return (3 * (c + 1) + NF - 1) // NF

                a2_pos = {}
                for ci in range(len(my_a2)):
                    if ci < 2:
                        a2_pos[ci] = 3 * ci
                    else:
                        a2_pos[ci] = max(3 * ci, NF * a2_wave(ci - 2) + 1)

                @blk.sync
                def _(sync):
                    widx = 0
                    nxt_a2 = [0]

                    def emit_a2_load(sync):
                        ci = nxt_a2[0]
                        if ci >= len(my_a2):
                            return
                        asrc, dh, dl, cw = my_a2[ci]
                        slot = (a2_base + ci) % 2
                        if ci >= 2:
                            wait(sync, dv_s, base[dv_s] + ci - 1)
                        amarks.append(slot_load(sync, s_p[slot], a2st[slot][:, 0:cw], asrc))
                        nxt_a2[0] += 1

                    for grp_kind, groups, wsrc in (("k", kgroups, wkT), ("v", vdp_pairs, wvT)):
                        for grp in groups:
                            if grp_kind == "k":
                                c0, cw = grp[0] * 128, len(grp) * 128
                            else:
                                c0, cw = grp[0] * VDW, len(grp) * VDW
                            for f in range(NF):
                                if widx >= 2:
                                    wait(sync, dv_h, base[dv_h] + widx - 1)
                                wmarks.append(slot_load(
                                    sync, s_w[widx % 2], wb2[widx % 2][:, 0:cw],
                                    wsrc[f * 128:(f + 1) * 128, c0:c0 + cw]))
                                widx += 1
                                while (nxt_a2[0] < len(my_a2)
                                       and a2_pos[nxt_a2[0]] <= widx):
                                    emit_a2_load(sync)
                    while nxt_a2[0] < len(my_a2):
                        emit_a2_load(sync)

                @blk.tensor
                def _(tensor):
                    widx = 0
                    nev = 0
                    for grp in kgroups:
                        if nev:
                            wait(tensor, dv_e, base[dv_e] + nev)
                        for f in range(NF):
                            wait(tensor, dv_h, base[dv_h] + widx + 1)
                            wait(tensor, dv_x, base[dv_x] + widx + 1)
                            for i, dc in enumerate(grp):
                                inst = mm3(tensor, bank(i),
                                           wsl[widx % 2][0][:, i * 128:(i + 1) * 128],
                                           wsl[widx % 2][1][:, i * 128:(i + 1) * 128],
                                           xsl[widx % 2][0], xsl[widx % 2][1],
                                           f == 0, f == NF - 1)
                            pe_w.inc(inst)
                            widx += 1
                        nev += len(grp)
                    for dps in vdp_pairs:
                        wait(tensor, dv_e, base[dv_e] + nev)
                        for f in range(NF):
                            wait(tensor, dv_h, base[dv_h] + widx + 1)
                            wait(tensor, dv_x, base[dv_x] + widx + 1)
                            for st in range(nst):
                                xh = xsl[widx % 2][0][:, st * 128:(st + 1) * 128]
                                xl = xsl[widx % 2][1][:, st * 128:(st + 1) * 128]
                                for di in range(len(dps)):
                                    bi = st * len(dps) + di
                                    tensor.matmul(bank(bi, VDW), xh,
                                                  wsl[widx % 2][0][:, di * VDW:(di + 1) * VDW],
                                                  start=(f == 0), stop=False)
                                    tensor.matmul(bank(bi, VDW), xh,
                                                  wsl[widx % 2][1][:, di * VDW:(di + 1) * VDW],
                                                  start=False, stop=False)
                                for di in range(len(dps)):
                                    bi = st * len(dps) + di
                                    inst = tensor.matmul(
                                        bank(bi, VDW), xl,
                                        wsl[widx % 2][0][:, di * VDW:(di + 1) * VDW],
                                        start=False, stop=(f == NF - 1))
                            pe_w.inc(inst)
                            widx += 1
                        nev += nst * len(dps)

                slot_seq = []

                @blk.scalar
                def _(scalar):
                    wait(scalar, dv_a, base[dv_a] + 4 + NF)
                    widx = 0
                    for grp_kind, groups in (("k", kgroups), ("v", vdp_pairs)):
                        for grp in groups:
                            cw = len(grp) * (128 if grp_kind == "k" else VDW)
                            for f in range(NF):
                                wait(scalar, s_w[widx % 2], wmarks[widx])
                                if widx >= 2:
                                    wait(scalar, pe_w, base[pe_w] + widx - 1)
                                se_a.inc(scalar.activation(
                                    out=wsl[widx % 2][0][:, 0:cw],
                                    in_=wb2[widx % 2][:, 0:cw], func=AF.Copy))
                                se_a.inc(scalar.activation(
                                    out=xsl[widx % 2][0], in_=xbuf(f), func=AF.Copy))
                                widx += 1

                @blk.vector
                def _(vector):
                    ev = 0
                    cnt = list(kv_slot_cnt)
                    nxt_sp = [0]
                    nsplit = [0]
                    se0 = base[se_a] + NF + 1

                    def emit_wx_split(vector, grp_w, f):
                        widx = nsplit[0]
                        cw = grp_w
                        wait(vector, se_a, se0 + 2 * (widx + 1))
                        dv_h.inc(vector.tensor_tensor(wsl[widx % 2][1][:, 0:cw],
                                                      wb2[widx % 2][:, 0:cw],
                                                      wsl[widx % 2][0][:, 0:cw],
                                                      ALU.subtract))
                        dv_x.inc(vector.tensor_tensor(xsl[widx % 2][1], xbuf(f),
                                                      xsl[widx % 2][0], ALU.subtract))
                        nsplit[0] += 1

                    def emit_a2_splits(vector, upto):
                        while nxt_sp[0] < min(upto, len(my_a2)):
                            ci = nxt_sp[0]
                            asrc, dh, dl, cw = my_a2[ci]
                            k = a2_base + ci
                            slot = k % 2
                            wait(vector, s_p[slot], amarks[ci])
                            if k // 2:
                                wait(vector, g_a[slot], 32 * (k // 2))
                            vector.tensor_copy(a2o[slot][0][:, 0:cw], a2st[slot][:, 0:cw])
                            vector.drain()
                            dv_s.inc(vector.tensor_tensor(a2o[slot][1][:, 0:cw],
                                                          a2st[slot][:, 0:cw],
                                                          a2o[slot][0][:, 0:cw],
                                                          ALU.subtract))
                            nxt_sp[0] += 1

                    wave = 0
                    for g, grp in enumerate(kgroups):
                        for f in range(NF):
                            emit_wx_split(vector, len(grp) * 128, f)
                        wait(vector, pe_w, base[pe_w] + NF * (g + 1))
                        for i, dc in enumerate(grp):
                            slot = ev % 4
                            if cnt[slot]:
                                wait(vector, g_k[slot], 16 * cnt[slot])
                            slot_seq.append(slot)
                            cnt[slot] += 1
                            dv_e.inc(vector.tensor_copy(kevac[slot], bank(i)))
                            ev += 1
                        wave += 1
                        emit_a2_splits(vector, (NF * wave) // 3)
                    for g, dps in enumerate(vdp_pairs):
                        for f in range(NF):
                            emit_wx_split(vector, len(dps) * VDW, f)
                        wait(vector, pe_w, base[pe_w] + NF * (len(kgroups) + g + 1))
                        bi = 0
                        for st in range(nst):
                            for di in range(len(dps)):
                                slot = ev % 4
                                if cnt[slot]:
                                    wait(vector, g_k[slot], 16 * cnt[slot])
                                slot_seq.append(slot)
                                cnt[slot] += 1
                                dv_e.inc(vector.tensor_copy(kevac[slot][:, 0:VDW],
                                                            bank(bi, VDW)))
                                bi += 1
                                ev += 1
                        wave += 1
                        emit_a2_splits(vector, (NF * wave) // 3)
                    emit_a2_splits(vector, len(my_a2))

                @blk.gpsimd
                def _(g):
                    ev = 0
                    nxt_st = [0]

                    def emit_a2_stores(g, upto):
                        while nxt_st[0] < min(upto, len(my_a2)):
                            ci = nxt_st[0]
                            asrc, dh, dl, cw = my_a2[ci]
                            k = a2_base + ci
                            slot = k % 2
                            wait(g, dv_s, base[dv_s] + ci + 1)
                            if g_a[slot].n:
                                wait(g, g_a[slot], g_a[slot].n)
                            g_a[slot].inc(g.dma_start(out=dh, in_=a2o[slot][0][:, 0:cw]), 16)
                            g_a[slot].inc(g.dma_start(out=dl, in_=a2o[slot][1][:, 0:cw]), 16)
                            nxt_st[0] += 1

                    wave = 0
                    for grp in kgroups:
                        for dc in grp:
                            slot = slot_seq[ev]
                            wait(g, dv_e, base[dv_e] + ev + 1)
                            if g_k[slot].n:
                                wait(g, g_k[slot], g_k[slot].n)
                            g_k[slot].inc(g.dma_start(
                                out=KTs[dc * 128:(dc + 1) * 128, sc * SC:(sc + 1) * SC],
                                in_=kevac[slot]), 16)
                            kv_slot_cnt[slot] += 1
                            ev += 1
                        wave += 1
                        emit_a2_stores(g, (NF * wave) // 3)
                    for dps in vdp_pairs:
                        for st in range(nst):
                            for dp in dps:
                                slot = slot_seq[ev]
                                wait(g, dv_e, base[dv_e] + ev + 1)
                                if g_k[slot].n:
                                    wait(g, g_k[slot], g_k[slot].n)
                                g_k[slot].inc(g.dma_start(
                                    out=Vs[(sc * nst + st) * 128:(sc * nst + st + 1) * 128,
                                           dp * VDW:(dp + 1) * VDW],
                                    in_=kevac[slot][:, 0:VDW]), 16)
                                kv_slot_cnt[slot] += 1
                                ev += 1
                        wave += 1
                        emit_a2_stores(g, (NF * wave) // 3)
                    emit_a2_stores(g, len(my_a2))
                    if sc == NSC - 1:
                        for s2 in range(4):
                            if g_k[s2].n:
                                wait(g, g_k[s2], g_k[s2].n)
                        for s2 in range(2):
                            if g_a[s2].n:
                                wait(g, g_a[s2], g_a[s2].n)

        for qc in range(NQC):
            with nc.named_scope(f"phB2_{qc}"), nc.Block(no_gpsimd_drain=True) as blk:
                base = snap()
                emit_ln(blk, xTq, qc * QCH, QCH, base)

                @blk.scalar
                def _(scalar):
                    wait(scalar, dv_a, base[dv_a] + 4 + NF)
                    for f in range(NF):
                        se_a.inc(scalar.activation(
                            out=xqh[:, f * SQ + qc * QCH:f * SQ + (qc + 1) * QCH],
                            in_=xbuf(f)[:, 0:QCH], func=AF.Copy))

                @blk.vector
                def _(vector):
                    for f in range(NF):
                        wait(vector, se_a, base[se_a] + NF + 1 + f + 1)
                        dv_a.inc(vector.tensor_tensor(
                            xql[:, f * SQ + qc * QCH:f * SQ + (qc + 1) * QCH],
                            xbuf(f)[:, 0:QCH],
                            xqh[:, f * SQ + qc * QCH:f * SQ + (qc + 1) * QCH],
                            ALU.subtract))

        def xqh_t(f, q0):
            return xqh[:, f * SQ + q0:f * SQ + q0 + QCH]

        def xql_t(f, q0):
            return xql[:, f * SQ + q0:f * SQ + q0 + QCH]

        for qc in range(NQC):
            q0 = qc * QCH
            with nc.named_scope(f"phC{qc}"), nc.Block(no_gpsimd_drain=True) as blk:
                base = snap()

                sched = []
                for e in range(2):
                    for gi, grp in enumerate(fgroups):
                        sched.append(('h', e, gi, grp))
                        sched.append(('o', e, gi, grp))

                tiles_per_step = []
                tile_srcs = []
                for kind, e, gi, grp in sched:
                    ha = (sWah, sWal) if e == 0 else (sW1ah, sW1al)
                    hb = (sWbh, sWbl) if e == 0 else (sW1bh, sW1bl)
                    if kind == 'h':
                        c0, cw = grp[0] * 128, len(grp) * 128
                        srcs = [(ha[0][f * 128:(f + 1) * 128, c0:c0 + cw],
                                 ha[1][f * 128:(f + 1) * 128, c0:c0 + cw], cw)
                                for f in range(NF)]
                    else:
                        srcs = []
                        for og in ogroups:
                            oc0, ocw = og[0] * 128, len(og) * 128
                            for fc in grp:
                                srcs.append((hb[0][fc * 128:(fc + 1) * 128, oc0:oc0 + ocw],
                                             hb[1][fc * 128:(fc + 1) * 128, oc0:oc0 + ocw],
                                             ocw))
                    tiles_per_step.append(len(srcs))
                    tile_srcs.append(srcs)
                tile_start = [0]
                for n in tiles_per_step:
                    tile_start.append(tile_start[-1] + n)

                comp = {}
                seo_c = 0
                dvo_c = 0
                for si, (kind, e, gi, grp) in enumerate(sched):
                    if kind != 'o':
                        continue
                    for bi2, og in enumerate(ogroups):
                        if e == 0 and gi == 0:
                            seo_c += len(og)
                            comp[(si, bi2)] = (se_o, seo_c)
                        else:
                            dvo_c += 1
                            comp[(si, bi2)] = (dv_o, dvo_c)
                    if e == 0 and gi == len(fgroups) - 1:
                        dvo_c += 1
                total_dvo = dvo_c

                cmarks = []

                @blk.sync
                def _(sync):
                    if qc == 0:
                        for s2 in range(2):
                            wait(sync, g_a[s2], g_a[s2].n)
                    widx = 0
                    for si in range(len(sched)):
                        for (hsrc, lsrc, cw) in tile_srcs[si]:
                            if widx >= 2:
                                wait(sync, pe_w, base[pe_w] + widx - 1)
                            cmarks.append(pair_load(
                                sync, s_w[widx % 2],
                                (wsl[widx % 2][0][:, 0:cw], wsl[widx % 2][1][:, 0:cw]),
                                (hsrc, lsrc)))
                            widx += 1

                @blk.tensor
                def _(tensor):
                    wait(tensor, dv_a, base[dv_a] + 1)
                    for f in range(NF):
                        inst = mm3(tensor, ps[0:1, 0:QCH], wgd_hi[:, f:f + 1],
                                   wgd_lo[:, f:f + 1], xqh_t(f, q0), xql_t(f, q0),
                                   f == 0, f == NF - 1)
                    pe_a.inc(inst)
                    wait(tensor, dv_a, base[dv_a] + 2)
                    pe_a.inc(tensor.matmul(bank(1, QCH), ones[0:1, :], r_gmask[:, 0:QCH],
                                           start=True, stop=True))
                    wait(tensor, dv_a, base[dv_a] + 4)
                    widx = 0
                    n_seh = 0
                    n_dvh = 0
                    last_comp = None
                    for si, (kind, e, gi, grp) in enumerate(sched):
                        if kind == 'h':
                            if last_comp is not None:
                                wait(tensor, last_comp[0], base[last_comp[0]] + last_comp[1])
                            n_dvh_c = n_dvh
                            for f in range(NF):
                                wait(tensor, s_w[widx % 2], cmarks[widx])
                                cw = len(grp) * 128
                                for i in range(len(grp)):
                                    inst = mm3(tensor, bank(i, QCH),
                                               wsl[widx % 2][0][:, i * 128:(i + 1) * 128],
                                               wsl[widx % 2][1][:, i * 128:(i + 1) * 128],
                                               xqh_t(f, q0), xql_t(f, q0),
                                               f == 0, f == NF - 1)
                                pe_w.inc(inst)
                                widx += 1
                            n_seh += 2 * len(grp)
                        else:
                            for bi2, og in enumerate(ogroups):
                                if bi2 > 0:
                                    c = comp[(si, bi2 - 1)]
                                    wait(tensor, c[0], base[c[0]] + c[1])
                                elif n_seh:
                                    wait(tensor, se_a, base[se_a] + n_seh)
                                for ti, fc in enumerate(grp):
                                    wait(tensor, dv_h, base[dv_h] + n_dvh + ti + 1)
                                    wait(tensor, s_w[widx % 2], cmarks[widx])
                                    hh = xn_hi[:, ti * QCH:(ti + 1) * QCH]
                                    hl = xn_lo[:, ti * QCH:(ti + 1) * QCH]
                                    for i in range(len(og)):
                                        inst = mm3(tensor, bank(i, QCH),
                                                   wsl[widx % 2][0][:, i * 128:(i + 1) * 128],
                                                   wsl[widx % 2][1][:, i * 128:(i + 1) * 128],
                                                   hh, hl, ti == 0, ti == len(grp) - 1)
                                    pe_w.inc(inst)
                                    widx += 1
                            n_dvh += len(grp)
                            last_comp = comp[(si, len(ogroups) - 1)]

                @blk.scalar
                def _(scalar):
                    wait(scalar, g_st, base[g_st])
                    widx_end_h = {}
                    widx = 0
                    for si2 in range(len(sched)):
                        widx += tiles_per_step[si2]
                        widx_end_h[si2] = widx
                    for si, (kind, e, gi, grp) in enumerate(sched):
                        if kind == 'h':
                            ba = c_b0a if e == 0 else c_b1a
                            fn = AF.Gelu if e == 0 else AF.Relu
                            wait(scalar, pe_w, base[pe_w] + widx_end_h[si])
                            for i, fc in enumerate(grp):
                                se_a.inc(scalar.activation(out=h32(i), in_=bank(i, QCH),
                                                           func=fn, bias=ba[:, fc:fc + 1]))
                                se_a.inc(scalar.activation(
                                    out=xn_hi[:, i * QCH:(i + 1) * QCH], in_=bank(i, QCH),
                                    func=fn, bias=ba[:, fc:fc + 1]))
                        else:
                            bb = c_b0b if e == 0 else c_b1b
                            if gi == 0:
                                for bi2, og in enumerate(ogroups):
                                    wait(scalar, pe_w,
                                         base[pe_w] + tile_start[si] + (bi2 + 1) * len(grp))
                                    for i, oc in enumerate(og):
                                        if e == 0:
                                            se_o.inc(scalar.activation(
                                                out=out0(oc), in_=bank(i, QCH),
                                                func=AF.Identity, bias=bb[:, oc:oc + 1]))
                                        else:
                                            se_o.inc(scalar.activation(
                                                out=RA[:, (8 + i) * QCH:(9 + i) * QCH],
                                                in_=bank(i, QCH),
                                                func=AF.Identity, bias=bb[:, oc:oc + 1]))

                @blk.vector
                def _(vector):
                    if qc == 0:
                        for s2 in range(2):
                            wait(vector, g_a[s2], g_a[s2].n)
                    vector.tensor_tensor(wgd, wg_sb[:, 0::2], wg_sb[:, 1::2], ALU.subtract)
                    vector.drain()
                    vector.tensor_copy(wgd_hi, wgd)
                    vector.drain()
                    dv_a.inc(vector.tensor_tensor(wgd_lo, wgd, wgd_hi, ALU.subtract))
                    wait(vector, pe_a, base[pe_a] + 1)
                    dv_a.inc(vector.tensor_scalar(r_gmask[:, 0:QCH], ps[0:1, 0:QCH],
                                                  0.0, None, ALU.is_ge))
                    wait(vector, pe_a, base[pe_a] + 2)
                    dv_a.inc(vector.tensor_copy(mask_b[:, 0:QCH], bank(1, QCH)))
                    vector.drain()
                    dv_a.inc(vector.tensor_scalar(invm[:, 0:QCH], mask_b[:, 0:QCH],
                                                  -1.0, 1.0, ALU.mult, ALU.add))
                    n_seh = 0
                    n_seo = 0
                    for si, (kind, e, gi, grp) in enumerate(sched):
                        if kind == 'h':
                            for i in range(len(grp)):
                                n_seh += 2
                                wait(vector, se_a, base[se_a] + n_seh)
                                dv_h.inc(vector.tensor_tensor(
                                    xn_lo[:, i * QCH:(i + 1) * QCH], h32(i),
                                    xn_hi[:, i * QCH:(i + 1) * QCH], ALU.subtract))
                        else:
                            for bi2, og in enumerate(ogroups):
                                if e == 0 and gi == 0:
                                    continue
                                pw = base[pe_w] + tile_start[si] + (bi2 + 1) * len(grp)
                                wait(vector, pe_w, pw)
                                if e == 0:
                                    vector.drain()
                                    for i, oc in enumerate(og):
                                        if i == len(og) - 1:
                                            dv_o.inc(vector.tensor_tensor(
                                                out0(oc), out0(oc), bank(i, QCH), ALU.add))
                                        else:
                                            vector.tensor_tensor(out0(oc), out0(oc),
                                                                 bank(i, QCH), ALU.add)
                                else:
                                    vector.drain()
                                    if gi == 0:
                                        wait(vector, se_o,
                                             base[se_o] + NF + (bi2 + 1) * len(og))
                                        for i, oc in enumerate(og):
                                            vector.tensor_tensor(
                                                RA[:, (8 + i) * QCH:(9 + i) * QCH],
                                                RA[:, (8 + i) * QCH:(9 + i) * QCH],
                                                invm[:, 0:QCH], ALU.mult)
                                    else:
                                        for i, oc in enumerate(og):
                                            vector.tensor_tensor(
                                                RA[:, (8 + i) * QCH:(9 + i) * QCH],
                                                bank(i, QCH), invm[:, 0:QCH], ALU.mult)
                                    vector.drain()
                                    for i, oc in enumerate(og):
                                        if i == len(og) - 1:
                                            dv_o.inc(vector.tensor_tensor(
                                                out0(oc), out0(oc),
                                                RA[:, (8 + i) * QCH:(9 + i) * QCH], ALU.add))
                                        else:
                                            vector.tensor_tensor(
                                                out0(oc), out0(oc),
                                                RA[:, (8 + i) * QCH:(9 + i) * QCH], ALU.add)
                            if e == 0 and gi == len(fgroups) - 1:
                                wait(vector, se_o, base[se_o] + NF)
                                vector.drain()
                                for oc in range(NF):
                                    if oc == NF - 1:
                                        dv_o.inc(vector.tensor_tensor(
                                            out0(oc), out0(oc), mask_b[:, 0:QCH], ALU.mult))
                                    else:
                                        vector.tensor_tensor(out0(oc), out0(oc),
                                                             mask_b[:, 0:QCH], ALU.mult)

                @blk.gpsimd
                def _(g):
                    wait(g, dv_o, base[dv_o] + total_dvo)
                    for oc in range(NF):
                        g_st.inc(g.dma_start(
                            out=Fs[oc * 128:(oc + 1) * 128, q0:q0 + QCH],
                            in_=out0(oc)), 16)
                    g.wait_ge(g_st.h, g_st.n)

        nqp = (SQ + 511) // 512
        for h in range(NH):
            with nc.named_scope(f"phD{h}"), nc.Block(no_gpsimd_drain=True) as blk:
                base = snap()

                def pt_base(i0):
                    return sum(j + 1 for j in range(i0))

                def dva_base(i0):
                    return sum((j + 1) + 2 for j in range(i0))

                def ev_base(i0):
                    return nqp + sum((j + 1) + 1 for j in range(i0))

                def sci(i0, kc):
                    return sum(j + 1 for j in range(i0)) + kc
                nsci = sum(j + 1 for j in range(NQT))

                chunk_list = [(i0, kc) for i0 in range(NQT) for kc in range(i0 + 1)]
                pk = [s_k, s_kg]
                k_start = [[pk[p][i].n for i in range(3)] for p in range(2)]
                kcnt = [[0, 0, 0], [0, 0, 0]]
                km_sem, kmarks = [], []
                for c in range(len(chunk_list)):
                    p, s = c % 2, c % 3
                    kcnt[p][s] += 1
                    km_sem.append(pk[p][s])
                    kmarks.append(k_start[p][s] + 16 * kcnt[p][s])

                def k_prev(c):
                    if c >= 3:
                        return km_sem[c - 3], kmarks[c - 3]
                    s = c % 3
                    ls = max(cc for cc in range(len(chunk_list)) if cc % 3 == s)
                    return pk[ls % 2][s], k_start[ls % 2][s]

                def emit_k_loads(eng, parity):
                    for c, (i0, kc) in enumerate(chunk_list):
                        if c % 2 != parity:
                            continue
                        if c >= 3:
                            wait(eng, pe_a, base[pe_a] + c)
                        psem, pv = k_prev(c)
                        if pv > 0:
                            wait(eng, psem, pv)
                        eng.dma_start(
                            out=kcs[c % 3],
                            in_=KTs[h * 128:(h + 1) * 128, kc * 512:(kc + 1) * 512]
                        ).then_inc(km_sem[c].h, 16)

                @blk.sync
                def _(sync):
                    if h == 0:
                        wait(sync, g_st, base[g_st])
                    for kb in range(NKB):
                        s_ld.inc(sync.dma_start(
                            out=Vb[:, kb * 128:(kb + 1) * 128],
                            in_=Vs[kb * 128:(kb + 1) * 128, h * 128:(h + 1) * 128]), 16)
                    for hf in range(2):
                        if hf == 1:
                            wait(sync, pe_a, base[pe_a] + 1)
                        for fi in range(FHALF):
                            f = hf * FHALF + fi
                            s_ld.inc(sync.dma_start(
                                out=wqh_b[:, fi * 128:(fi + 1) * 128],
                                in_=swqh[f * 128:(f + 1) * 128, h * 128:(h + 1) * 128]), 16)
                            s_ld.inc(sync.dma_start(
                                out=wql_b[:, fi * 128:(fi + 1) * 128],
                                in_=swql[f * 128:(f + 1) * 128, h * 128:(h + 1) * 128]), 16)
                        sync.wait_ge(s_ld.h, s_ld.n)
                    emit_k_loads(sync, 0)

                @blk.gpsimd
                def _(g):
                    emit_k_loads(g, 1)

                for _p in range(2):
                    for _i in range(3):
                        pk[_p][_i].n = k_start[_p][_i] + 16 * kcnt[_p][_i]

                sbanks = [0, 1, 5, 6, 7]

                def chunk2tile(c):
                    i0 = 0
                    while c >= i0 + 1:
                        c -= i0 + 1
                        i0 += 1
                    return i0, c

                def emit_scores(tensor, i0, k0, k1):
                    for kc in range(k0, k1):
                        c = sci(i0, kc)
                        bk = sbanks[c % 5]
                        if c == 0:
                            wait(tensor, dv_e, base[dv_e] + nqp)
                        if c >= 5:
                            i0p, kcp = chunk2tile(c - 5)
                            wait(tensor, dv_a, base[dv_a] + dva_base(i0p) + kcp + 1)
                        wait(tensor, km_sem[c], kmarks[c])
                        pe_a.inc(tensor.matmul(
                            bank(bk), Qh[:, i0 * 128:(i0 + 1) * 128],
                            kcs[c % 3], start=True, stop=True))

                def emit_av(tensor, i0, g, ng):
                    wait(tensor, dv_e, base[dv_e] + ev_base(i0) + g + 1)
                    for t in range(4):
                        kb = 4 * g + t
                        inst = tensor.matmul(
                            ps[:, 4 * 512:4 * 512 + 128], Vb[:, kb * 128:(kb + 1) * 128],
                            ptb[g % 2][:, t * 128:(t + 1) * 128],
                            start=(kb == 0), stop=(kb == 4 * ng - 1))
                    pe_o.inc(inst)

                @blk.tensor
                def _(tensor):
                    for hf in range(2):
                        wait(tensor, s_ld,
                             base[s_ld] + 16 * (NKB + 2 * FHALF * (hf + 1)))
                        for qp in range(nqp):
                            qw = min(512, SQ - qp * 512)
                            for fi in range(FHALF):
                                f = hf * FHALF + fi
                                inst = mm3(tensor, bank(5 + qp, qw),
                                           wqh_b[:, fi * 128:(fi + 1) * 128],
                                           wql_b[:, fi * 128:(fi + 1) * 128],
                                           xqh[:, f * SQ + qp * 512:f * SQ + qp * 512 + qw],
                                           xql[:, f * SQ + qp * 512:f * SQ + qp * 512 + qw],
                                           hf == 0 and fi == 0,
                                           hf == 1 and fi == FHALF - 1)
                        pe_a.inc(inst)
                    emit_scores(tensor, 0, 0, 1)
                    for i0 in range(NQT):
                        ng = i0 + 1
                        if i0 + 1 < NQT:
                            emit_scores(tensor, i0 + 1, 0, min(5, i0 + 2))
                        wait(tensor, se_a, base[se_a] + 2 * (i0 + 1))
                        for g in range(ng):
                            if g >= 2:
                                wait(tensor, dv_e, base[dv_e] + ev_base(i0) + g - 1)
                            for t in range(4):
                                inst = tensor.transpose(
                                    ps[:, (2 + g % 2) * 512 + t * 128:
                                       (2 + g % 2) * 512 + (t + 1) * 128],
                                    SP_sb[:, (4 * g + t) * 128:(4 * g + t + 1) * 128],
                                    ident)
                            pe_t.inc(inst)
                            if g >= 1:
                                emit_av(tensor, i0, g - 1, ng)
                        emit_av(tensor, i0, ng - 1, ng)
                        if i0 + 1 < NQT and i0 + 2 > 5:
                            emit_scores(tensor, i0 + 1, 5, i0 + 2)

                @blk.vector
                def _(vector):
                    for qp in range(nqp):
                        qw = min(512, SQ - qp * 512)
                        wait(vector, pe_a, base[pe_a] + 2)
                        dv_e.inc(vector.tensor_scalar_mul(Qh[:, qp * 512:qp * 512 + qw],
                                                          bank(5 + qp, qw), scale))
                    for i0 in range(NQT):
                        ng = i0 + 1
                        nkc = i0 + 1
                        wait(vector, se_a, base[se_a] + 2 * i0)
                        wait(vector, pe_t, base[pe_t] + pt_base(i0))
                        for kc in range(nkc):
                            c = sci(i0, kc)
                            bk = sbanks[c % 5]
                            wait(vector, pe_a, base[pe_a] + 2 + c + 1)
                            if kc == nkc - 1:
                                vector.tensor_tensor(bank(bk), bank(bk), maskt, ALU.add)
                                vector.drain()
                            vector.reduce_max(mx[:, kc:kc + 1], bank(bk), axis=AX.X)
                            dv_a.inc(vector.tensor_copy(SP_sb[:, kc * 512:(kc + 1) * 512],
                                                        bank(bk)))
                        vector.drain()
                        dv_a.inc(vector.reduce_max(negm, mx[:, 0:nkc], axis=AX.X,
                                                   negate=True))
                        wait(vector, se_a, base[se_a] + 2 * i0 + 1)
                        vector.tensor_scalar_add(rden, den, 1e-8)
                        vector.drain()
                        dv_a.inc(vector.reciprocal(rden, rden))
                        for g in range(ng):
                            wait(vector, pe_t, base[pe_t] + pt_base(i0) + g + 1)
                            if g >= 2:
                                wait(vector, pe_o, base[pe_o] + pt_base(i0) + g - 1)
                            dv_e.inc(vector.tensor_copy(
                                ptb[g % 2],
                                ps[:, (2 + g % 2) * 512:(3 + g % 2) * 512]))
                        wait(vector, pe_o, base[pe_o] + pt_base(i0) + ng)
                        dv_e.inc(vector.tensor_copy(
                            attnbuf(h, i0 * 128, (i0 + 1) * 128),
                            ps[:, 4 * 512:4 * 512 + 128]))

                @blk.scalar
                def _(scalar):
                    for i0 in range(NQT):
                        E = 4 * i0 + 4
                        nkc = i0 + 1
                        wait(scalar, dv_a, base[dv_a] + dva_base(i0) + nkc + 1)
                        wait(scalar, pe_t, base[pe_t] + pt_base(i0))
                        se_a.inc(scalar.activation(out=SP_sb[:, 0:E * 128],
                                                   in_=SP_sb[:, 0:E * 128], func=AF.Exp,
                                                   scale=1.0, bias=negm, accum_out=den))
                        wait(scalar, dv_a, base[dv_a] + dva_base(i0) + nkc + 2)
                        se_a.inc(scalar.activation(out=SP_sb[:, 0:E * 128],
                                                   in_=SP_sb[:, 0:E * 128],
                                                   func=AF.Identity,
                                                   bias=zero_col[:, 0:1], scale=rden))

        with nc.named_scope("phE"), nc.Block(no_gpsimd_drain=True) as blk:
            base = snap()
            EQC = SQ // QCH
            wmarks = []
            pmarks = []
            smarks = []

            @blk.sync
            def _(sync):
                wait(sync, g_st, base[g_st])
                widx = 0
                lidx = 0
                for grp in ogroups:
                    c0, cw = grp[0] * 128, len(grp) * 128
                    for qcc in range(EQC):
                        q0 = qcc * QCH
                        for f in range(NF):
                            if widx >= 2:
                                wait(sync, pe_w, base[pe_w] + widx - 1)
                            wmarks.append(pair_load(
                                sync, s_w[widx % 2],
                                (wsl[widx % 2][0][:, 0:cw], wsl[widx % 2][1][:, 0:cw]),
                                (swoh[f * 128:(f + 1) * 128, c0:c0 + cw],
                                 swol[f * 128:(f + 1) * 128, c0:c0 + cw])))
                            widx += 1
                        for oc in grp:
                            slot = lidx % 2
                            if lidx >= 2:
                                wait(sync, dv_a, base[dv_a] + lidx - 1)
                            if s_p[slot].n:
                                wait(sync, s_p[slot], s_p[slot].n)
                            s_p[slot].inc(sync.dma_start(
                                out=ld2[2 * slot][:, 0:QCH],
                                in_=Fs[oc * 128:(oc + 1) * 128, q0:q0 + QCH]), 16)
                            s_p[slot].inc(sync.dma_start(
                                out=ld2[2 * slot + 1][:, 0:QCH],
                                in_=xTq[oc * 128:(oc + 1) * 128, q0:q0 + QCH]), 16)
                            pmarks.append(s_p[slot].n)
                            lidx += 1

            @blk.vector
            def _(vector):
                widx = 0
                lidx = 0
                for grp in ogroups:
                    for qcc in range(EQC):
                        q0 = qcc * QCH
                        for f in range(NF):
                            slot = widx % 2
                            if widx >= 2:
                                wait(vector, pe_w, base[pe_w] + widx - 1)
                            vector.tensor_copy(aspl[slot][0][:, 0:QCH],
                                               attnbuf(f, q0, q0 + QCH))
                            vector.drain()
                            dv_s.inc(vector.tensor_tensor(
                                aspl[slot][1][:, 0:QCH], attnbuf(f, q0, q0 + QCH),
                                aspl[slot][0][:, 0:QCH], ALU.subtract))
                            widx += 1
                        gq = widx // NF
                        wait(vector, pe_w, base[pe_w] + gq * NF)
                        for i, oc in enumerate(grp):
                            slot = lidx % 2
                            wait(vector, s_p[slot], pmarks[lidx])
                            if lidx >= 2:
                                wait(vector, g_o[slot], 16 * (lidx // 2))
                            dv_e.inc(vector.tensor_tensor(
                                ob2[slot], bank(i, QCH), ld2[2 * slot][:, 0:QCH], ALU.add))
                            vector.drain()
                            dv_a.inc(vector.tensor_tensor(
                                ob2[slot], ob2[slot], ld2[2 * slot + 1][:, 0:QCH], ALU.add))
                            lidx += 1

            @blk.tensor
            def _(tensor):
                widx = 0
                nev = 0
                for grp in ogroups:
                    for qcc in range(EQC):
                        if nev:
                            wait(tensor, dv_e, base[dv_e] + nev)
                        for f in range(NF):
                            slot = widx % 2
                            wait(tensor, s_w[slot], wmarks[widx])
                            wait(tensor, dv_s, base[dv_s] + widx + 1)
                            for i in range(len(grp)):
                                inst = mm3(tensor, bank(i, QCH),
                                           wsl[slot][0][:, i * 128:(i + 1) * 128],
                                           wsl[slot][1][:, i * 128:(i + 1) * 128],
                                           aspl[slot][0][:, 0:QCH], aspl[slot][1][:, 0:QCH],
                                           f == 0, f == NF - 1)
                            pe_w.inc(inst)
                            widx += 1
                        nev += len(grp)

            @blk.gpsimd
            def _(g):
                lidx = 0
                for grp in ogroups:
                    for qcc in range(EQC):
                        q0 = qcc * QCH
                        for oc in grp:
                            slot = lidx % 2
                            wait(g, dv_a, base[dv_a] + lidx + 1)
                            if g_o[slot].n:
                                wait(g, g_o[slot], g_o[slot].n)
                            g_o[slot].inc(g.dma_start(
                                out=outT[oc * 128:(oc + 1) * 128, q0:q0 + QCH],
                                in_=ob2[slot]), 16)
                            lidx += 1
                g.wait_ge(g_o[0].h, g_o[0].n)
                g.wait_ge(g_o[1].h, g_o[1].n)

    return nc


def _make_masks(j):
    r = np.arange(128)[:, None]
    out = np.zeros((128, 512), dtype=np.float32)
    for t in range(4):
        c = np.arange(128)[None, :]
        allowed = (128 * t + c) <= (4 * r + j)
        out[:, t * 128:(t + 1) * 128] = np.where(allowed, 0.0, NEG_BIG).astype(np.float32)
    return out


def _consts_tile():
    c = np.zeros((128, 272), dtype=np.float32)
    c[:, 0:128] = 1.0
    c[:, 128:256] = np.eye(128, dtype=np.float32)
    c[:, 256] = 1e-5
    return c


def kernel(L0_attn_x, L0_attn_wq, L0_attn_wk, L0_attn_wv, L0_attn_wo,
           L0_ffn_Wg, L0_ffn_W0a, L0_ffn_b0a, L0_ffn_W0b, L0_ffn_b0b,
           L0_ffn_W1a, L0_ffn_b1a, L0_ffn_W1b, L0_ffn_b1b):
    x = np.asarray(L0_attn_x)
    B, S, H = x.shape
    NH = 16
    FF = np.asarray(L0_ffn_W0a).shape[0]
    NF = H // 128
    NFC = FF // 128

    nc = build_nc(H=H, S=S, NH=NH, FF=FF)

    def t(a):
        return np.ascontiguousarray(np.asarray(a, dtype=np.float32).T)

    def b2(a, n):
        return np.ascontiguousarray(np.asarray(a, dtype=np.float32).reshape(n, 128).T)

    shared = {
        "wqT": t(L0_attn_wq), "wkT": t(L0_attn_wk), "wvT": t(L0_attn_wv),
        "woT": t(L0_attn_wo), "wgT": t(L0_ffn_Wg),
        "W0aT": t(L0_ffn_W0a), "W0bT": t(L0_ffn_W0b),
        "W1aT": t(L0_ffn_W1a), "W1bT": t(L0_ffn_W1b),
        "b0a2": b2(L0_ffn_b0a, NFC), "b1a2": b2(L0_ffn_b1a, NFC),
        "b0b2": b2(L0_ffn_b0b, NF), "b1b2": b2(L0_ffn_b1b, NF),
        "consts": _consts_tile(),
    }
    in_maps = []
    for core in range(8):
        b, j = core // 4, core % 4
        im = dict(shared)
        im["xTs"] = np.ascontiguousarray(x[b].T)
        im["xTq"] = np.ascontiguousarray(x[b, j::4, :].T)
        im["masks"] = _make_masks(j)
        in_maps.append(im)

    res = run_bass_kernel_spmd(nc, in_maps, core_ids=list(range(8)))

    out = np.empty((B, S, H), dtype=np.float32)
    for core in range(8):
        b, j = core // 4, core % 4
        out[b, j::4, :] = res.results[core]["outT"].T
    return out


# version 20
# speedup vs baseline: 1.0287x; 1.0160x over previous
import math
import contextlib
import numpy as np
import concourse.bass as bass
import concourse.mybir as mybir
from concourse.bass_utils import run_bass_kernel_spmd

F32 = mybir.dt.float32
BF16 = mybir.dt.bfloat16
AF = mybir.ActivationFunctionType
ALU = mybir.AluOpType
AX = mybir.AxisListType

NEG_BIG = -1.0e30


def build_nc(H=2048, S=4096, NH=16, FF=4096):
    HD = H // NH
    assert HD == 128 and H % 128 == 0 and FF % 128 == 0 and S % 512 == 0
    NF = H // 128
    SC = 512
    NSC = S // SC
    SQ = S // 4
    NQT = SQ // 128
    NFC = FF // 128
    QCH = min(512, SQ)
    NQC = SQ // QCH
    NKB = S // 128
    scale = 1.0 / math.sqrt(HD)

    nc = bass.Bass("TRN2", target_bir_lowering=False)

    xTs = nc.dram_tensor("xTs", [H, S], F32, kind="ExternalInput").ap()
    xTq = nc.dram_tensor("xTq", [H, SQ], F32, kind="ExternalInput").ap()
    wqT = nc.dram_tensor("wqT", [H, H], F32, kind="ExternalInput").ap()
    wkT = nc.dram_tensor("wkT", [H, H], F32, kind="ExternalInput").ap()
    wvT = nc.dram_tensor("wvT", [H, H], F32, kind="ExternalInput").ap()
    woT = nc.dram_tensor("woT", [H, H], F32, kind="ExternalInput").ap()
    wgT = nc.dram_tensor("wgT", [H, 2], F32, kind="ExternalInput").ap()
    W0aT = nc.dram_tensor("W0aT", [H, FF], F32, kind="ExternalInput").ap()
    W0bT = nc.dram_tensor("W0bT", [FF, H], F32, kind="ExternalInput").ap()
    W1aT = nc.dram_tensor("W1aT", [H, FF], F32, kind="ExternalInput").ap()
    W1bT = nc.dram_tensor("W1bT", [FF, H], F32, kind="ExternalInput").ap()
    b0a2 = nc.dram_tensor("b0a2", [128, NFC], F32, kind="ExternalInput").ap()
    b1a2 = nc.dram_tensor("b1a2", [128, NFC], F32, kind="ExternalInput").ap()
    b0b2 = nc.dram_tensor("b0b2", [128, NF], F32, kind="ExternalInput").ap()
    b1b2 = nc.dram_tensor("b1b2", [128, NF], F32, kind="ExternalInput").ap()
    masks_in = nc.dram_tensor("masks", [128, 512], F32, kind="ExternalInput").ap()
    consts = nc.dram_tensor("consts", [128, 272], F32, kind="ExternalInput").ap()
    outT = nc.dram_tensor("outT", [H, SQ], F32, kind="ExternalOutput").ap()
    KTs = nc.dram_tensor("KTs", [H, S], F32, kind="Internal").ap()
    Vs = nc.dram_tensor("Vs", [S, H], F32, kind="Internal").ap()
    Fs = nc.dram_tensor("Fs", [H, SQ], F32, kind="Internal").ap()
    sWah = nc.dram_tensor("sWah", [H, FF], BF16, kind="Internal").ap()
    sWal = nc.dram_tensor("sWal", [H, FF], BF16, kind="Internal").ap()
    sWbh = nc.dram_tensor("sWbh", [FF, H], BF16, kind="Internal").ap()
    sWbl = nc.dram_tensor("sWbl", [FF, H], BF16, kind="Internal").ap()
    sW1ah = nc.dram_tensor("sW1ah", [H, FF], BF16, kind="Internal").ap()
    sW1al = nc.dram_tensor("sW1al", [H, FF], BF16, kind="Internal").ap()
    sW1bh = nc.dram_tensor("sW1bh", [FF, H], BF16, kind="Internal").ap()
    sW1bl = nc.dram_tensor("sW1bl", [FF, H], BF16, kind="Internal").ap()
    swqh = nc.dram_tensor("swqh", [H, H], BF16, kind="Internal").ap()
    swql = nc.dram_tensor("swql", [H, H], BF16, kind="Internal").ap()
    swoh = nc.dram_tensor("swoh", [H, H], BF16, kind="Internal").ap()
    swol = nc.dram_tensor("swol", [H, H], BF16, kind="Internal").ap()

    kgroups = [list(range(g, min(g + 8, NF))) for g in range(0, NF, 8)]
    ndp = max(1, H // 512)
    VDW = min(512, H)
    vdp_pairs = [list(range(d, min(d + 2, ndp))) for d in range(0, ndp, 2)]
    nst = SC // 128
    fgroups = [list(range(g, min(g + 8, NFC))) for g in range(0, NFC, 8)]
    ogroups = [list(range(g, min(g + 8, NF))) for g in range(0, NF, 8)]
    GW = 8 * QCH

    a2_chunks = []
    for srcT, dh, dl in ((W0aT, sWah, sWal), (W0bT, sWbh, sWbl),
                         (W1aT, sW1ah, sW1al), (W1bT, sW1bh, sW1bl),
                         (wqT, swqh, swql), (woT, swoh, swol)):
        rows, cols = srcT.tensor.shape
        for rb in range(rows // 128):
            for c0 in range(0, cols, 2048):
                cw = min(2048, cols - c0)
                a2_chunks.append((srcT[rb * 128:(rb + 1) * 128, c0:c0 + cw],
                                  dh[rb * 128:(rb + 1) * 128, c0:c0 + cw],
                                  dl[rb * 128:(rb + 1) * 128, c0:c0 + cw], cw))

    ctx = contextlib.ExitStack()
    with ctx:
        xqh = ctx.enter_context(nc.sbuf_tensor("xqh", [128, NF * SQ], BF16))
        xql = ctx.enter_context(nc.sbuf_tensor("xql", [128, NF * SQ], BF16))
        RA = ctx.enter_context(nc.sbuf_tensor("RA", [128, 16384], F32))
        RB = ctx.enter_context(nc.sbuf_tensor("RB", [128, max(2 * S + 1536, 4096)], F32))
        XN = ctx.enter_context(nc.sbuf_tensor("XN", [128, 2 * max(GW, 4096)], BF16))
        WS = ctx.enter_context(nc.sbuf_tensor("WS", [128, 6144], BF16))
        RC = ctx.enter_context(nc.sbuf_tensor("RC", [128, 2304], F32))
        RS = ctx.enter_context(nc.sbuf_tensor("RS", [128, 1024], F32))
        ps = ctx.enter_context(nc.psum_tensor("ps", [128, 4096], F32))

        def bank(k, w=512):
            return ps[:, k * 512:k * 512 + w]

        ones = RS[:, 0:128]
        ident = RS[:, 128:256]
        eps_col = RS[:, 256:257]
        zero_col = RS[:, 257:258]
        maskt = RS[:, 272:784]
        c_b0a = RS[:, 784:784 + NFC]
        c_b1a = RS[:, 816:816 + NFC]
        c_b0b = RS[:, 848:848 + NF]
        c_b1b = RS[:, 864:864 + NF]
        wg_sb = RS[:, 880:880 + 2 * NF]
        mx = RS[:, 912:920]
        m_col = RS[:, 920:921]
        negm = RS[:, 921:922]
        den = RS[:, 922:923]
        rden = RS[:, 923:924]
        wgd = RS[:, 928:928 + NF]

        SP_sb = RB[:, 0:S]
        Vb = RB[:, S:2 * S]
        kcs = [RB[:, 2 * S + i * 512:2 * S + (i + 1) * 512] for i in range(3)]
        kevac = [RB[:, i * 512:(i + 1) * 512] for i in range(4)]
        sqbuf = RB[:, 2048:2560]
        rowA = RB[0:1, 2560:3072]
        rowB = RB[0:1, 3072:3584]
        rowC = RB[0:1, 3584:4096]
        r_mean = rowA
        r_msq, r_sd, r_mmr = rowB, rowB, rowB
        r_var, r_rstd = rowC, rowC

        def out0(oc):
            return RB[:, oc * QCH:(oc + 1) * QCH]
        ld2 = [RB[:, 2048 + i * 512:2048 + (i + 1) * 512] for i in range(4)]
        ob2 = [RB[:, i * 512:i * 512 + QCH] for i in range(2)]

        xn_hi = XN[:, 0:GW]
        xn_lo = XN[:, GW:2 * GW]
        a2o = [(XN[:, 0:2048], XN[:, 2048:4096]),
               (XN[:, 4096:6144], XN[:, 6144:8192])]

        wsl = [(WS[:, i * 2048:i * 2048 + 1024], WS[:, i * 2048 + 1024:(i + 1) * 2048])
               for i in range(2)]
        FHALF = max(1, NF // 2)
        wqh_b = WS[:, 4096:4096 + FHALF * 128]
        wql_b = WS[:, 4096 + FHALF * 128:4096 + 2 * FHALF * 128]
        aspl = [(WS[:, 4096 + i * 1024:4096 + i * 1024 + 512],
                 WS[:, 4096 + i * 1024 + 512:4096 + (i + 1) * 1024])
                for i in range(2)]
        xsl = [(WS[:, 4096 + i * 1024:4096 + i * 1024 + 512],
                WS[:, 4096 + i * 1024 + 512:4096 + (i + 1) * 1024])
               for i in range(2)]
        wgd_hi = WS[:, 4096:4096 + NF]
        wgd_lo = WS[:, 4096 + 64:4096 + 64 + NF]

        wb2 = [RC[:, 0:1024], RC[:, 1024:2048]]
        mask_b = RC[:, 0:512]
        r_gmask = RC[0:1, 0:512]
        invm = RC[:, 512:1024]
        Qh = RC[:, 0:SQ]
        ptb = [RC[:, 1024 + i * 512:1024 + (i + 1) * 512] for i in range(2)]

        def h32(i):
            return RA[:, i * QCH:(i + 1) * QCH]

        def xbuf(f):
            return RA[:, f * SC:(f + 1) * SC]

        a2st = [RA[:, 8192 + i * 2048:8192 + (i + 1) * 2048] for i in range(2)]

        def attnbuf(h, c0, c1):
            return RA[:, h * SQ + c0:h * SQ + c1]

        class Sem:
            def __init__(self, name):
                self.h = ctx.enter_context(nc.semaphore(name))
                self.n = 0

            def inc(self, inst, k=1):
                inst.then_inc(self.h, k)
                self.n += k
                return inst

        s_ld = Sem("s_ld")
        s_w = [Sem("s_w0"), Sem("s_w1")]
        s_p = [Sem("s_p0"), Sem("s_p1")]
        s_k = [Sem("s_k0"), Sem("s_k1"), Sem("s_k2")]
        g_st = Sem("g_st")
        g_k = [Sem(f"g_k{i}") for i in range(4)]
        g_o = [Sem("g_o0"), Sem("g_o1")]
        g_a = [Sem("g_a0"), Sem("g_a1")]
        pe_a = Sem("pe_a")
        pe_w = Sem("pe_w")
        pe_t = Sem("pe_t")
        pe_o = Sem("pe_o")
        dv_a = Sem("dv_a")
        dv_e = Sem("dv_e")
        dv_s = Sem("dv_s")
        dv_h = Sem("dv_h")
        dv_x = Sem("dv_x")
        dv_o = Sem("dv_o")
        se_a = Sem("se_a")
        se_o = Sem("se_o")
        ALL = [s_ld, g_st, pe_a, pe_w, pe_t, pe_o, dv_a, dv_e, dv_s, dv_h, dv_o, dv_x,
               se_a, se_o] + s_w + s_p + s_k + g_k + g_o + g_a

        def snap():
            return {s: s.n for s in ALL}

        def wait(eng, sem, v):
            eng.wait_ge(sem.h, v)

        def slot_load(sync, slot_sem, out_ap, in_ap):
            if slot_sem.n:
                wait(sync, slot_sem, slot_sem.n)
            slot_sem.inc(sync.dma_start(out=out_ap, in_=in_ap), 16)
            return slot_sem.n

        def mm3(tensor, out_ap, sl_hi, sl_lo, rhs_hi, rhs_lo, start, stop):
            tensor.matmul(out_ap, sl_hi, rhs_hi, start=start, stop=False)
            tensor.matmul(out_ap, sl_hi, rhs_lo, start=False, stop=False)
            return tensor.matmul(out_ap, sl_lo, rhs_hi, start=False, stop=stop)

        def pair_load(sync, slot_sem, outs, ins):
            if slot_sem.n:
                wait(sync, slot_sem, slot_sem.n)
            for o, i in zip(outs, ins):
                slot_sem.inc(sync.dma_start(out=o, in_=i), 16)
            return slot_sem.n

        with nc.named_scope("phA"), nc.Block(no_gpsimd_drain=True) as blk:
            @blk.sync
            def _(sync):
                s_ld.inc(sync.dma_start(out=maskt, in_=masks_in), 16)
                s_ld.inc(sync.dma_start(out=RS[:, 0:272], in_=consts), 16)
                s_ld.inc(sync.dma_start(out=c_b0a, in_=b0a2), 16)
                s_ld.inc(sync.dma_start(out=c_b1a, in_=b1a2), 16)
                s_ld.inc(sync.dma_start(out=c_b0b, in_=b0b2), 16)
                s_ld.inc(sync.dma_start(out=c_b1b, in_=b1b2), 16)
                for f in range(NF):
                    s_ld.inc(sync.dma_start(out=wg_sb[:, 2 * f:2 * f + 2],
                                            in_=wgT[f * 128:(f + 1) * 128, :]), 16)
                sync.wait_ge(s_ld.h, s_ld.n)


        def emit_ln(blk, src_dram, col0, width, base):
            @blk.sync
            def _(sync):
                for f in range(NF):
                    s_ld.inc(sync.dma_start(
                        out=xbuf(f)[:, 0:width],
                        in_=src_dram[f * 128:(f + 1) * 128, col0:col0 + width]), 16)
                sync.wait_ge(s_ld.h, s_ld.n)

            xtot = base[s_ld] + 16 * NF

            @blk.scalar
            def _(scalar):
                wait(scalar, s_ld, xtot)
                for f in range(NF):
                    if f > 0:
                        wait(scalar, pe_a, base[pe_a] + f)
                    se_a.inc(scalar.activation(out=sqbuf[:, 0:width],
                                               in_=xbuf(f)[:, 0:width], func=AF.Square))
                wait(scalar, dv_a, base[dv_a] + 2)
                se_a.inc(scalar.activation(out=r_sd[:, 0:width], in_=r_var[:, 0:width],
                                           func=AF.Sqrt, bias=eps_col[0:1, 0:1]))

            @blk.tensor
            def _(tensor):
                wait(tensor, s_ld, xtot)
                for f in range(NF):
                    wait(tensor, se_a, base[se_a] + f + 1)
                    tensor.matmul(ps[0:1, 0:width], ones[:, 0:1], xbuf(f)[:, 0:width],
                                  start=(f == 0), stop=(f == NF - 1))
                    pe_a.inc(tensor.matmul(ps[0:1, 512:512 + width], ones[:, 0:1],
                                           sqbuf[:, 0:width],
                                           start=(f == 0), stop=(f == NF - 1)))
                wait(tensor, dv_a, base[dv_a] + 4)
                tensor.matmul(bank(2, width), ones[0:1, :], r_rstd[:, 0:width],
                              start=True, stop=True)
                pe_a.inc(tensor.matmul(bank(3, width), ones[0:1, :], r_mmr[:, 0:width],
                                       start=True, stop=True))

            @blk.vector
            def _(vector):
                wait(vector, pe_a, base[pe_a] + NF)
                dv_a.inc(vector.tensor_scalar_mul(r_mean[:, 0:width], ps[0:1, 0:width], 1.0 / H))
                vector.tensor_scalar_mul(r_var[:, 0:width], ps[0:1, 512:512 + width], 1.0 / H)
                vector.drain()
                vector.tensor_tensor(r_msq[:, 0:width], r_mean[:, 0:width],
                                     r_mean[:, 0:width], ALU.mult)
                vector.drain()
                dv_a.inc(vector.tensor_tensor(r_var[:, 0:width], r_var[:, 0:width],
                                              r_msq[:, 0:width], ALU.subtract))
                wait(vector, se_a, base[se_a] + NF + 1)
                dv_a.inc(vector.reciprocal(r_rstd[:, 0:width], r_sd[:, 0:width]))
                vector.drain()
                dv_a.inc(vector.tensor_tensor(r_mmr[:, 0:width], r_mean[:, 0:width],
                                              r_rstd[:, 0:width], ALU.mult))
                wait(vector, pe_a, base[pe_a] + NF + 1)
                for f in range(NF):
                    vector.tensor_tensor(xbuf(f)[:, 0:width], xbuf(f)[:, 0:width],
                                         bank(2, width), ALU.mult)
                vector.drain()
                for f in range(NF):
                    dv_a.inc(vector.tensor_tensor(xbuf(f)[:, 0:width], xbuf(f)[:, 0:width],
                                                  bank(3, width), ALU.subtract))

        kv_slot_cnt = [0, 0, 0, 0]
        a2_done = 0
        a2_per_blk = (len(a2_chunks) + NSC - 1) // NSC
        for sc in range(NSC):
            with nc.named_scope(f"phB{sc}"), nc.Block(no_gpsimd_drain=True) as blk:
                base = snap()
                emit_ln(blk, xTs, sc * SC, SC, base)
                wmarks = []
                amarks = []
                a2_base = a2_done
                my_a2 = a2_chunks[a2_done:a2_done + a2_per_blk]
                a2_done += len(my_a2)

                def a2_wave(c):
                    return (3 * (c + 1) + NF - 1) // NF

                a2_pos = {}
                for ci in range(len(my_a2)):
                    if ci < 2:
                        a2_pos[ci] = 3 * ci
                    else:
                        a2_pos[ci] = max(3 * ci, NF * a2_wave(ci - 2) + 1)

                @blk.sync
                def _(sync):
                    widx = 0
                    nxt_a2 = [0]

                    def emit_a2_load(sync):
                        ci = nxt_a2[0]
                        if ci >= len(my_a2):
                            return
                        asrc, dh, dl, cw = my_a2[ci]
                        slot = (a2_base + ci) % 2
                        if ci >= 2:
                            wait(sync, dv_s, base[dv_s] + ci - 1)
                        amarks.append(slot_load(sync, s_p[slot], a2st[slot][:, 0:cw], asrc))
                        nxt_a2[0] += 1

                    for grp_kind, groups, wsrc in (("k", kgroups, wkT), ("v", vdp_pairs, wvT)):
                        for grp in groups:
                            if grp_kind == "k":
                                c0, cw = grp[0] * 128, len(grp) * 128
                            else:
                                c0, cw = grp[0] * VDW, len(grp) * VDW
                            for f in range(NF):
                                if widx >= 2:
                                    wait(sync, dv_h, base[dv_h] + widx - 1)
                                wmarks.append(slot_load(
                                    sync, s_w[widx % 2], wb2[widx % 2][:, 0:cw],
                                    wsrc[f * 128:(f + 1) * 128, c0:c0 + cw]))
                                widx += 1
                                while (nxt_a2[0] < len(my_a2)
                                       and a2_pos[nxt_a2[0]] <= widx):
                                    emit_a2_load(sync)
                    while nxt_a2[0] < len(my_a2):
                        emit_a2_load(sync)

                @blk.tensor
                def _(tensor):
                    widx = 0
                    nev = 0
                    for grp in kgroups:
                        if nev:
                            wait(tensor, dv_e, base[dv_e] + nev)
                        for f in range(NF):
                            wait(tensor, dv_h, base[dv_h] + widx + 1)
                            wait(tensor, dv_x, base[dv_x] + widx + 1)
                            for i, dc in enumerate(grp):
                                inst = mm3(tensor, bank(i),
                                           wsl[widx % 2][0][:, i * 128:(i + 1) * 128],
                                           wsl[widx % 2][1][:, i * 128:(i + 1) * 128],
                                           xsl[widx % 2][0], xsl[widx % 2][1],
                                           f == 0, f == NF - 1)
                            pe_w.inc(inst)
                            widx += 1
                        nev += len(grp)
                    for dps in vdp_pairs:
                        wait(tensor, dv_e, base[dv_e] + nev)
                        for f in range(NF):
                            wait(tensor, dv_h, base[dv_h] + widx + 1)
                            wait(tensor, dv_x, base[dv_x] + widx + 1)
                            for st in range(nst):
                                xh = xsl[widx % 2][0][:, st * 128:(st + 1) * 128]
                                xl = xsl[widx % 2][1][:, st * 128:(st + 1) * 128]
                                for di in range(len(dps)):
                                    bi = st * len(dps) + di
                                    tensor.matmul(bank(bi, VDW), xh,
                                                  wsl[widx % 2][0][:, di * VDW:(di + 1) * VDW],
                                                  start=(f == 0), stop=False)
                                    tensor.matmul(bank(bi, VDW), xh,
                                                  wsl[widx % 2][1][:, di * VDW:(di + 1) * VDW],
                                                  start=False, stop=False)
                                for di in range(len(dps)):
                                    bi = st * len(dps) + di
                                    inst = tensor.matmul(
                                        bank(bi, VDW), xl,
                                        wsl[widx % 2][0][:, di * VDW:(di + 1) * VDW],
                                        start=False, stop=(f == NF - 1))
                            pe_w.inc(inst)
                            widx += 1
                        nev += nst * len(dps)

                slot_seq = []

                @blk.scalar
                def _(scalar):
                    wait(scalar, dv_a, base[dv_a] + 4 + NF)
                    widx = 0
                    for grp_kind, groups in (("k", kgroups), ("v", vdp_pairs)):
                        for grp in groups:
                            cw = len(grp) * (128 if grp_kind == "k" else VDW)
                            for f in range(NF):
                                wait(scalar, s_w[widx % 2], wmarks[widx])
                                if widx >= 2:
                                    wait(scalar, pe_w, base[pe_w] + widx - 1)
                                se_a.inc(scalar.activation(
                                    out=wsl[widx % 2][0][:, 0:cw],
                                    in_=wb2[widx % 2][:, 0:cw], func=AF.Copy))
                                se_a.inc(scalar.activation(
                                    out=xsl[widx % 2][0], in_=xbuf(f), func=AF.Copy))
                                widx += 1

                @blk.vector
                def _(vector):
                    ev = 0
                    cnt = list(kv_slot_cnt)
                    nxt_sp = [0]
                    nsplit = [0]
                    se0 = base[se_a] + NF + 1

                    def emit_wx_split(vector, grp_w, f):
                        widx = nsplit[0]
                        cw = grp_w
                        wait(vector, se_a, se0 + 2 * (widx + 1))
                        dv_h.inc(vector.tensor_tensor(wsl[widx % 2][1][:, 0:cw],
                                                      wb2[widx % 2][:, 0:cw],
                                                      wsl[widx % 2][0][:, 0:cw],
                                                      ALU.subtract))
                        dv_x.inc(vector.tensor_tensor(xsl[widx % 2][1], xbuf(f),
                                                      xsl[widx % 2][0], ALU.subtract))
                        nsplit[0] += 1

                    def emit_a2_splits(vector, upto):
                        while nxt_sp[0] < min(upto, len(my_a2)):
                            ci = nxt_sp[0]
                            asrc, dh, dl, cw = my_a2[ci]
                            k = a2_base + ci
                            slot = k % 2
                            wait(vector, s_p[slot], amarks[ci])
                            if k // 2:
                                wait(vector, g_a[slot], 32 * (k // 2))
                            vector.tensor_copy(a2o[slot][0][:, 0:cw], a2st[slot][:, 0:cw])
                            vector.drain()
                            dv_s.inc(vector.tensor_tensor(a2o[slot][1][:, 0:cw],
                                                          a2st[slot][:, 0:cw],
                                                          a2o[slot][0][:, 0:cw],
                                                          ALU.subtract))
                            nxt_sp[0] += 1

                    wave = 0
                    for g, grp in enumerate(kgroups):
                        for f in range(NF):
                            emit_wx_split(vector, len(grp) * 128, f)
                        wait(vector, pe_w, base[pe_w] + NF * (g + 1))
                        for i, dc in enumerate(grp):
                            slot = ev % 4
                            if cnt[slot]:
                                wait(vector, g_k[slot], 16 * cnt[slot])
                            slot_seq.append(slot)
                            cnt[slot] += 1
                            dv_e.inc(vector.tensor_copy(kevac[slot], bank(i)))
                            ev += 1
                        wave += 1
                        emit_a2_splits(vector, (NF * wave) // 3)
                    for g, dps in enumerate(vdp_pairs):
                        for f in range(NF):
                            emit_wx_split(vector, len(dps) * VDW, f)
                        wait(vector, pe_w, base[pe_w] + NF * (len(kgroups) + g + 1))
                        bi = 0
                        for st in range(nst):
                            for di in range(len(dps)):
                                slot = ev % 4
                                if cnt[slot]:
                                    wait(vector, g_k[slot], 16 * cnt[slot])
                                slot_seq.append(slot)
                                cnt[slot] += 1
                                dv_e.inc(vector.tensor_copy(kevac[slot][:, 0:VDW],
                                                            bank(bi, VDW)))
                                bi += 1
                                ev += 1
                        wave += 1
                        emit_a2_splits(vector, (NF * wave) // 3)
                    emit_a2_splits(vector, len(my_a2))

                @blk.gpsimd
                def _(g):
                    ev = 0
                    nxt_st = [0]

                    def emit_a2_stores(g, upto):
                        while nxt_st[0] < min(upto, len(my_a2)):
                            ci = nxt_st[0]
                            asrc, dh, dl, cw = my_a2[ci]
                            k = a2_base + ci
                            slot = k % 2
                            wait(g, dv_s, base[dv_s] + ci + 1)
                            if g_a[slot].n:
                                wait(g, g_a[slot], g_a[slot].n)
                            g_a[slot].inc(g.dma_start(out=dh, in_=a2o[slot][0][:, 0:cw]), 16)
                            g_a[slot].inc(g.dma_start(out=dl, in_=a2o[slot][1][:, 0:cw]), 16)
                            nxt_st[0] += 1

                    wave = 0
                    for grp in kgroups:
                        for dc in grp:
                            slot = slot_seq[ev]
                            wait(g, dv_e, base[dv_e] + ev + 1)
                            if g_k[slot].n:
                                wait(g, g_k[slot], g_k[slot].n)
                            g_k[slot].inc(g.dma_start(
                                out=KTs[dc * 128:(dc + 1) * 128, sc * SC:(sc + 1) * SC],
                                in_=kevac[slot]), 16)
                            kv_slot_cnt[slot] += 1
                            ev += 1
                        wave += 1
                        emit_a2_stores(g, (NF * wave) // 3)
                    for dps in vdp_pairs:
                        for st in range(nst):
                            for dp in dps:
                                slot = slot_seq[ev]
                                wait(g, dv_e, base[dv_e] + ev + 1)
                                if g_k[slot].n:
                                    wait(g, g_k[slot], g_k[slot].n)
                                g_k[slot].inc(g.dma_start(
                                    out=Vs[(sc * nst + st) * 128:(sc * nst + st + 1) * 128,
                                           dp * VDW:(dp + 1) * VDW],
                                    in_=kevac[slot][:, 0:VDW]), 16)
                                kv_slot_cnt[slot] += 1
                                ev += 1
                        wave += 1
                        emit_a2_stores(g, (NF * wave) // 3)
                    emit_a2_stores(g, len(my_a2))
                    if sc == NSC - 1:
                        for s2 in range(4):
                            if g_k[s2].n:
                                wait(g, g_k[s2], g_k[s2].n)
                        for s2 in range(2):
                            if g_a[s2].n:
                                wait(g, g_a[s2], g_a[s2].n)

        for qc in range(NQC):
            with nc.named_scope(f"phB2_{qc}"), nc.Block(no_gpsimd_drain=True) as blk:
                base = snap()
                emit_ln(blk, xTq, qc * QCH, QCH, base)

                @blk.scalar
                def _(scalar):
                    wait(scalar, dv_a, base[dv_a] + 4 + NF)
                    for f in range(NF):
                        se_a.inc(scalar.activation(
                            out=xqh[:, f * SQ + qc * QCH:f * SQ + (qc + 1) * QCH],
                            in_=xbuf(f)[:, 0:QCH], func=AF.Copy))

                @blk.vector
                def _(vector):
                    for f in range(NF):
                        wait(vector, se_a, base[se_a] + NF + 1 + f + 1)
                        dv_a.inc(vector.tensor_tensor(
                            xql[:, f * SQ + qc * QCH:f * SQ + (qc + 1) * QCH],
                            xbuf(f)[:, 0:QCH],
                            xqh[:, f * SQ + qc * QCH:f * SQ + (qc + 1) * QCH],
                            ALU.subtract))

        def xqh_t(f, q0):
            return xqh[:, f * SQ + q0:f * SQ + q0 + QCH]

        def xql_t(f, q0):
            return xql[:, f * SQ + q0:f * SQ + q0 + QCH]

        for qc in range(NQC):
            q0 = qc * QCH
            with nc.named_scope(f"phC{qc}"), nc.Block(no_gpsimd_drain=True) as blk:
                base = snap()

                sched = []
                for e in range(2):
                    for gi, grp in enumerate(fgroups):
                        sched.append(('h', e, gi, grp))
                        sched.append(('o', e, gi, grp))

                tiles_per_step = []
                tile_srcs = []
                for kind, e, gi, grp in sched:
                    ha = (sWah, sWal) if e == 0 else (sW1ah, sW1al)
                    hb = (sWbh, sWbl) if e == 0 else (sW1bh, sW1bl)
                    if kind == 'h':
                        c0, cw = grp[0] * 128, len(grp) * 128
                        srcs = [(ha[0][f * 128:(f + 1) * 128, c0:c0 + cw],
                                 ha[1][f * 128:(f + 1) * 128, c0:c0 + cw], cw)
                                for f in range(NF)]
                    else:
                        srcs = []
                        for og in ogroups:
                            oc0, ocw = og[0] * 128, len(og) * 128
                            for fc in grp:
                                srcs.append((hb[0][fc * 128:(fc + 1) * 128, oc0:oc0 + ocw],
                                             hb[1][fc * 128:(fc + 1) * 128, oc0:oc0 + ocw],
                                             ocw))
                    tiles_per_step.append(len(srcs))
                    tile_srcs.append(srcs)
                tile_start = [0]
                for n in tiles_per_step:
                    tile_start.append(tile_start[-1] + n)

                comp = {}
                seo_c = 0
                dvo_c = 0
                for si, (kind, e, gi, grp) in enumerate(sched):
                    if kind != 'o':
                        continue
                    for bi2, og in enumerate(ogroups):
                        if e == 0 and gi == 0:
                            seo_c += len(og)
                            comp[(si, bi2)] = (se_o, seo_c)
                        else:
                            dvo_c += 1
                            comp[(si, bi2)] = (dv_o, dvo_c)
                    if e == 0 and gi == len(fgroups) - 1:
                        dvo_c += 1
                total_dvo = dvo_c

                cmarks = []

                @blk.sync
                def _(sync):
                    if qc == 0:
                        for s2 in range(2):
                            wait(sync, g_a[s2], g_a[s2].n)
                    widx = 0
                    for si in range(len(sched)):
                        for (hsrc, lsrc, cw) in tile_srcs[si]:
                            if widx >= 2:
                                wait(sync, pe_w, base[pe_w] + widx - 1)
                            cmarks.append(pair_load(
                                sync, s_w[widx % 2],
                                (wsl[widx % 2][0][:, 0:cw], wsl[widx % 2][1][:, 0:cw]),
                                (hsrc, lsrc)))
                            widx += 1

                @blk.tensor
                def _(tensor):
                    wait(tensor, dv_a, base[dv_a] + 1)
                    for f in range(NF):
                        inst = mm3(tensor, ps[0:1, 0:QCH], wgd_hi[:, f:f + 1],
                                   wgd_lo[:, f:f + 1], xqh_t(f, q0), xql_t(f, q0),
                                   f == 0, f == NF - 1)
                    pe_a.inc(inst)
                    wait(tensor, dv_a, base[dv_a] + 2)
                    pe_a.inc(tensor.matmul(bank(1, QCH), ones[0:1, :], r_gmask[:, 0:QCH],
                                           start=True, stop=True))
                    wait(tensor, dv_a, base[dv_a] + 4)
                    widx = 0
                    n_seh = 0
                    n_dvh = 0
                    last_comp = None
                    for si, (kind, e, gi, grp) in enumerate(sched):
                        if kind == 'h':
                            if last_comp is not None:
                                wait(tensor, last_comp[0], base[last_comp[0]] + last_comp[1])
                            n_dvh_c = n_dvh
                            for f in range(NF):
                                wait(tensor, s_w[widx % 2], cmarks[widx])
                                cw = len(grp) * 128
                                for i in range(len(grp)):
                                    inst = mm3(tensor, bank(i, QCH),
                                               wsl[widx % 2][0][:, i * 128:(i + 1) * 128],
                                               wsl[widx % 2][1][:, i * 128:(i + 1) * 128],
                                               xqh_t(f, q0), xql_t(f, q0),
                                               f == 0, f == NF - 1)
                                pe_w.inc(inst)
                                widx += 1
                            n_seh += 2 * len(grp)
                        else:
                            for bi2, og in enumerate(ogroups):
                                if bi2 > 0:
                                    c = comp[(si, bi2 - 1)]
                                    wait(tensor, c[0], base[c[0]] + c[1])
                                elif n_seh:
                                    wait(tensor, se_a, base[se_a] + n_seh)
                                for ti, fc in enumerate(grp):
                                    wait(tensor, dv_h, base[dv_h] + n_dvh + ti + 1)
                                    wait(tensor, s_w[widx % 2], cmarks[widx])
                                    hh = xn_hi[:, ti * QCH:(ti + 1) * QCH]
                                    hl = xn_lo[:, ti * QCH:(ti + 1) * QCH]
                                    for i in range(len(og)):
                                        inst = mm3(tensor, bank(i, QCH),
                                                   wsl[widx % 2][0][:, i * 128:(i + 1) * 128],
                                                   wsl[widx % 2][1][:, i * 128:(i + 1) * 128],
                                                   hh, hl, ti == 0, ti == len(grp) - 1)
                                    pe_w.inc(inst)
                                    widx += 1
                            n_dvh += len(grp)
                            last_comp = comp[(si, len(ogroups) - 1)]

                @blk.scalar
                def _(scalar):
                    wait(scalar, g_st, base[g_st])
                    widx_end_h = {}
                    widx = 0
                    for si2 in range(len(sched)):
                        widx += tiles_per_step[si2]
                        widx_end_h[si2] = widx
                    for si, (kind, e, gi, grp) in enumerate(sched):
                        if kind == 'h':
                            ba = c_b0a if e == 0 else c_b1a
                            fn = AF.Gelu if e == 0 else AF.Relu
                            wait(scalar, pe_w, base[pe_w] + widx_end_h[si])
                            for i, fc in enumerate(grp):
                                se_a.inc(scalar.activation(out=h32(i), in_=bank(i, QCH),
                                                           func=fn, bias=ba[:, fc:fc + 1]))
                                se_a.inc(scalar.activation(
                                    out=xn_hi[:, i * QCH:(i + 1) * QCH], in_=bank(i, QCH),
                                    func=fn, bias=ba[:, fc:fc + 1]))
                        else:
                            bb = c_b0b if e == 0 else c_b1b
                            if gi == 0:
                                for bi2, og in enumerate(ogroups):
                                    wait(scalar, pe_w,
                                         base[pe_w] + tile_start[si] + (bi2 + 1) * len(grp))
                                    for i, oc in enumerate(og):
                                        if e == 0:
                                            se_o.inc(scalar.activation(
                                                out=out0(oc), in_=bank(i, QCH),
                                                func=AF.Identity, bias=bb[:, oc:oc + 1]))
                                        else:
                                            se_o.inc(scalar.activation(
                                                out=RA[:, (8 + i) * QCH:(9 + i) * QCH],
                                                in_=bank(i, QCH),
                                                func=AF.Identity, bias=bb[:, oc:oc + 1]))

                @blk.vector
                def _(vector):
                    if qc == 0:
                        for s2 in range(2):
                            wait(vector, g_a[s2], g_a[s2].n)
                    vector.tensor_tensor(wgd, wg_sb[:, 0::2], wg_sb[:, 1::2], ALU.subtract)
                    vector.drain()
                    vector.tensor_copy(wgd_hi, wgd)
                    vector.drain()
                    dv_a.inc(vector.tensor_tensor(wgd_lo, wgd, wgd_hi, ALU.subtract))
                    wait(vector, pe_a, base[pe_a] + 1)
                    dv_a.inc(vector.tensor_scalar(r_gmask[:, 0:QCH], ps[0:1, 0:QCH],
                                                  0.0, None, ALU.is_ge))
                    wait(vector, pe_a, base[pe_a] + 2)
                    dv_a.inc(vector.tensor_copy(mask_b[:, 0:QCH], bank(1, QCH)))
                    vector.drain()
                    dv_a.inc(vector.tensor_scalar(invm[:, 0:QCH], mask_b[:, 0:QCH],
                                                  -1.0, 1.0, ALU.mult, ALU.add))
                    n_seh = 0
                    n_seo = 0
                    for si, (kind, e, gi, grp) in enumerate(sched):
                        if kind == 'h':
                            for i in range(len(grp)):
                                n_seh += 2
                                wait(vector, se_a, base[se_a] + n_seh)
                                dv_h.inc(vector.tensor_tensor(
                                    xn_lo[:, i * QCH:(i + 1) * QCH], h32(i),
                                    xn_hi[:, i * QCH:(i + 1) * QCH], ALU.subtract))
                        else:
                            for bi2, og in enumerate(ogroups):
                                if e == 0 and gi == 0:
                                    continue
                                pw = base[pe_w] + tile_start[si] + (bi2 + 1) * len(grp)
                                wait(vector, pe_w, pw)
                                if e == 0:
                                    vector.drain()
                                    for i, oc in enumerate(og):
                                        if i == len(og) - 1:
                                            dv_o.inc(vector.tensor_tensor(
                                                out0(oc), out0(oc), bank(i, QCH), ALU.add))
                                        else:
                                            vector.tensor_tensor(out0(oc), out0(oc),
                                                                 bank(i, QCH), ALU.add)
                                else:
                                    vector.drain()
                                    if gi == 0:
                                        wait(vector, se_o,
                                             base[se_o] + NF + (bi2 + 1) * len(og))
                                        for i, oc in enumerate(og):
                                            vector.tensor_tensor(
                                                RA[:, (8 + i) * QCH:(9 + i) * QCH],
                                                RA[:, (8 + i) * QCH:(9 + i) * QCH],
                                                invm[:, 0:QCH], ALU.mult)
                                    else:
                                        for i, oc in enumerate(og):
                                            vector.tensor_tensor(
                                                RA[:, (8 + i) * QCH:(9 + i) * QCH],
                                                bank(i, QCH), invm[:, 0:QCH], ALU.mult)
                                    vector.drain()
                                    for i, oc in enumerate(og):
                                        if i == len(og) - 1:
                                            dv_o.inc(vector.tensor_tensor(
                                                out0(oc), out0(oc),
                                                RA[:, (8 + i) * QCH:(9 + i) * QCH], ALU.add))
                                        else:
                                            vector.tensor_tensor(
                                                out0(oc), out0(oc),
                                                RA[:, (8 + i) * QCH:(9 + i) * QCH], ALU.add)
                            if e == 0 and gi == len(fgroups) - 1:
                                wait(vector, se_o, base[se_o] + NF)
                                vector.drain()
                                for oc in range(NF):
                                    if oc == NF - 1:
                                        dv_o.inc(vector.tensor_tensor(
                                            out0(oc), out0(oc), mask_b[:, 0:QCH], ALU.mult))
                                    else:
                                        vector.tensor_tensor(out0(oc), out0(oc),
                                                             mask_b[:, 0:QCH], ALU.mult)

                @blk.gpsimd
                def _(g):
                    wait(g, dv_o, base[dv_o] + total_dvo)
                    for oc in range(NF):
                        g_st.inc(g.dma_start(
                            out=Fs[oc * 128:(oc + 1) * 128, q0:q0 + QCH],
                            in_=out0(oc)), 16)
                    g.wait_ge(g_st.h, g_st.n)

        nqp = (SQ + 511) // 512
        for h in range(NH):
            with nc.named_scope(f"phD{h}"), nc.Block(no_gpsimd_drain=True) as blk:
                base = snap()

                def pt_base(i0):
                    return sum(j + 1 for j in range(i0))

                def dva_base(i0):
                    return sum((j + 1) + 2 for j in range(i0))

                def ev_base(i0):
                    return nqp + sum((j + 1) + 1 for j in range(i0))

                def sci(i0, kc):
                    return sum(j + 1 for j in range(i0)) + kc
                nsci = sum(j + 1 for j in range(NQT))

                kmarks = []

                @blk.sync
                def _(sync):
                    if h == 0:
                        wait(sync, g_st, base[g_st])
                    s_ld.inc(sync.dma_start(
                        out=Vb[:, 0:NKB * 128].rearrange("p (kb c) -> p kb c", kb=NKB),
                        in_=Vs[:, h * 128:(h + 1) * 128].rearrange(
                            "(kb p) c -> p kb c", kb=NKB)), 16)
                    for hf in range(2):
                        if hf == 1:
                            wait(sync, pe_a, base[pe_a] + 1)
                        for fi in range(FHALF):
                            f = hf * FHALF + fi
                            s_ld.inc(sync.dma_start(
                                out=wqh_b[:, fi * 128:(fi + 1) * 128],
                                in_=swqh[f * 128:(f + 1) * 128, h * 128:(h + 1) * 128]), 16)
                            s_ld.inc(sync.dma_start(
                                out=wql_b[:, fi * 128:(fi + 1) * 128],
                                in_=swql[f * 128:(f + 1) * 128, h * 128:(h + 1) * 128]), 16)
                        sync.wait_ge(s_ld.h, s_ld.n)
                    c = 0
                    for i0 in range(NQT):
                        for kc in range(i0 + 1):
                            slot = c % 3
                            if c >= 3:
                                wait(sync, pe_a, base[pe_a] + c)
                            kmarks.append(slot_load(
                                sync, s_k[slot], kcs[slot],
                                KTs[h * 128:(h + 1) * 128, kc * 512:(kc + 1) * 512]))
                            c += 1

                sbanks = [0, 1, 5, 6, 7]

                def chunk2tile(c):
                    i0 = 0
                    while c >= i0 + 1:
                        c -= i0 + 1
                        i0 += 1
                    return i0, c

                def emit_scores(tensor, i0, k0, k1):
                    for kc in range(k0, k1):
                        c = sci(i0, kc)
                        bk = sbanks[c % 5]
                        if c == 0:
                            wait(tensor, dv_e, base[dv_e] + nqp)
                        if c >= 5:
                            i0p, kcp = chunk2tile(c - 5)
                            wait(tensor, dv_a, base[dv_a] + dva_base(i0p) + kcp + 1)
                        wait(tensor, s_k[c % 3], kmarks[c])
                        pe_a.inc(tensor.matmul(
                            bank(bk), Qh[:, i0 * 128:(i0 + 1) * 128],
                            kcs[c % 3], start=True, stop=True))

                def emit_av(tensor, i0, g, ng):
                    wait(tensor, dv_e, base[dv_e] + ev_base(i0) + g + 1)
                    for t in range(4):
                        kb = 4 * g + t
                        inst = tensor.matmul(
                            ps[:, 4 * 512:4 * 512 + 128], Vb[:, kb * 128:(kb + 1) * 128],
                            ptb[g % 2][:, t * 128:(t + 1) * 128],
                            start=(kb == 0), stop=(kb == 4 * ng - 1))
                    pe_o.inc(inst)

                @blk.tensor
                def _(tensor):
                    for hf in range(2):
                        wait(tensor, s_ld,
                             base[s_ld] + 16 * (1 + 2 * FHALF * (hf + 1)))
                        for qp in range(nqp):
                            qw = min(512, SQ - qp * 512)
                            for fi in range(FHALF):
                                f = hf * FHALF + fi
                                inst = mm3(tensor, bank(5 + qp, qw),
                                           wqh_b[:, fi * 128:(fi + 1) * 128],
                                           wql_b[:, fi * 128:(fi + 1) * 128],
                                           xqh[:, f * SQ + qp * 512:f * SQ + qp * 512 + qw],
                                           xql[:, f * SQ + qp * 512:f * SQ + qp * 512 + qw],
                                           hf == 0 and fi == 0,
                                           hf == 1 and fi == FHALF - 1)
                        pe_a.inc(inst)
                    emit_scores(tensor, 0, 0, 1)
                    for i0 in range(NQT):
                        ng = i0 + 1
                        if i0 + 1 < NQT:
                            emit_scores(tensor, i0 + 1, 0, min(5, i0 + 2))
                        wait(tensor, se_a, base[se_a] + 2 * (i0 + 1))
                        for g in range(ng):
                            if g >= 2:
                                wait(tensor, dv_e, base[dv_e] + ev_base(i0) + g - 1)
                            for t in range(4):
                                inst = tensor.transpose(
                                    ps[:, (2 + g % 2) * 512 + t * 128:
                                       (2 + g % 2) * 512 + (t + 1) * 128],
                                    SP_sb[:, (4 * g + t) * 128:(4 * g + t + 1) * 128],
                                    ident)
                            pe_t.inc(inst)
                            if g >= 1:
                                emit_av(tensor, i0, g - 1, ng)
                        emit_av(tensor, i0, ng - 1, ng)
                        if i0 + 1 < NQT and i0 + 2 > 5:
                            emit_scores(tensor, i0 + 1, 5, i0 + 2)

                @blk.vector
                def _(vector):
                    for qp in range(nqp):
                        qw = min(512, SQ - qp * 512)
                        wait(vector, pe_a, base[pe_a] + 2)
                        dv_e.inc(vector.tensor_scalar_mul(Qh[:, qp * 512:qp * 512 + qw],
                                                          bank(5 + qp, qw), scale))
                    for i0 in range(NQT):
                        ng = i0 + 1
                        nkc = i0 + 1
                        wait(vector, se_a, base[se_a] + 2 * i0)
                        wait(vector, pe_t, base[pe_t] + pt_base(i0))
                        for kc in range(nkc):
                            c = sci(i0, kc)
                            bk = sbanks[c % 5]
                            wait(vector, pe_a, base[pe_a] + 2 + c + 1)
                            if kc == nkc - 1:
                                vector.tensor_tensor(bank(bk), bank(bk), maskt, ALU.add)
                                vector.drain()
                            vector.reduce_max(mx[:, kc:kc + 1], bank(bk), axis=AX.X)
                            dv_a.inc(vector.tensor_copy(SP_sb[:, kc * 512:(kc + 1) * 512],
                                                        bank(bk)))
                        vector.drain()
                        dv_a.inc(vector.reduce_max(negm, mx[:, 0:nkc], axis=AX.X,
                                                   negate=True))
                        wait(vector, se_a, base[se_a] + 2 * i0 + 1)
                        vector.tensor_scalar_add(rden, den, 1e-8)
                        vector.drain()
                        dv_a.inc(vector.reciprocal(rden, rden))
                        for g in range(ng):
                            wait(vector, pe_t, base[pe_t] + pt_base(i0) + g + 1)
                            if g >= 2:
                                wait(vector, pe_o, base[pe_o] + pt_base(i0) + g - 1)
                            dv_e.inc(vector.tensor_copy(
                                ptb[g % 2],
                                ps[:, (2 + g % 2) * 512:(3 + g % 2) * 512]))
                        wait(vector, pe_o, base[pe_o] + pt_base(i0) + ng)
                        dv_e.inc(vector.tensor_copy(
                            attnbuf(h, i0 * 128, (i0 + 1) * 128),
                            ps[:, 4 * 512:4 * 512 + 128]))

                @blk.scalar
                def _(scalar):
                    for i0 in range(NQT):
                        E = 4 * i0 + 4
                        nkc = i0 + 1
                        wait(scalar, dv_a, base[dv_a] + dva_base(i0) + nkc + 1)
                        wait(scalar, pe_t, base[pe_t] + pt_base(i0))
                        se_a.inc(scalar.activation(out=SP_sb[:, 0:E * 128],
                                                   in_=SP_sb[:, 0:E * 128], func=AF.Exp,
                                                   scale=1.0, bias=negm, accum_out=den))
                        wait(scalar, dv_a, base[dv_a] + dva_base(i0) + nkc + 2)
                        se_a.inc(scalar.activation(out=SP_sb[:, 0:E * 128],
                                                   in_=SP_sb[:, 0:E * 128],
                                                   func=AF.Identity,
                                                   bias=zero_col[:, 0:1], scale=rden))

        with nc.named_scope("phE"), nc.Block(no_gpsimd_drain=True) as blk:
            base = snap()
            EQC = SQ // QCH
            wmarks = []
            pmarks = []
            smarks = []

            @blk.sync
            def _(sync):
                wait(sync, g_st, base[g_st])
                widx = 0
                lidx = 0
                for grp in ogroups:
                    c0, cw = grp[0] * 128, len(grp) * 128
                    for qcc in range(EQC):
                        q0 = qcc * QCH
                        for f in range(NF):
                            if widx >= 2:
                                wait(sync, pe_w, base[pe_w] + widx - 1)
                            wmarks.append(pair_load(
                                sync, s_w[widx % 2],
                                (wsl[widx % 2][0][:, 0:cw], wsl[widx % 2][1][:, 0:cw]),
                                (swoh[f * 128:(f + 1) * 128, c0:c0 + cw],
                                 swol[f * 128:(f + 1) * 128, c0:c0 + cw])))
                            widx += 1
                        for oc in grp:
                            slot = lidx % 2
                            if lidx >= 2:
                                wait(sync, dv_a, base[dv_a] + lidx - 1)
                            if s_p[slot].n:
                                wait(sync, s_p[slot], s_p[slot].n)
                            s_p[slot].inc(sync.dma_start(
                                out=ld2[2 * slot][:, 0:QCH],
                                in_=Fs[oc * 128:(oc + 1) * 128, q0:q0 + QCH]), 16)
                            s_p[slot].inc(sync.dma_start(
                                out=ld2[2 * slot + 1][:, 0:QCH],
                                in_=xTq[oc * 128:(oc + 1) * 128, q0:q0 + QCH]), 16)
                            pmarks.append(s_p[slot].n)
                            lidx += 1

            @blk.vector
            def _(vector):
                widx = 0
                lidx = 0
                for grp in ogroups:
                    for qcc in range(EQC):
                        q0 = qcc * QCH
                        for f in range(NF):
                            slot = widx % 2
                            if widx >= 2:
                                wait(vector, pe_w, base[pe_w] + widx - 1)
                            vector.tensor_copy(aspl[slot][0][:, 0:QCH],
                                               attnbuf(f, q0, q0 + QCH))
                            vector.drain()
                            dv_s.inc(vector.tensor_tensor(
                                aspl[slot][1][:, 0:QCH], attnbuf(f, q0, q0 + QCH),
                                aspl[slot][0][:, 0:QCH], ALU.subtract))
                            widx += 1
                        gq = widx // NF
                        wait(vector, pe_w, base[pe_w] + gq * NF)
                        for i, oc in enumerate(grp):
                            slot = lidx % 2
                            wait(vector, s_p[slot], pmarks[lidx])
                            if lidx >= 2:
                                wait(vector, g_o[slot], 16 * (lidx // 2))
                            dv_e.inc(vector.tensor_tensor(
                                ob2[slot], bank(i, QCH), ld2[2 * slot][:, 0:QCH], ALU.add))
                            vector.drain()
                            dv_a.inc(vector.tensor_tensor(
                                ob2[slot], ob2[slot], ld2[2 * slot + 1][:, 0:QCH], ALU.add))
                            lidx += 1

            @blk.tensor
            def _(tensor):
                widx = 0
                nev = 0
                for grp in ogroups:
                    for qcc in range(EQC):
                        if nev:
                            wait(tensor, dv_e, base[dv_e] + nev)
                        for f in range(NF):
                            slot = widx % 2
                            wait(tensor, s_w[slot], wmarks[widx])
                            wait(tensor, dv_s, base[dv_s] + widx + 1)
                            for i in range(len(grp)):
                                inst = mm3(tensor, bank(i, QCH),
                                           wsl[slot][0][:, i * 128:(i + 1) * 128],
                                           wsl[slot][1][:, i * 128:(i + 1) * 128],
                                           aspl[slot][0][:, 0:QCH], aspl[slot][1][:, 0:QCH],
                                           f == 0, f == NF - 1)
                            pe_w.inc(inst)
                            widx += 1
                        nev += len(grp)

            @blk.gpsimd
            def _(g):
                lidx = 0
                for grp in ogroups:
                    for qcc in range(EQC):
                        q0 = qcc * QCH
                        for oc in grp:
                            slot = lidx % 2
                            wait(g, dv_a, base[dv_a] + lidx + 1)
                            if g_o[slot].n:
                                wait(g, g_o[slot], g_o[slot].n)
                            g_o[slot].inc(g.dma_start(
                                out=outT[oc * 128:(oc + 1) * 128, q0:q0 + QCH],
                                in_=ob2[slot]), 16)
                            lidx += 1
                g.wait_ge(g_o[0].h, g_o[0].n)
                g.wait_ge(g_o[1].h, g_o[1].n)

    return nc


def _make_masks(j):
    r = np.arange(128)[:, None]
    out = np.zeros((128, 512), dtype=np.float32)
    for t in range(4):
        c = np.arange(128)[None, :]
        allowed = (128 * t + c) <= (4 * r + j)
        out[:, t * 128:(t + 1) * 128] = np.where(allowed, 0.0, NEG_BIG).astype(np.float32)
    return out


def _consts_tile():
    c = np.zeros((128, 272), dtype=np.float32)
    c[:, 0:128] = 1.0
    c[:, 128:256] = np.eye(128, dtype=np.float32)
    c[:, 256] = 1e-5
    return c


def kernel(L0_attn_x, L0_attn_wq, L0_attn_wk, L0_attn_wv, L0_attn_wo,
           L0_ffn_Wg, L0_ffn_W0a, L0_ffn_b0a, L0_ffn_W0b, L0_ffn_b0b,
           L0_ffn_W1a, L0_ffn_b1a, L0_ffn_W1b, L0_ffn_b1b):
    x = np.asarray(L0_attn_x)
    B, S, H = x.shape
    NH = 16
    FF = np.asarray(L0_ffn_W0a).shape[0]
    NF = H // 128
    NFC = FF // 128

    nc = build_nc(H=H, S=S, NH=NH, FF=FF)

    def t(a):
        return np.ascontiguousarray(np.asarray(a, dtype=np.float32).T)

    def b2(a, n):
        return np.ascontiguousarray(np.asarray(a, dtype=np.float32).reshape(n, 128).T)

    shared = {
        "wqT": t(L0_attn_wq), "wkT": t(L0_attn_wk), "wvT": t(L0_attn_wv),
        "woT": t(L0_attn_wo), "wgT": t(L0_ffn_Wg),
        "W0aT": t(L0_ffn_W0a), "W0bT": t(L0_ffn_W0b),
        "W1aT": t(L0_ffn_W1a), "W1bT": t(L0_ffn_W1b),
        "b0a2": b2(L0_ffn_b0a, NFC), "b1a2": b2(L0_ffn_b1a, NFC),
        "b0b2": b2(L0_ffn_b0b, NF), "b1b2": b2(L0_ffn_b1b, NF),
        "consts": _consts_tile(),
    }
    in_maps = []
    for core in range(8):
        b, j = core // 4, core % 4
        im = dict(shared)
        im["xTs"] = np.ascontiguousarray(x[b].T)
        im["xTq"] = np.ascontiguousarray(x[b, j::4, :].T)
        im["masks"] = _make_masks(j)
        in_maps.append(im)

    res = run_bass_kernel_spmd(nc, in_maps, core_ids=list(range(8)))

    out = np.empty((B, S, H), dtype=np.float32)
    for core in range(8):
        b, j = core // 4, core % 4
        out[b, j::4, :] = res.results[core]["outT"].T
    return out


# version 21
# speedup vs baseline: 1.0604x; 1.0307x over previous
import math
import contextlib
import numpy as np
import concourse.bass as bass
import concourse.mybir as mybir
from concourse.bass_utils import run_bass_kernel_spmd

F32 = mybir.dt.float32
BF16 = mybir.dt.bfloat16
AF = mybir.ActivationFunctionType
ALU = mybir.AluOpType
AX = mybir.AxisListType

NEG_BIG = -1.0e30


def build_nc(H=2048, S=4096, NH=16, FF=4096):
    HD = H // NH
    assert HD == 128 and H % 128 == 0 and FF % 128 == 0 and S % 512 == 0
    NF = H // 128
    SC = 512
    NSC = S // SC
    SQ = S // 4
    NQT = SQ // 128
    NFC = FF // 128
    QCH = min(512, SQ)
    NQC = SQ // QCH
    NKB = S // 128
    scale = 1.0 / math.sqrt(HD)

    nc = bass.Bass("TRN2", target_bir_lowering=False)

    xTs = nc.dram_tensor("xTs", [H, S], F32, kind="ExternalInput").ap()
    xTq = nc.dram_tensor("xTq", [H, SQ], F32, kind="ExternalInput").ap()
    wqT = nc.dram_tensor("wqT", [H, H], F32, kind="ExternalInput").ap()
    wkT = nc.dram_tensor("wkT", [H, H], F32, kind="ExternalInput").ap()
    wvT = nc.dram_tensor("wvT", [H, H], F32, kind="ExternalInput").ap()
    woT = nc.dram_tensor("woT", [H, H], F32, kind="ExternalInput").ap()
    wgT = nc.dram_tensor("wgT", [H, 2], F32, kind="ExternalInput").ap()
    W0aT = nc.dram_tensor("W0aT", [H, FF], F32, kind="ExternalInput").ap()
    W0bT = nc.dram_tensor("W0bT", [FF, H], F32, kind="ExternalInput").ap()
    W1aT = nc.dram_tensor("W1aT", [H, FF], F32, kind="ExternalInput").ap()
    W1bT = nc.dram_tensor("W1bT", [FF, H], F32, kind="ExternalInput").ap()
    b0a2 = nc.dram_tensor("b0a2", [128, NFC], F32, kind="ExternalInput").ap()
    b1a2 = nc.dram_tensor("b1a2", [128, NFC], F32, kind="ExternalInput").ap()
    b0b2 = nc.dram_tensor("b0b2", [128, NF], F32, kind="ExternalInput").ap()
    b1b2 = nc.dram_tensor("b1b2", [128, NF], F32, kind="ExternalInput").ap()
    masks_in = nc.dram_tensor("masks", [128, 512], F32, kind="ExternalInput").ap()
    consts = nc.dram_tensor("consts", [128, 272], F32, kind="ExternalInput").ap()
    outT = nc.dram_tensor("outT", [H, SQ], F32, kind="ExternalOutput").ap()
    KTs = nc.dram_tensor("KTs", [H, S], F32, kind="Internal").ap()
    Vs = nc.dram_tensor("Vs", [S, H], F32, kind="Internal").ap()
    Fs = nc.dram_tensor("Fs", [H, SQ], F32, kind="Internal").ap()
    sWah = nc.dram_tensor("sWah", [H, FF], BF16, kind="Internal").ap()
    sWal = nc.dram_tensor("sWal", [H, FF], BF16, kind="Internal").ap()
    sWbh = nc.dram_tensor("sWbh", [FF, H], BF16, kind="Internal").ap()
    sWbl = nc.dram_tensor("sWbl", [FF, H], BF16, kind="Internal").ap()
    sW1ah = nc.dram_tensor("sW1ah", [H, FF], BF16, kind="Internal").ap()
    sW1al = nc.dram_tensor("sW1al", [H, FF], BF16, kind="Internal").ap()
    sW1bh = nc.dram_tensor("sW1bh", [FF, H], BF16, kind="Internal").ap()
    sW1bl = nc.dram_tensor("sW1bl", [FF, H], BF16, kind="Internal").ap()
    swqh = nc.dram_tensor("swqh", [H, H], BF16, kind="Internal").ap()
    swql = nc.dram_tensor("swql", [H, H], BF16, kind="Internal").ap()
    swoh = nc.dram_tensor("swoh", [H, H], BF16, kind="Internal").ap()
    swol = nc.dram_tensor("swol", [H, H], BF16, kind="Internal").ap()

    kgroups = [list(range(g, min(g + 8, NF))) for g in range(0, NF, 8)]
    ndp = max(1, H // 512)
    VDW = min(512, H)
    vdp_pairs = [list(range(d, min(d + 2, ndp))) for d in range(0, ndp, 2)]
    nst = SC // 128
    fgroups = [list(range(g, min(g + 8, NFC))) for g in range(0, NFC, 8)]
    ogroups = [list(range(g, min(g + 8, NF))) for g in range(0, NF, 8)]
    GW = 8 * QCH

    a2_chunks = []
    for srcT, dh, dl in ((W0aT, sWah, sWal), (W0bT, sWbh, sWbl),
                         (W1aT, sW1ah, sW1al), (W1bT, sW1bh, sW1bl),
                         (wqT, swqh, swql), (woT, swoh, swol)):
        rows, cols = srcT.tensor.shape
        for rb in range(rows // 128):
            for c0 in range(0, cols, 2048):
                cw = min(2048, cols - c0)
                a2_chunks.append((srcT[rb * 128:(rb + 1) * 128, c0:c0 + cw],
                                  dh[rb * 128:(rb + 1) * 128, c0:c0 + cw],
                                  dl[rb * 128:(rb + 1) * 128, c0:c0 + cw], cw))

    ctx = contextlib.ExitStack()
    with ctx:
        xqh = ctx.enter_context(nc.sbuf_tensor("xqh", [128, NF * SQ], BF16))
        xql = ctx.enter_context(nc.sbuf_tensor("xql", [128, NF * SQ], BF16))
        RA = ctx.enter_context(nc.sbuf_tensor("RA", [128, 16384], F32))
        RB = ctx.enter_context(nc.sbuf_tensor("RB", [128, max(2 * S + 1536, 4096)], F32))
        XN = ctx.enter_context(nc.sbuf_tensor("XN", [128, 2 * max(GW, 4096)], BF16))
        WS = ctx.enter_context(nc.sbuf_tensor("WS", [128, 6144], BF16))
        RC = ctx.enter_context(nc.sbuf_tensor("RC", [128, 2304], F32))
        RS = ctx.enter_context(nc.sbuf_tensor("RS", [128, 1024], F32))
        ps = ctx.enter_context(nc.psum_tensor("ps", [128, 4096], F32))

        def bank(k, w=512):
            return ps[:, k * 512:k * 512 + w]

        ones = RS[:, 0:128]
        ident = RS[:, 128:256]
        eps_col = RS[:, 256:257]
        zero_col = RS[:, 257:258]
        maskt = RS[:, 272:784]
        c_b0a = RS[:, 784:784 + NFC]
        c_b1a = RS[:, 816:816 + NFC]
        c_b0b = RS[:, 848:848 + NF]
        c_b1b = RS[:, 864:864 + NF]
        wg_sb = RS[:, 880:880 + 2 * NF]
        mx = RS[:, 912:920]
        m_col = RS[:, 920:921]
        negm = RS[:, 921:922]
        den = RS[:, 922:923]
        rden = RS[:, 923:924]
        wgd = RS[:, 928:928 + NF]

        SP_sb = RB[:, 0:S]
        Vb = RB[:, S:2 * S]
        kcs = [RB[:, 2 * S + i * 512:2 * S + (i + 1) * 512] for i in range(3)]
        kevac = [RB[:, i * 512:(i + 1) * 512] for i in range(4)]
        sqbuf = RB[:, 2048:2560]
        rowA = RB[0:1, 2560:3072]
        rowB = RB[0:1, 3072:3584]
        rowC = RB[0:1, 3584:4096]
        r_mean = rowA
        r_msq, r_sd, r_mmr = rowB, rowB, rowB
        r_var, r_rstd = rowC, rowC

        def out0(oc):
            return RB[:, oc * QCH:(oc + 1) * QCH]
        ld2 = [RB[:, 2048 + i * 512:2048 + (i + 1) * 512] for i in range(4)]
        ob2 = [RB[:, i * 512:i * 512 + QCH] for i in range(2)]

        xn_hi = XN[:, 0:GW]
        xn_lo = XN[:, GW:2 * GW]
        a2o = [(XN[:, 0:2048], XN[:, 2048:4096]),
               (XN[:, 4096:6144], XN[:, 6144:8192])]

        wsl = [(WS[:, i * 2048:i * 2048 + 1024], WS[:, i * 2048 + 1024:(i + 1) * 2048])
               for i in range(2)]
        FHALF = max(1, NF // 2)
        wqh_b = WS[:, 4096:4096 + FHALF * 128]
        wql_b = WS[:, 4096 + FHALF * 128:4096 + 2 * FHALF * 128]
        aspl = [(WS[:, 4096 + i * 1024:4096 + i * 1024 + 512],
                 WS[:, 4096 + i * 1024 + 512:4096 + (i + 1) * 1024])
                for i in range(2)]
        xsl = [(WS[:, 4096 + i * 1024:4096 + i * 1024 + 512],
                WS[:, 4096 + i * 1024 + 512:4096 + (i + 1) * 1024])
               for i in range(2)]
        wgd_hi = WS[:, 4096:4096 + NF]
        wgd_lo = WS[:, 4096 + 64:4096 + 64 + NF]

        wb2 = [RC[:, 0:1024], RC[:, 1024:2048]]
        mask_b = RC[:, 0:512]
        r_gmask = RC[0:1, 0:512]
        invm = RC[:, 512:1024]
        Qh = RC[:, 0:SQ]
        ptb = [RC[:, 1024 + i * 512:1024 + (i + 1) * 512] for i in range(2)]

        def h32(i):
            return RA[:, i * QCH:(i + 1) * QCH]

        def xbuf(f):
            return RA[:, f * SC:(f + 1) * SC]

        a2st = [RA[:, 8192 + i * 2048:8192 + (i + 1) * 2048] for i in range(2)]

        def attnbuf(h, c0, c1):
            return RA[:, h * SQ + c0:h * SQ + c1]

        class Sem:
            def __init__(self, name):
                self.h = ctx.enter_context(nc.semaphore(name))
                self.n = 0

            def inc(self, inst, k=1):
                inst.then_inc(self.h, k)
                self.n += k
                return inst

        s_ld = Sem("s_ld")
        s_w = [Sem("s_w0"), Sem("s_w1")]
        s_p = [Sem("s_p0"), Sem("s_p1")]
        s_k = [Sem("s_k0"), Sem("s_k1"), Sem("s_k2")]
        g_st = Sem("g_st")
        g_k = [Sem(f"g_k{i}") for i in range(4)]
        g_o = [Sem("g_o0"), Sem("g_o1")]
        g_a = [Sem("g_a0"), Sem("g_a1")]
        pe_a = Sem("pe_a")
        pe_w = Sem("pe_w")
        pe_t = Sem("pe_t")
        pe_o = Sem("pe_o")
        dv_a = Sem("dv_a")
        dv_e = Sem("dv_e")
        dv_s = Sem("dv_s")
        dv_h = Sem("dv_h")
        dv_x = Sem("dv_x")
        dv_o = Sem("dv_o")
        se_a = Sem("se_a")
        se_o = Sem("se_o")
        ALL = [s_ld, g_st, pe_a, pe_w, pe_t, pe_o, dv_a, dv_e, dv_s, dv_h, dv_o, dv_x,
               se_a, se_o] + s_w + s_p + s_k + g_k + g_o + g_a

        def snap():
            return {s: s.n for s in ALL}

        def wait(eng, sem, v):
            eng.wait_ge(sem.h, v)

        def slot_load(sync, slot_sem, out_ap, in_ap):
            if slot_sem.n:
                wait(sync, slot_sem, slot_sem.n)
            slot_sem.inc(sync.dma_start(out=out_ap, in_=in_ap), 16)
            return slot_sem.n

        def mm3(tensor, out_ap, sl_hi, sl_lo, rhs_hi, rhs_lo, start, stop):
            tensor.matmul(out_ap, sl_hi, rhs_hi, start=start, stop=False)
            tensor.matmul(out_ap, sl_hi, rhs_lo, start=False, stop=False)
            return tensor.matmul(out_ap, sl_lo, rhs_hi, start=False, stop=stop)

        def pair_load(sync, slot_sem, outs, ins):
            if slot_sem.n:
                wait(sync, slot_sem, slot_sem.n)
            for o, i in zip(outs, ins):
                slot_sem.inc(sync.dma_start(out=o, in_=i), 16)
            return slot_sem.n

        with nc.named_scope("phA"), nc.Block(no_gpsimd_drain=True) as blk:
            @blk.sync
            def _(sync):
                s_ld.inc(sync.dma_start(out=maskt, in_=masks_in), 16)
                s_ld.inc(sync.dma_start(out=RS[:, 0:272], in_=consts), 16)
                s_ld.inc(sync.dma_start(out=c_b0a, in_=b0a2), 16)
                s_ld.inc(sync.dma_start(out=c_b1a, in_=b1a2), 16)
                s_ld.inc(sync.dma_start(out=c_b0b, in_=b0b2), 16)
                s_ld.inc(sync.dma_start(out=c_b1b, in_=b1b2), 16)
                for f in range(NF):
                    s_ld.inc(sync.dma_start(out=wg_sb[:, 2 * f:2 * f + 2],
                                            in_=wgT[f * 128:(f + 1) * 128, :]), 16)
                sync.wait_ge(s_ld.h, s_ld.n)


        def emit_ln(blk, src_dram, col0, width, base):
            @blk.sync
            def _(sync):
                for f in range(NF):
                    s_ld.inc(sync.dma_start(
                        out=xbuf(f)[:, 0:width],
                        in_=src_dram[f * 128:(f + 1) * 128, col0:col0 + width]), 16)
                sync.wait_ge(s_ld.h, s_ld.n)

            xtot = base[s_ld] + 16 * NF

            @blk.scalar
            def _(scalar):
                wait(scalar, s_ld, xtot)
                for f in range(NF):
                    if f > 0:
                        wait(scalar, pe_a, base[pe_a] + f)
                    se_a.inc(scalar.activation(out=sqbuf[:, 0:width],
                                               in_=xbuf(f)[:, 0:width], func=AF.Square))
                wait(scalar, dv_a, base[dv_a] + 2)
                se_a.inc(scalar.activation(out=r_sd[:, 0:width], in_=r_var[:, 0:width],
                                           func=AF.Sqrt, bias=eps_col[0:1, 0:1]))

            @blk.tensor
            def _(tensor):
                wait(tensor, s_ld, xtot)
                for f in range(NF):
                    wait(tensor, se_a, base[se_a] + f + 1)
                    tensor.matmul(ps[0:1, 0:width], ones[:, 0:1], xbuf(f)[:, 0:width],
                                  start=(f == 0), stop=(f == NF - 1))
                    pe_a.inc(tensor.matmul(ps[0:1, 512:512 + width], ones[:, 0:1],
                                           sqbuf[:, 0:width],
                                           start=(f == 0), stop=(f == NF - 1)))
                wait(tensor, dv_a, base[dv_a] + 4)
                tensor.matmul(bank(2, width), ones[0:1, :], r_rstd[:, 0:width],
                              start=True, stop=True)
                pe_a.inc(tensor.matmul(bank(3, width), ones[0:1, :], r_mmr[:, 0:width],
                                       start=True, stop=True))

            @blk.vector
            def _(vector):
                wait(vector, pe_a, base[pe_a] + NF)
                dv_a.inc(vector.tensor_scalar_mul(r_mean[:, 0:width], ps[0:1, 0:width], 1.0 / H))
                vector.tensor_scalar_mul(r_var[:, 0:width], ps[0:1, 512:512 + width], 1.0 / H)
                vector.drain()
                vector.tensor_tensor(r_msq[:, 0:width], r_mean[:, 0:width],
                                     r_mean[:, 0:width], ALU.mult)
                vector.drain()
                dv_a.inc(vector.tensor_tensor(r_var[:, 0:width], r_var[:, 0:width],
                                              r_msq[:, 0:width], ALU.subtract))
                wait(vector, se_a, base[se_a] + NF + 1)
                dv_a.inc(vector.reciprocal(r_rstd[:, 0:width], r_sd[:, 0:width]))
                vector.drain()
                dv_a.inc(vector.tensor_tensor(r_mmr[:, 0:width], r_mean[:, 0:width],
                                              r_rstd[:, 0:width], ALU.mult))
                wait(vector, pe_a, base[pe_a] + NF + 1)
                for f in range(NF):
                    vector.tensor_tensor(xbuf(f)[:, 0:width], xbuf(f)[:, 0:width],
                                         bank(2, width), ALU.mult)
                vector.drain()
                for f in range(NF):
                    dv_a.inc(vector.tensor_tensor(xbuf(f)[:, 0:width], xbuf(f)[:, 0:width],
                                                  bank(3, width), ALU.subtract))

        kv_slot_cnt = [0, 0, 0, 0]
        a2_done = 0
        a2_per_blk = (len(a2_chunks) + NSC - 1) // NSC
        for sc in range(NSC):
            with nc.named_scope(f"phB{sc}"), nc.Block(no_gpsimd_drain=True) as blk:
                base = snap()
                emit_ln(blk, xTs, sc * SC, SC, base)
                wmarks = []
                amarks = []
                a2_base = a2_done
                my_a2 = a2_chunks[a2_done:a2_done + a2_per_blk]
                a2_done += len(my_a2)

                def a2_wave(c):
                    return (3 * (c + 1) + NF - 1) // NF

                a2_pos = {}
                for ci in range(len(my_a2)):
                    if ci < 2:
                        a2_pos[ci] = 3 * ci
                    else:
                        a2_pos[ci] = max(3 * ci, NF * a2_wave(ci - 2) + 1)

                @blk.sync
                def _(sync):
                    widx = 0
                    nxt_a2 = [0]

                    def emit_a2_load(sync):
                        ci = nxt_a2[0]
                        if ci >= len(my_a2):
                            return
                        asrc, dh, dl, cw = my_a2[ci]
                        slot = (a2_base + ci) % 2
                        if ci >= 2:
                            wait(sync, dv_s, base[dv_s] + ci - 1)
                        amarks.append(slot_load(sync, s_p[slot], a2st[slot][:, 0:cw], asrc))
                        nxt_a2[0] += 1

                    for grp_kind, groups, wsrc in (("k", kgroups, wkT), ("v", vdp_pairs, wvT)):
                        for grp in groups:
                            if grp_kind == "k":
                                c0, cw = grp[0] * 128, len(grp) * 128
                            else:
                                c0, cw = grp[0] * VDW, len(grp) * VDW
                            for f in range(NF):
                                if widx >= 2:
                                    wait(sync, dv_h, base[dv_h] + widx - 1)
                                wmarks.append(slot_load(
                                    sync, s_w[widx % 2], wb2[widx % 2][:, 0:cw],
                                    wsrc[f * 128:(f + 1) * 128, c0:c0 + cw]))
                                widx += 1
                                while (nxt_a2[0] < len(my_a2)
                                       and a2_pos[nxt_a2[0]] <= widx):
                                    emit_a2_load(sync)
                    while nxt_a2[0] < len(my_a2):
                        emit_a2_load(sync)

                @blk.tensor
                def _(tensor):
                    widx = 0
                    nev = 0
                    for grp in kgroups:
                        if nev:
                            wait(tensor, dv_e, base[dv_e] + nev)
                        for f in range(NF):
                            wait(tensor, dv_h, base[dv_h] + widx + 1)
                            wait(tensor, dv_x, base[dv_x] + widx + 1)
                            for i, dc in enumerate(grp):
                                inst = mm3(tensor, bank(i),
                                           wsl[widx % 2][0][:, i * 128:(i + 1) * 128],
                                           wsl[widx % 2][1][:, i * 128:(i + 1) * 128],
                                           xsl[widx % 2][0], xsl[widx % 2][1],
                                           f == 0, f == NF - 1)
                            pe_w.inc(inst)
                            widx += 1
                        nev += len(grp)
                    for dps in vdp_pairs:
                        wait(tensor, dv_e, base[dv_e] + nev)
                        for f in range(NF):
                            wait(tensor, dv_h, base[dv_h] + widx + 1)
                            wait(tensor, dv_x, base[dv_x] + widx + 1)
                            for st in range(nst):
                                xh = xsl[widx % 2][0][:, st * 128:(st + 1) * 128]
                                xl = xsl[widx % 2][1][:, st * 128:(st + 1) * 128]
                                for di in range(len(dps)):
                                    bi = st * len(dps) + di
                                    tensor.matmul(bank(bi, VDW), xh,
                                                  wsl[widx % 2][0][:, di * VDW:(di + 1) * VDW],
                                                  start=(f == 0), stop=False)
                                    tensor.matmul(bank(bi, VDW), xh,
                                                  wsl[widx % 2][1][:, di * VDW:(di + 1) * VDW],
                                                  start=False, stop=False)
                                for di in range(len(dps)):
                                    bi = st * len(dps) + di
                                    inst = tensor.matmul(
                                        bank(bi, VDW), xl,
                                        wsl[widx % 2][0][:, di * VDW:(di + 1) * VDW],
                                        start=False, stop=(f == NF - 1))
                            pe_w.inc(inst)
                            widx += 1
                        nev += nst * len(dps)

                slot_seq = []

                @blk.scalar
                def _(scalar):
                    wait(scalar, dv_a, base[dv_a] + 4 + NF)
                    widx = 0
                    for grp_kind, groups in (("k", kgroups), ("v", vdp_pairs)):
                        for grp in groups:
                            cw = len(grp) * (128 if grp_kind == "k" else VDW)
                            for f in range(NF):
                                wait(scalar, s_w[widx % 2], wmarks[widx])
                                if widx >= 2:
                                    wait(scalar, pe_w, base[pe_w] + widx - 1)
                                se_a.inc(scalar.activation(
                                    out=wsl[widx % 2][0][:, 0:cw],
                                    in_=wb2[widx % 2][:, 0:cw], func=AF.Copy))
                                se_a.inc(scalar.activation(
                                    out=xsl[widx % 2][0], in_=xbuf(f), func=AF.Copy))
                                widx += 1

                @blk.vector
                def _(vector):
                    ev = 0
                    cnt = list(kv_slot_cnt)
                    nxt_sp = [0]
                    nsplit = [0]
                    se0 = base[se_a] + NF + 1

                    def emit_wx_split(vector, grp_w, f):
                        widx = nsplit[0]
                        cw = grp_w
                        wait(vector, se_a, se0 + 2 * (widx + 1))
                        dv_h.inc(vector.tensor_tensor(wsl[widx % 2][1][:, 0:cw],
                                                      wb2[widx % 2][:, 0:cw],
                                                      wsl[widx % 2][0][:, 0:cw],
                                                      ALU.subtract))
                        dv_x.inc(vector.tensor_tensor(xsl[widx % 2][1], xbuf(f),
                                                      xsl[widx % 2][0], ALU.subtract))
                        nsplit[0] += 1

                    def emit_a2_splits(vector, upto):
                        while nxt_sp[0] < min(upto, len(my_a2)):
                            ci = nxt_sp[0]
                            asrc, dh, dl, cw = my_a2[ci]
                            k = a2_base + ci
                            slot = k % 2
                            wait(vector, s_p[slot], amarks[ci])
                            if k // 2:
                                wait(vector, g_a[slot], 32 * (k // 2))
                            vector.tensor_copy(a2o[slot][0][:, 0:cw], a2st[slot][:, 0:cw])
                            vector.drain()
                            dv_s.inc(vector.tensor_tensor(a2o[slot][1][:, 0:cw],
                                                          a2st[slot][:, 0:cw],
                                                          a2o[slot][0][:, 0:cw],
                                                          ALU.subtract))
                            nxt_sp[0] += 1

                    wave = 0
                    for g, grp in enumerate(kgroups):
                        for f in range(NF):
                            emit_wx_split(vector, len(grp) * 128, f)
                        wait(vector, pe_w, base[pe_w] + NF * (g + 1))
                        for i, dc in enumerate(grp):
                            slot = ev % 4
                            if cnt[slot]:
                                wait(vector, g_k[slot], 16 * cnt[slot])
                            slot_seq.append(slot)
                            cnt[slot] += 1
                            dv_e.inc(vector.tensor_copy(kevac[slot], bank(i)))
                            ev += 1
                        wave += 1
                        emit_a2_splits(vector, (NF * wave) // 3)
                    for g, dps in enumerate(vdp_pairs):
                        for f in range(NF):
                            emit_wx_split(vector, len(dps) * VDW, f)
                        wait(vector, pe_w, base[pe_w] + NF * (len(kgroups) + g + 1))
                        bi = 0
                        for st in range(nst):
                            for di in range(len(dps)):
                                slot = ev % 4
                                if cnt[slot]:
                                    wait(vector, g_k[slot], 16 * cnt[slot])
                                slot_seq.append(slot)
                                cnt[slot] += 1
                                dv_e.inc(vector.tensor_copy(kevac[slot][:, 0:VDW],
                                                            bank(bi, VDW)))
                                bi += 1
                                ev += 1
                        wave += 1
                        emit_a2_splits(vector, (NF * wave) // 3)
                    emit_a2_splits(vector, len(my_a2))

                @blk.gpsimd
                def _(g):
                    ev = 0
                    nxt_st = [0]

                    def emit_a2_stores(g, upto):
                        while nxt_st[0] < min(upto, len(my_a2)):
                            ci = nxt_st[0]
                            asrc, dh, dl, cw = my_a2[ci]
                            k = a2_base + ci
                            slot = k % 2
                            wait(g, dv_s, base[dv_s] + ci + 1)
                            if g_a[slot].n:
                                wait(g, g_a[slot], g_a[slot].n)
                            g_a[slot].inc(g.dma_start(out=dh, in_=a2o[slot][0][:, 0:cw]), 16)
                            g_a[slot].inc(g.dma_start(out=dl, in_=a2o[slot][1][:, 0:cw]), 16)
                            nxt_st[0] += 1

                    wave = 0
                    for grp in kgroups:
                        for dc in grp:
                            slot = slot_seq[ev]
                            wait(g, dv_e, base[dv_e] + ev + 1)
                            if g_k[slot].n:
                                wait(g, g_k[slot], g_k[slot].n)
                            g_k[slot].inc(g.dma_start(
                                out=KTs[dc * 128:(dc + 1) * 128, sc * SC:(sc + 1) * SC],
                                in_=kevac[slot]), 16)
                            kv_slot_cnt[slot] += 1
                            ev += 1
                        wave += 1
                        emit_a2_stores(g, (NF * wave) // 3)
                    for dps in vdp_pairs:
                        for st in range(nst):
                            for dp in dps:
                                slot = slot_seq[ev]
                                wait(g, dv_e, base[dv_e] + ev + 1)
                                if g_k[slot].n:
                                    wait(g, g_k[slot], g_k[slot].n)
                                g_k[slot].inc(g.dma_start(
                                    out=Vs[(sc * nst + st) * 128:(sc * nst + st + 1) * 128,
                                           dp * VDW:(dp + 1) * VDW],
                                    in_=kevac[slot][:, 0:VDW]), 16)
                                kv_slot_cnt[slot] += 1
                                ev += 1
                        wave += 1
                        emit_a2_stores(g, (NF * wave) // 3)
                    emit_a2_stores(g, len(my_a2))
                    if sc == NSC - 1:
                        for s2 in range(4):
                            if g_k[s2].n:
                                wait(g, g_k[s2], g_k[s2].n)
                        for s2 in range(2):
                            if g_a[s2].n:
                                wait(g, g_a[s2], g_a[s2].n)

        for qc in range(NQC):
            with nc.named_scope(f"phB2_{qc}"), nc.Block(no_gpsimd_drain=True) as blk:
                base = snap()
                emit_ln(blk, xTq, qc * QCH, QCH, base)

                @blk.scalar
                def _(scalar):
                    wait(scalar, dv_a, base[dv_a] + 4 + NF)
                    for f in range(NF):
                        se_a.inc(scalar.activation(
                            out=xqh[:, f * SQ + qc * QCH:f * SQ + (qc + 1) * QCH],
                            in_=xbuf(f)[:, 0:QCH], func=AF.Copy))

                @blk.vector
                def _(vector):
                    for f in range(NF):
                        wait(vector, se_a, base[se_a] + NF + 1 + f + 1)
                        dv_a.inc(vector.tensor_tensor(
                            xql[:, f * SQ + qc * QCH:f * SQ + (qc + 1) * QCH],
                            xbuf(f)[:, 0:QCH],
                            xqh[:, f * SQ + qc * QCH:f * SQ + (qc + 1) * QCH],
                            ALU.subtract))

        def xqh_t(f, q0):
            return xqh[:, f * SQ + q0:f * SQ + q0 + QCH]

        def xql_t(f, q0):
            return xql[:, f * SQ + q0:f * SQ + q0 + QCH]

        for qc in range(NQC):
            q0 = qc * QCH
            with nc.named_scope(f"phC{qc}"), nc.Block(no_gpsimd_drain=True) as blk:
                base = snap()

                sched = []
                for e in range(2):
                    for gi, grp in enumerate(fgroups):
                        sched.append(('h', e, gi, grp))
                        sched.append(('o', e, gi, grp))

                tiles_per_step = []
                tile_srcs = []
                for kind, e, gi, grp in sched:
                    ha = (sWah, sWal) if e == 0 else (sW1ah, sW1al)
                    hb = (sWbh, sWbl) if e == 0 else (sW1bh, sW1bl)
                    if kind == 'h':
                        c0, cw = grp[0] * 128, len(grp) * 128
                        srcs = [(ha[0][f * 128:(f + 1) * 128, c0:c0 + cw],
                                 ha[1][f * 128:(f + 1) * 128, c0:c0 + cw], cw)
                                for f in range(NF)]
                    else:
                        srcs = []
                        for og in ogroups:
                            oc0, ocw = og[0] * 128, len(og) * 128
                            for fc in grp:
                                srcs.append((hb[0][fc * 128:(fc + 1) * 128, oc0:oc0 + ocw],
                                             hb[1][fc * 128:(fc + 1) * 128, oc0:oc0 + ocw],
                                             ocw))
                    tiles_per_step.append(len(srcs))
                    tile_srcs.append(srcs)
                tile_start = [0]
                for n in tiles_per_step:
                    tile_start.append(tile_start[-1] + n)

                comp = {}
                seo_c = 0
                dvo_c = 0
                for si, (kind, e, gi, grp) in enumerate(sched):
                    if kind != 'o':
                        continue
                    for bi2, og in enumerate(ogroups):
                        if e == 0 and gi == 0:
                            seo_c += len(og)
                            comp[(si, bi2)] = (se_o, seo_c)
                        else:
                            dvo_c += 1
                            comp[(si, bi2)] = (dv_o, dvo_c)
                    if e == 0 and gi == len(fgroups) - 1:
                        dvo_c += 1
                total_dvo = dvo_c

                cmarks = []

                @blk.sync
                def _(sync):
                    if qc == 0:
                        for s2 in range(2):
                            wait(sync, g_a[s2], g_a[s2].n)
                    widx = 0
                    for si in range(len(sched)):
                        for (hsrc, lsrc, cw) in tile_srcs[si]:
                            if widx >= 2:
                                wait(sync, pe_w, base[pe_w] + widx - 1)
                            cmarks.append(pair_load(
                                sync, s_w[widx % 2],
                                (wsl[widx % 2][0][:, 0:cw], wsl[widx % 2][1][:, 0:cw]),
                                (hsrc, lsrc)))
                            widx += 1

                @blk.tensor
                def _(tensor):
                    wait(tensor, dv_a, base[dv_a] + 1)
                    for f in range(NF):
                        inst = mm3(tensor, ps[0:1, 0:QCH], wgd_hi[:, f:f + 1],
                                   wgd_lo[:, f:f + 1], xqh_t(f, q0), xql_t(f, q0),
                                   f == 0, f == NF - 1)
                    pe_a.inc(inst)
                    wait(tensor, dv_a, base[dv_a] + 2)
                    pe_a.inc(tensor.matmul(bank(1, QCH), ones[0:1, :], r_gmask[:, 0:QCH],
                                           start=True, stop=True))
                    wait(tensor, dv_a, base[dv_a] + 4)
                    widx = 0
                    n_seh = 0
                    n_dvh = 0
                    last_comp = None
                    for si, (kind, e, gi, grp) in enumerate(sched):
                        if kind == 'h':
                            if last_comp is not None:
                                wait(tensor, last_comp[0], base[last_comp[0]] + last_comp[1])
                            n_dvh_c = n_dvh
                            for f in range(NF):
                                wait(tensor, s_w[widx % 2], cmarks[widx])
                                cw = len(grp) * 128
                                for i in range(len(grp)):
                                    inst = mm3(tensor, bank(i, QCH),
                                               wsl[widx % 2][0][:, i * 128:(i + 1) * 128],
                                               wsl[widx % 2][1][:, i * 128:(i + 1) * 128],
                                               xqh_t(f, q0), xql_t(f, q0),
                                               f == 0, f == NF - 1)
                                pe_w.inc(inst)
                                widx += 1
                            n_seh += 2 * len(grp)
                        else:
                            for bi2, og in enumerate(ogroups):
                                if bi2 > 0:
                                    c = comp[(si, bi2 - 1)]
                                    wait(tensor, c[0], base[c[0]] + c[1])
                                elif n_seh:
                                    wait(tensor, se_a, base[se_a] + n_seh)
                                for ti, fc in enumerate(grp):
                                    wait(tensor, dv_h, base[dv_h] + n_dvh + ti + 1)
                                    wait(tensor, s_w[widx % 2], cmarks[widx])
                                    hh = xn_hi[:, ti * QCH:(ti + 1) * QCH]
                                    hl = xn_lo[:, ti * QCH:(ti + 1) * QCH]
                                    for i in range(len(og)):
                                        inst = mm3(tensor, bank(i, QCH),
                                                   wsl[widx % 2][0][:, i * 128:(i + 1) * 128],
                                                   wsl[widx % 2][1][:, i * 128:(i + 1) * 128],
                                                   hh, hl, ti == 0, ti == len(grp) - 1)
                                    pe_w.inc(inst)
                                    widx += 1
                            n_dvh += len(grp)
                            last_comp = comp[(si, len(ogroups) - 1)]

                @blk.scalar
                def _(scalar):
                    wait(scalar, g_st, base[g_st])
                    widx_end_h = {}
                    widx = 0
                    for si2 in range(len(sched)):
                        widx += tiles_per_step[si2]
                        widx_end_h[si2] = widx
                    for si, (kind, e, gi, grp) in enumerate(sched):
                        if kind == 'h':
                            ba = c_b0a if e == 0 else c_b1a
                            fn = AF.Gelu if e == 0 else AF.Relu
                            wait(scalar, pe_w, base[pe_w] + widx_end_h[si])
                            for i, fc in enumerate(grp):
                                se_a.inc(scalar.activation(out=h32(i), in_=bank(i, QCH),
                                                           func=fn, bias=ba[:, fc:fc + 1]))
                                se_a.inc(scalar.activation(
                                    out=xn_hi[:, i * QCH:(i + 1) * QCH], in_=bank(i, QCH),
                                    func=fn, bias=ba[:, fc:fc + 1]))
                        else:
                            bb = c_b0b if e == 0 else c_b1b
                            if gi == 0:
                                for bi2, og in enumerate(ogroups):
                                    wait(scalar, pe_w,
                                         base[pe_w] + tile_start[si] + (bi2 + 1) * len(grp))
                                    for i, oc in enumerate(og):
                                        if e == 0:
                                            se_o.inc(scalar.activation(
                                                out=out0(oc), in_=bank(i, QCH),
                                                func=AF.Identity, bias=bb[:, oc:oc + 1]))
                                        else:
                                            se_o.inc(scalar.activation(
                                                out=RA[:, (8 + i) * QCH:(9 + i) * QCH],
                                                in_=bank(i, QCH),
                                                func=AF.Identity, bias=bb[:, oc:oc + 1]))

                @blk.vector
                def _(vector):
                    if qc == 0:
                        for s2 in range(2):
                            wait(vector, g_a[s2], g_a[s2].n)
                    vector.tensor_tensor(wgd, wg_sb[:, 0::2], wg_sb[:, 1::2], ALU.subtract)
                    vector.drain()
                    vector.tensor_copy(wgd_hi, wgd)
                    vector.drain()
                    dv_a.inc(vector.tensor_tensor(wgd_lo, wgd, wgd_hi, ALU.subtract))
                    wait(vector, pe_a, base[pe_a] + 1)
                    dv_a.inc(vector.tensor_scalar(r_gmask[:, 0:QCH], ps[0:1, 0:QCH],
                                                  0.0, None, ALU.is_ge))
                    wait(vector, pe_a, base[pe_a] + 2)
                    dv_a.inc(vector.tensor_copy(mask_b[:, 0:QCH], bank(1, QCH)))
                    vector.drain()
                    dv_a.inc(vector.tensor_scalar(invm[:, 0:QCH], mask_b[:, 0:QCH],
                                                  -1.0, 1.0, ALU.mult, ALU.add))
                    n_seh = 0
                    n_seo = 0
                    for si, (kind, e, gi, grp) in enumerate(sched):
                        if kind == 'h':
                            for i in range(len(grp)):
                                n_seh += 2
                                wait(vector, se_a, base[se_a] + n_seh)
                                dv_h.inc(vector.tensor_tensor(
                                    xn_lo[:, i * QCH:(i + 1) * QCH], h32(i),
                                    xn_hi[:, i * QCH:(i + 1) * QCH], ALU.subtract))
                        else:
                            for bi2, og in enumerate(ogroups):
                                if e == 0 and gi == 0:
                                    continue
                                pw = base[pe_w] + tile_start[si] + (bi2 + 1) * len(grp)
                                wait(vector, pe_w, pw)
                                if e == 0:
                                    vector.drain()
                                    for i, oc in enumerate(og):
                                        if i == len(og) - 1:
                                            dv_o.inc(vector.tensor_tensor(
                                                out0(oc), out0(oc), bank(i, QCH), ALU.add))
                                        else:
                                            vector.tensor_tensor(out0(oc), out0(oc),
                                                                 bank(i, QCH), ALU.add)
                                else:
                                    vector.drain()
                                    if gi == 0:
                                        wait(vector, se_o,
                                             base[se_o] + NF + (bi2 + 1) * len(og))
                                        for i, oc in enumerate(og):
                                            vector.tensor_tensor(
                                                RA[:, (8 + i) * QCH:(9 + i) * QCH],
                                                RA[:, (8 + i) * QCH:(9 + i) * QCH],
                                                invm[:, 0:QCH], ALU.mult)
                                    else:
                                        for i, oc in enumerate(og):
                                            vector.tensor_tensor(
                                                RA[:, (8 + i) * QCH:(9 + i) * QCH],
                                                bank(i, QCH), invm[:, 0:QCH], ALU.mult)
                                    vector.drain()
                                    for i, oc in enumerate(og):
                                        if i == len(og) - 1:
                                            dv_o.inc(vector.tensor_tensor(
                                                out0(oc), out0(oc),
                                                RA[:, (8 + i) * QCH:(9 + i) * QCH], ALU.add))
                                        else:
                                            vector.tensor_tensor(
                                                out0(oc), out0(oc),
                                                RA[:, (8 + i) * QCH:(9 + i) * QCH], ALU.add)
                            if e == 0 and gi == len(fgroups) - 1:
                                wait(vector, se_o, base[se_o] + NF)
                                vector.drain()
                                for oc in range(NF):
                                    if oc == NF - 1:
                                        dv_o.inc(vector.tensor_tensor(
                                            out0(oc), out0(oc), mask_b[:, 0:QCH], ALU.mult))
                                    else:
                                        vector.tensor_tensor(out0(oc), out0(oc),
                                                             mask_b[:, 0:QCH], ALU.mult)

                @blk.gpsimd
                def _(g):
                    wait(g, dv_o, base[dv_o] + total_dvo)
                    for oc in range(NF):
                        g_st.inc(g.dma_start(
                            out=Fs[oc * 128:(oc + 1) * 128, q0:q0 + QCH],
                            in_=out0(oc)), 16)
                    g.wait_ge(g_st.h, g_st.n)

        nqp = (SQ + 511) // 512
        for h in range(NH):
            with nc.named_scope(f"phD{h}"), nc.Block(no_gpsimd_drain=True) as blk:
                base = snap()

                def pt_base(i0):
                    return sum(j + 1 for j in range(i0))

                def dva_base(i0):
                    return sum((j + 1) + 2 for j in range(i0))

                def ev_base(i0):
                    return nqp + sum((j + 1) + 1 for j in range(i0))

                def sci(i0, kc):
                    return sum(j + 1 for j in range(i0)) + kc
                nsci = sum(j + 1 for j in range(NQT))

                kmarks = []

                @blk.sync
                def _(sync):
                    if h == 0:
                        wait(sync, g_st, base[g_st])
                    for hf in range(2):
                        if hf == 1:
                            for kb in range(NKB):
                                s_ld.inc(sync.dma_start(
                                    out=Vb[:, kb * 128:(kb + 1) * 128],
                                    in_=Vs[kb * 128:(kb + 1) * 128,
                                           h * 128:(h + 1) * 128]), 16)
                            wait(sync, pe_a, base[pe_a] + 1)
                        for fi in range(FHALF):
                            f = hf * FHALF + fi
                            s_ld.inc(sync.dma_start(
                                out=wqh_b[:, fi * 128:(fi + 1) * 128],
                                in_=swqh[f * 128:(f + 1) * 128, h * 128:(h + 1) * 128]), 16)
                            s_ld.inc(sync.dma_start(
                                out=wql_b[:, fi * 128:(fi + 1) * 128],
                                in_=swql[f * 128:(f + 1) * 128, h * 128:(h + 1) * 128]), 16)
                        sync.wait_ge(s_ld.h, s_ld.n)
                    c = 0
                    for i0 in range(NQT):
                        for kc in range(i0 + 1):
                            slot = c % 3
                            if c >= 3:
                                wait(sync, pe_a, base[pe_a] + c)
                            kmarks.append(slot_load(
                                sync, s_k[slot], kcs[slot],
                                KTs[h * 128:(h + 1) * 128, kc * 512:(kc + 1) * 512]))
                            c += 1

                sbanks = [0, 1, 5, 6, 7]

                def chunk2tile(c):
                    i0 = 0
                    while c >= i0 + 1:
                        c -= i0 + 1
                        i0 += 1
                    return i0, c

                def emit_scores(tensor, i0, k0, k1):
                    for kc in range(k0, k1):
                        c = sci(i0, kc)
                        bk = sbanks[c % 5]
                        if c == 0:
                            wait(tensor, dv_e, base[dv_e] + nqp)
                        if c >= 5:
                            i0p, kcp = chunk2tile(c - 5)
                            wait(tensor, dv_a, base[dv_a] + dva_base(i0p) + kcp + 1)
                        wait(tensor, s_k[c % 3], kmarks[c])
                        pe_a.inc(tensor.matmul(
                            bank(bk), Qh[:, i0 * 128:(i0 + 1) * 128],
                            kcs[c % 3], start=True, stop=True))

                def emit_av(tensor, i0, g, ng):
                    wait(tensor, dv_e, base[dv_e] + ev_base(i0) + g + 1)
                    for t in range(4):
                        kb = 4 * g + t
                        inst = tensor.matmul(
                            ps[:, 4 * 512:4 * 512 + 128], Vb[:, kb * 128:(kb + 1) * 128],
                            ptb[g % 2][:, t * 128:(t + 1) * 128],
                            start=(kb == 0), stop=(kb == 4 * ng - 1))
                    pe_o.inc(inst)

                @blk.tensor
                def _(tensor):
                    for hf in range(2):
                        wait(tensor, s_ld,
                             base[s_ld] + 16 * (2 * FHALF * (hf + 1) + NKB * hf))
                        for qp in range(nqp):
                            qw = min(512, SQ - qp * 512)
                            for fi in range(FHALF):
                                f = hf * FHALF + fi
                                inst = mm3(tensor, bank(5 + qp, qw),
                                           wqh_b[:, fi * 128:(fi + 1) * 128],
                                           wql_b[:, fi * 128:(fi + 1) * 128],
                                           xqh[:, f * SQ + qp * 512:f * SQ + qp * 512 + qw],
                                           xql[:, f * SQ + qp * 512:f * SQ + qp * 512 + qw],
                                           hf == 0 and fi == 0,
                                           hf == 1 and fi == FHALF - 1)
                        pe_a.inc(inst)
                    emit_scores(tensor, 0, 0, 1)
                    for i0 in range(NQT):
                        ng = i0 + 1
                        if i0 + 1 < NQT:
                            emit_scores(tensor, i0 + 1, 0, min(5, i0 + 2))
                        wait(tensor, se_a, base[se_a] + 2 * (i0 + 1))
                        for g in range(ng):
                            if g >= 2:
                                wait(tensor, dv_e, base[dv_e] + ev_base(i0) + g - 1)
                            for t in range(4):
                                inst = tensor.transpose(
                                    ps[:, (2 + g % 2) * 512 + t * 128:
                                       (2 + g % 2) * 512 + (t + 1) * 128],
                                    SP_sb[:, (4 * g + t) * 128:(4 * g + t + 1) * 128],
                                    ident)
                            pe_t.inc(inst)
                            if g >= 1:
                                emit_av(tensor, i0, g - 1, ng)
                        emit_av(tensor, i0, ng - 1, ng)
                        if i0 + 1 < NQT and i0 + 2 > 5:
                            emit_scores(tensor, i0 + 1, 5, i0 + 2)

                @blk.vector
                def _(vector):
                    for qp in range(nqp):
                        qw = min(512, SQ - qp * 512)
                        wait(vector, pe_a, base[pe_a] + 2)
                        dv_e.inc(vector.tensor_scalar_mul(Qh[:, qp * 512:qp * 512 + qw],
                                                          bank(5 + qp, qw), scale))
                    for i0 in range(NQT):
                        ng = i0 + 1
                        nkc = i0 + 1
                        wait(vector, se_a, base[se_a] + 2 * i0)
                        wait(vector, pe_t, base[pe_t] + pt_base(i0))
                        for kc in range(nkc):
                            c = sci(i0, kc)
                            bk = sbanks[c % 5]
                            wait(vector, pe_a, base[pe_a] + 2 + c + 1)
                            if kc == nkc - 1:
                                vector.tensor_tensor(bank(bk), bank(bk), maskt, ALU.add)
                                vector.drain()
                            vector.reduce_max(mx[:, kc:kc + 1], bank(bk), axis=AX.X)
                            dv_a.inc(vector.tensor_copy(SP_sb[:, kc * 512:(kc + 1) * 512],
                                                        bank(bk)))
                        vector.drain()
                        dv_a.inc(vector.reduce_max(negm, mx[:, 0:nkc], axis=AX.X,
                                                   negate=True))
                        wait(vector, se_a, base[se_a] + 2 * i0 + 1)
                        vector.tensor_scalar_add(rden, den, 1e-8)
                        vector.drain()
                        dv_a.inc(vector.reciprocal(rden, rden))
                        for g in range(ng):
                            wait(vector, pe_t, base[pe_t] + pt_base(i0) + g + 1)
                            if g >= 2:
                                wait(vector, pe_o, base[pe_o] + pt_base(i0) + g - 1)
                            dv_e.inc(vector.tensor_copy(
                                ptb[g % 2],
                                ps[:, (2 + g % 2) * 512:(3 + g % 2) * 512]))
                        wait(vector, pe_o, base[pe_o] + pt_base(i0) + ng)
                        dv_e.inc(vector.tensor_copy(
                            attnbuf(h, i0 * 128, (i0 + 1) * 128),
                            ps[:, 4 * 512:4 * 512 + 128]))

                @blk.scalar
                def _(scalar):
                    for i0 in range(NQT):
                        E = 4 * i0 + 4
                        nkc = i0 + 1
                        wait(scalar, dv_a, base[dv_a] + dva_base(i0) + nkc + 1)
                        wait(scalar, pe_t, base[pe_t] + pt_base(i0))
                        se_a.inc(scalar.activation(out=SP_sb[:, 0:E * 128],
                                                   in_=SP_sb[:, 0:E * 128], func=AF.Exp,
                                                   scale=1.0, bias=negm, accum_out=den))
                        wait(scalar, dv_a, base[dv_a] + dva_base(i0) + nkc + 2)
                        se_a.inc(scalar.activation(out=SP_sb[:, 0:E * 128],
                                                   in_=SP_sb[:, 0:E * 128],
                                                   func=AF.Identity,
                                                   bias=zero_col[:, 0:1], scale=rden))

        with nc.named_scope("phE"), nc.Block(no_gpsimd_drain=True) as blk:
            base = snap()
            EQC = SQ // QCH
            wmarks = []
            pmarks = []
            smarks = []

            @blk.sync
            def _(sync):
                wait(sync, g_st, base[g_st])
                widx = 0
                lidx = 0
                for grp in ogroups:
                    c0, cw = grp[0] * 128, len(grp) * 128
                    for qcc in range(EQC):
                        q0 = qcc * QCH
                        for f in range(NF):
                            if widx >= 2:
                                wait(sync, pe_w, base[pe_w] + widx - 1)
                            wmarks.append(pair_load(
                                sync, s_w[widx % 2],
                                (wsl[widx % 2][0][:, 0:cw], wsl[widx % 2][1][:, 0:cw]),
                                (swoh[f * 128:(f + 1) * 128, c0:c0 + cw],
                                 swol[f * 128:(f + 1) * 128, c0:c0 + cw])))
                            widx += 1
                        for oc in grp:
                            slot = lidx % 2
                            if lidx >= 2:
                                wait(sync, dv_a, base[dv_a] + lidx - 1)
                            if s_p[slot].n:
                                wait(sync, s_p[slot], s_p[slot].n)
                            s_p[slot].inc(sync.dma_start(
                                out=ld2[2 * slot][:, 0:QCH],
                                in_=Fs[oc * 128:(oc + 1) * 128, q0:q0 + QCH]), 16)
                            s_p[slot].inc(sync.dma_start(
                                out=ld2[2 * slot + 1][:, 0:QCH],
                                in_=xTq[oc * 128:(oc + 1) * 128, q0:q0 + QCH]), 16)
                            pmarks.append(s_p[slot].n)
                            lidx += 1

            @blk.vector
            def _(vector):
                widx = 0
                lidx = 0
                for grp in ogroups:
                    for qcc in range(EQC):
                        q0 = qcc * QCH
                        for f in range(NF):
                            slot = widx % 2
                            if widx >= 2:
                                wait(vector, pe_w, base[pe_w] + widx - 1)
                            vector.tensor_copy(aspl[slot][0][:, 0:QCH],
                                               attnbuf(f, q0, q0 + QCH))
                            vector.drain()
                            dv_s.inc(vector.tensor_tensor(
                                aspl[slot][1][:, 0:QCH], attnbuf(f, q0, q0 + QCH),
                                aspl[slot][0][:, 0:QCH], ALU.subtract))
                            widx += 1
                        gq = widx // NF
                        wait(vector, pe_w, base[pe_w] + gq * NF)
                        for i, oc in enumerate(grp):
                            slot = lidx % 2
                            wait(vector, s_p[slot], pmarks[lidx])
                            if lidx >= 2:
                                wait(vector, g_o[slot], 16 * (lidx // 2))
                            dv_e.inc(vector.tensor_tensor(
                                ob2[slot], bank(i, QCH), ld2[2 * slot][:, 0:QCH], ALU.add))
                            vector.drain()
                            dv_a.inc(vector.tensor_tensor(
                                ob2[slot], ob2[slot], ld2[2 * slot + 1][:, 0:QCH], ALU.add))
                            lidx += 1

            @blk.tensor
            def _(tensor):
                widx = 0
                nev = 0
                for grp in ogroups:
                    for qcc in range(EQC):
                        if nev:
                            wait(tensor, dv_e, base[dv_e] + nev)
                        for f in range(NF):
                            slot = widx % 2
                            wait(tensor, s_w[slot], wmarks[widx])
                            wait(tensor, dv_s, base[dv_s] + widx + 1)
                            for i in range(len(grp)):
                                inst = mm3(tensor, bank(i, QCH),
                                           wsl[slot][0][:, i * 128:(i + 1) * 128],
                                           wsl[slot][1][:, i * 128:(i + 1) * 128],
                                           aspl[slot][0][:, 0:QCH], aspl[slot][1][:, 0:QCH],
                                           f == 0, f == NF - 1)
                            pe_w.inc(inst)
                            widx += 1
                        nev += len(grp)

            @blk.gpsimd
            def _(g):
                lidx = 0
                for grp in ogroups:
                    for qcc in range(EQC):
                        q0 = qcc * QCH
                        for oc in grp:
                            slot = lidx % 2
                            wait(g, dv_a, base[dv_a] + lidx + 1)
                            if g_o[slot].n:
                                wait(g, g_o[slot], g_o[slot].n)
                            g_o[slot].inc(g.dma_start(
                                out=outT[oc * 128:(oc + 1) * 128, q0:q0 + QCH],
                                in_=ob2[slot]), 16)
                            lidx += 1
                g.wait_ge(g_o[0].h, g_o[0].n)
                g.wait_ge(g_o[1].h, g_o[1].n)

    return nc


def _make_masks(j):
    r = np.arange(128)[:, None]
    out = np.zeros((128, 512), dtype=np.float32)
    for t in range(4):
        c = np.arange(128)[None, :]
        allowed = (128 * t + c) <= (4 * r + j)
        out[:, t * 128:(t + 1) * 128] = np.where(allowed, 0.0, NEG_BIG).astype(np.float32)
    return out


def _consts_tile():
    c = np.zeros((128, 272), dtype=np.float32)
    c[:, 0:128] = 1.0
    c[:, 128:256] = np.eye(128, dtype=np.float32)
    c[:, 256] = 1e-5
    return c


def kernel(L0_attn_x, L0_attn_wq, L0_attn_wk, L0_attn_wv, L0_attn_wo,
           L0_ffn_Wg, L0_ffn_W0a, L0_ffn_b0a, L0_ffn_W0b, L0_ffn_b0b,
           L0_ffn_W1a, L0_ffn_b1a, L0_ffn_W1b, L0_ffn_b1b):
    x = np.asarray(L0_attn_x)
    B, S, H = x.shape
    NH = 16
    FF = np.asarray(L0_ffn_W0a).shape[0]
    NF = H // 128
    NFC = FF // 128

    nc = build_nc(H=H, S=S, NH=NH, FF=FF)

    def t(a):
        return np.ascontiguousarray(np.asarray(a, dtype=np.float32).T)

    def b2(a, n):
        return np.ascontiguousarray(np.asarray(a, dtype=np.float32).reshape(n, 128).T)

    shared = {
        "wqT": t(L0_attn_wq), "wkT": t(L0_attn_wk), "wvT": t(L0_attn_wv),
        "woT": t(L0_attn_wo), "wgT": t(L0_ffn_Wg),
        "W0aT": t(L0_ffn_W0a), "W0bT": t(L0_ffn_W0b),
        "W1aT": t(L0_ffn_W1a), "W1bT": t(L0_ffn_W1b),
        "b0a2": b2(L0_ffn_b0a, NFC), "b1a2": b2(L0_ffn_b1a, NFC),
        "b0b2": b2(L0_ffn_b0b, NF), "b1b2": b2(L0_ffn_b1b, NF),
        "consts": _consts_tile(),
    }
    in_maps = []
    for core in range(8):
        b, j = core // 4, core % 4
        im = dict(shared)
        im["xTs"] = np.ascontiguousarray(x[b].T)
        im["xTq"] = np.ascontiguousarray(x[b, j::4, :].T)
        im["masks"] = _make_masks(j)
        in_maps.append(im)

    res = run_bass_kernel_spmd(nc, in_maps, core_ids=list(range(8)))

    out = np.empty((B, S, H), dtype=np.float32)
    for core in range(8):
        b, j = core // 4, core % 4
        out[b, j::4, :] = res.results[core]["outT"].T
    return out
